# version 01
# baseline (speedup 1.0000x reference)
import contextlib
import numpy as np
import ml_dtypes
import concourse.bass as bass
import concourse.mybir as mybir
from concourse.bass_utils import run_bass_kernel_spmd

F32 = mybir.dt.float32
BF16 = mybir.dt.bfloat16
AF = mybir.ActivationFunctionType
ALU = mybir.AluOpType
AX = mybir.AxisListType

NCORES = 8
TOK = 2048
HALF = 1024
D = 2048
ND = 16
H = 1408
NHC = 11
E = 8
DI = 3072
NKI = 24
NH = 48
PH = 64
NCH = 16
EPS = 1e-5


class Prog:

    ENGS = ("t", "v", "a", "s", "g")

    def __init__(self, b):
        self.b = b
        self.q = {e: [] for e in self.ENGS}

    def on(self, eng, fn):
        self.q[eng].append(fn)

    def wait(self, eng, sem, val=None):
        v = self.b.cnt[sem] if val is None else val
        self.q[eng].append(lambda e, s=self.b.sems[sem], v=v: e.wait_ge(s, v))

    def emit(self, eng, fn, inc=None):
        incs = []
        if inc:
            for sem, k in inc:
                self.b.cnt[sem] += k
                incs.append((self.b.sems[sem], k))

        def run(e, fn=fn, incs=incs):
            i = fn(e)
            for s, k in incs:
                i = i.then_inc(s, k)
        self.q[eng].append(run)
        return {s: self.b.cnt[s] for s in self.b.cnt} if inc else None

    def flush(self):
        nc = self.b.nc
        with nc.Block() as blk:
            if self.q["t"]:
                blk.tensor(lambda e, fns=self.q["t"]: [f(e) for f in fns])
            if self.q["v"]:
                blk.vector(lambda e, fns=self.q["v"]: [f(e) for f in fns])
            if self.q["a"]:
                blk.scalar(lambda e, fns=self.q["a"]: [f(e) for f in fns])
            if self.q["s"]:
                blk.sync(lambda e, fns=self.q["s"]: [f(e) for f in fns])
            if self.q["g"]:
                blk.gpsimd(lambda e, fns=self.q["g"]: [f(e) for f in fns])
        self.q = {e: [] for e in self.ENGS}


class MK:
    def __init__(self, phases=("ffn1", "mamba", "ffn2"), detect_races=False):
        self.phases = phases
        nc = bass.Bass("TRN2", target_bir_lowering=False, detect_race_conditions=detect_races)
        self.nc = nc
        self.cnt = {}
        self.sems = {}
        for name in ["ld", "st", "sc", "gp", "c0", "c1", "c2", "c3", "w0", "w1", "z0", "z1", "y0", "y1", "cc", "mm", "ev", "ac"]:
            self.sems[name] = nc.alloc_semaphore(name)
            self.cnt[name] = 0

        ap = lambda n, s, d=F32: nc.dram_tensor(n, s, d, kind="ExternalInput").ap()
        self.x_in = ap("x_fm", [D, TOK])
        self.wg_s = ap("wg_s", [D, H]);  self.wu_s = ap("wu_s", [D, H])
        self.wd_s = ap("wd_s", [H, D])
        self.wg_e = ap("wg_e", [E, D, H]); self.wu_e = ap("wu_e", [E, D, H])
        self.wd_e = ap("wd_e", [E, H, D])
        self.wr_in = ap("wr", [D, E])
        self.br_rb = ap("br_rb", [E, 2])
        self.inproj = ap("inproj", [6448, D])
        self.outproj = ap("outproj", [D, DI])
        self.convw_in = ap("convw", [128, NKI * 4])
        self.convb_in = ap("convb", [128, NKI])
        self.dtbias_in = ap("dtbias", [NH, 1])
        self.alog_in = ap("alog", [NH, 1])
        self.drow_in = ap("drow", [1, NH])
        self.normw_in = ap("normw", [128, ND])
        self.innw_in = ap("innw", [1, DI])
        self.h0_in = ap("h0_in", [128, NH * PH])
        self.convinit_in = ap("conv_init", [128, NKI * 3])
        self.flaginv_in = ap("flaginv", [128, 1])
        self.hmask_in = ap("hmask", [128, E])
        self.cm128_in = ap("cmask128", [128, E])
        self.idf_in = ap("idf_in", [128, 128])
        self.idb_in = nc.dram_tensor("idb_in", [128, 128], BF16, kind="ExternalInput").ap()
        self.oh_in = ap("oh_in", [E, E * 128])

        self.y_out = nc.dram_tensor("y_fm", [D, TOK], F32, kind="ExternalOutput").ap()
        self.dbg = False

        dr = lambda n, s, d: nc.dram_tensor(n, s, d, kind="Internal")
        self.x1_d = dr("x1_d", [D, TOK], F32)
        self.wg_s_bf = dr("wg_s_bf", [D, H], BF16)
        self.wu_s_bf = dr("wu_s_bf", [D, H], BF16)
        self.wd_s_bf = dr("wd_s_bf", [H, D], BF16)
        self.wg_bf = dr("wg_bf", [E, D, H], BF16)
        self.wu_bf = dr("wu_bf", [E, D, H], BF16)
        self.wd_bf = dr("wd_bf", [E, H, D], BF16)
        self.wr_bf = dr("wr_bf", [D, E], BF16)
        self.inproj_bf = dr("inproj_bf", [6448, D], BF16)
        self.inprojT_bf = dr("inprojT_bf", [D, 6400], BF16)
        self.outproj_bf = dr("outproj_bf", [D, DI], BF16)
        self.outprojT_bf = dr("outprojT_bf", [DI, D], BF16)
        self.z_d = dr("z_d", [DI, TOK], BF16)
        self.xc_d = dr("xc_d", [DI, TOK], BF16)
        self.yg_d = dr("yg_d", [TOK, DI], BF16)
        self.a_d = dr("a_d", [1, NH * TOK], F32)
        self.bounce = dr("bounce", [1, 8192], F32)

        self.cc_halo_in = dr("cc_halo_in", [128, NKI * 3], BF16)
        self.cc_halo_out = dr("cc_halo_out", [NCORES * 128, NKI * 3], BF16)
        self.cc_sum_in = dr("cc_sum_in", [129, NH * PH], F32)
        self.cc_sum_out = dr("cc_sum_out", [NCORES * 129, NH * PH], F32)

        self.ctx = contextlib.ExitStack()
        self.uid = 0
        self.wf_hist = 0

    def _nm(self, n):
        self.uid += 1
        return f"{n}_{self.uid}"

    def sb(self, n, sh, d):
        return self.ctx.enter_context(self.nc.sbuf_tensor(self._nm(n), sh, d))

    def psum(self, es, n, sh, d=F32):
        return es.enter_context(self.nc.psum_tensor(self._nm(n), sh, d))

    def build(self):
        with self.ctx:
            self._alloc_globals()
            self._p0()
            src = self.x_in
            if "ffn1" in self.phases:
                self._ffn_call(src, self.x1_d.ap(), first=True)
                src = self.x1_d.ap()
            if "mamba" in self.phases:
                self._mamba(src, self.x1_d.ap())
                src = self.x1_d.ap()
            if "ffn2" in self.phases:
                self._ffn_call(src, self.y_out, first=False)
            else:
                self._copy_out(src)
            self._final_wait()
        return self.nc

    def _alloc_globals(self):
        s = self.sb
        self.idf = s("idf", [128, 128], F32)
        self.idb = s("idb", [128, 128], BF16)
        self.ones_rf = s("ones_rf", [1, 128], F32)
        self.ones_cb = s("ones_cb", [128, 1], BF16)
        self.ones_rb = s("ones_rb", [1, 128], BF16)
        self.zero128 = s("zero128", [128, 128], F32)
        self.epsc = s("epsc", [128, 1], F32)
        self.wr_sb = s("wr_sb", [128, ND * E], BF16)
        self.biasr = s("biasr", [E, 1], F32)
        self.normw_sb = s("normw_sb", [128, ND], F32)
        self.dcols = s("dcols", [128, NH], F32)
        self.rowtmp = s("rowtmp", [1, 64], F32)
        self.convw_sb = s("convw_sb", [128, NKI * 4], F32)
        self.convb_sb = s("convb_sb", [128, NKI], F32)
        self.flaginv_sb = s("flaginv_sb", [128, 1], F32)
        self.hmask_sb = s("hmask_sb", [128, E], F32)
        self.dtb = s("dtb", [NH, 1], F32)
        self.acol = s("acol", [NH, 1], F32)
        self.cm128 = s("cm128", [128, E], F32)
        self.cm128i = s("cm128i", [128, E], F32)
        self.brrb = s("brrb", [E, 2], F32)
        self.dtlhs = s("dtlhs", [128, ND * NH], BF16)

    def _p0(self):
        b = self
        P = Prog(b)
        loads = [
            (self.idf[:, :], self.idf_in), (self.idb[:, :], self.idb_in),
            (self.normw_sb[:, :], self.normw_in),
            (self.convw_sb[:, :], self.convw_in), (self.convb_sb[:, :], self.convb_in),
            (self.flaginv_sb[:, :], self.flaginv_in), (self.hmask_sb[:, :], self.hmask_in),
            (self.dtb[:, :], self.dtbias_in), (self.acol[:, :], self.alog_in),
            (self.cm128[:, :], self.cm128_in), (self.brrb[:, :], self.br_rb),
        ]
        for dst, srcc in loads:
            P.emit("s", lambda e, d=dst, x=srcc: e.dma_start(out=d, in_=x),
                   inc=[("ld", 16)])
        P.emit("g", lambda e: e.dma_start(out=self.wr_bf.ap(), in_=self.wr_in),
               inc=[("gp", 16)])
        P.wait("g", "gp")
        P.emit("g", lambda e: e.dma_start(
            out=self.wr_sb[:, :].rearrange("p (c ee) -> p c ee", ee=E),
            in_=self.wr_bf.ap().rearrange("(c p) ee -> p c ee", p=128)),
            inc=[("gp", 16)])
        P.emit("g", lambda e: e.memset(self.ones_rf[:, :], 1.0))
        P.emit("g", lambda e: e.memset(self.ones_cb[:, :], 1.0))
        P.emit("g", lambda e: e.memset(self.ones_rb[:, :], 1.0))
        P.emit("g", lambda e: e.memset(self.zero128[:, :], 0.0))
        P.emit("g", lambda e: e.memset(self.epsc[:, :], EPS))
        self.castgrp = {}
        casts = [("c0", self.wg_s_bf.ap(), self.wg_s),
                 ("c0", self.wu_s_bf.ap(), self.wu_s),
                 ("c0", self.wd_s_bf.ap(), self.wd_s)]
        for e_ in range(E):
            grp = "c0" if e_ == 0 else ("c1" if e_ <= 3 else "c2")
            casts += [(grp, self.wg_bf.ap()[e_], self.wg_e[e_]),
                      (grp, self.wu_bf.ap()[e_], self.wu_e[e_]),
                      (grp, self.wd_bf.ap()[e_], self.wd_e[e_])]
        casts += [("c3", self.inproj_bf.ap(), self.inproj),
                  ("c3", self.outproj_bf.ap(), self.outproj)]
        for grp, dst, srcc in casts:
            P.emit("g", lambda e, d=dst, x=srcc: e.dma_start(out=d, in_=x),
                   inc=[(grp, 16)])
        self.castgrp = {g: self.cnt[g] for g in ("c0", "c1", "c2", "c3")}
        P.wait("s", "ld")
        P.flush()

        P = Prog(b)
        P.emit("v", lambda e: e.tensor_add(self.biasr[:, :], self.brrb[:, 0:1],
                                           self.brrb[:, 1:2]))
        P.emit("s", lambda e: e.dma_start(out=self.rowtmp[0:1, 0:NH], in_=self.drow_in),
               inc=[("ld", 16)])
        P.wait("t", "ld")
        with self.nc.psum_tensor(self._nm("ps0_p0"), [128, 512], F32) as ps0:
            P.emit("t", lambda e, p=ps0: e.matmul(p[:, 0:NH], self.ones_rf[:, :],
                                                  self.rowtmp[0:1, 0:NH],
                                                  start=True, stop=True),
                   inc=[("mm", 1)])
            P.wait("v", "mm")
            P.emit("v", lambda e, p=ps0: e.tensor_copy(self.dcols[:, :], p[:, 0:NH]))
            P.flush()

        P = Prog(b)
        P.wait("a", "c3", self.castgrp["c3"])
        stg = self.sb("p0stage", [128, 6400], BF16)
        for dc in range(ND):
            P.emit("a", lambda e, dc=dc, stg=stg: e.dma_start_transpose(
                stg[:, 0:6400], self.inproj_bf.ap()[0:6400, dc * 128:(dc + 1) * 128]),
                inc=[("sc", 16)])
            P.wait("a", "sc")
            P.emit("a", lambda e, dc=dc, stg=stg: e.dma_start(
                out=self.inprojT_bf.ap()[dc * 128:(dc + 1) * 128, :], in_=stg[:, 0:6400]),
                inc=[("sc", 16)])
            P.wait("a", "sc")
        
        for kc in range(NKI):
            P.emit("a", lambda e, kc=kc, stg=stg: e.dma_start_transpose(
                stg[:, 0:2048], self.outproj_bf.ap()[:, kc * 128:(kc + 1) * 128]),
                inc=[("sc", 16)])
            P.wait("a", "sc")
            P.emit("a", lambda e, kc=kc, stg=stg: e.dma_start(
                out=self.outprojT_bf.ap()[kc * 128:(kc + 1) * 128, :], in_=stg[:, 0:2048]),
                inc=[("sc", 16)])
            P.wait("a", "sc")
        P.emit("a", lambda e, stg=stg: e.dma_start(
            out=stg[0:48, 0:2048], in_=self.inproj_bf.ap()[6400:6448, :]),
            inc=[("sc", 16)])
        P.wait("a", "sc")
        P.flush()

        P = Prog(b)
        with self.nc.psum_tensor(self._nm("ps_tp0"), [128, 1024], BF16) as pstb:
            for dc in range(ND):
                if dc > 0:
                    P.wait("t", "ev")
                P.emit("t", lambda e, dc=dc, p=pstb, stg=stg: e.transpose(
                    p[:, 0:48], stg[0:48, dc * 128:(dc + 1) * 128], self.idb[0:48, 0:48]),
                    inc=[("mm", 1)])
                P.wait("v", "mm")
                P.emit("v", lambda e, dc=dc, p=pstb: e.tensor_copy(
                    self.dtlhs[:, dc * NH:(dc + 1) * NH], p[:, 0:48]), inc=[("ev", 1)])
            P.flush()

    def _ln_tile(self, P, xsrc, sink, tglob, psb, mode="ln"):
        b = self
        ps_sx, ps_sq, ps_r1, ps_r2 = psb
        P.wait("t", "ev")
        P.wait("t", "ac")
        P.wait("a", "mm")
        P.wait("v", "mm")
        for dc in range(ND):
            P.emit("v", lambda e, dc=dc: e.tensor_copy(sink(dc), xsrc(dc)),
                   inc=[("ev", 1)])
            P.wait("a", "ev")
            P.emit("a", lambda e, dc=dc: e.activation(
                self.xsq[:, dc * 512:(dc + 1) * 512], sink(dc),
                AF.Square), inc=[("ac", 1)])
            P.wait("t", "ev")
            P.emit("t", lambda e, dc=dc: e.matmul(
                ps_sx[0:1, 0:512], self.ones_cb[:, :], sink(dc),
                start=(dc == 0), stop=(dc == ND - 1)), inc=([("mm", 1)] if dc == ND - 1 else None))
            P.wait("t", "ac")
            P.emit("t", lambda e, dc=dc: e.matmul(
                ps_sq[0:1, 0:512], self.ones_cb[:, :],
                self.xsq[:, dc * 512:(dc + 1) * 512],
                start=(dc == 0), stop=(dc == ND - 1)), inc=([("mm", 1)] if dc == ND - 1 else None))
        P.wait("v", "mm")
        P.emit("v", lambda e: e.tensor_scalar(self.rowa[0:1, :], ps_sx[0:1, 0:512],
                                              1.0 / D, None, ALU.mult), inc=[("ev", 1)])
        P.emit("v", lambda e: e.tensor_scalar(self.rowb[0:1, :], ps_sq[0:1, 0:512],
                                              1.0 / D, None, ALU.mult), inc=[("ev", 1)])
        P.wait("a", "ev")
        P.emit("a", lambda e: e.activation(self.rowc[0:1, :], self.rowa[0:1, :],
                                           AF.Square), inc=[("ac", 1)])
        P.wait("v", "ac")
        P.emit("v", lambda e: e.tensor_sub(self.rowb[0:1, :], self.rowb[0:1, :],
                                           self.rowc[0:1, :]), inc=[("ev", 1)])
        P.wait("a", "ev")
        P.emit("a", lambda e: e.activation(self.rowc[0:1, :], self.rowb[0:1, :],
                                           AF.Ln, bias=self.epsc[0:1, :]))
        P.emit("a", lambda e: e.activation(self.rowc[0:1, :], self.rowc[0:1, :],
                                           AF.Exp, scale=-0.5), inc=[("ac", 1)])
        scale_row = self.rowc
        if mode == "mamba":
            P.wait("v", "ac")
            P.emit("v", lambda e: e.tensor_copy(self.mrow[0:1, tglob:tglob + 512],
                                                self.rowa[0:1, :]))
            P.emit("v", lambda e: e.tensor_copy(self.srow[0:1, tglob:tglob + 512],
                                                self.rowc[0:1, :]))
            P.emit("v", lambda e: e.tensor_mul(self.rowd[0:1, :], self.rowc[0:1, :],
                                               self.rowc[0:1, :]))
            P.emit("v", lambda e: e.tensor_mul(self.rowd[0:1, :], self.rowd[0:1, :],
                                               self.rowb[0:1, :]), inc=[("ev", 1)])
            P.wait("a", "ev")
            P.emit("a", lambda e: e.activation(self.rowd[0:1, :], self.rowd[0:1, :],
                                               AF.Ln, bias=self.epsc[0:1, :]))
            P.emit("a", lambda e: e.activation(self.rowd[0:1, :], self.rowd[0:1, :],
                                               AF.Exp, scale=-0.5), inc=[("ac", 1)])
            P.wait("v", "ac")
            P.emit("v", lambda e: e.tensor_mul(self.rowd[0:1, :], self.rowd[0:1, :],
                                               self.rowc[0:1, :]), inc=[("ev", 1)])
            scale_row = self.rowd
        P.wait("t", "ev" if mode == "mamba" else "ac")
        P.emit("t", lambda e: e.matmul(ps_r1[:, 0:512], self.ones_rf[:, :],
                                       self.rowa[0:1, :], start=True, stop=True))
        P.emit("t", lambda e, sr=scale_row: e.matmul(ps_r2[:, 0:512], self.ones_rf[:, :],
                                                     sr[0:1, :], start=True, stop=True),
               inc=[("mm", 1)])
        P.wait("v", "mm")
        for dc in range(ND):
            P.emit("v", lambda e, dc=dc: e.tensor_sub(self.tf32[:, 0:512], xsrc(dc),
                                                      ps_r1[:, 0:512]))
            if mode == "mamba":
                P.emit("v", lambda e, dc=dc: e.tensor_mul(self.tf32[:, 0:512],
                                                          self.tf32[:, 0:512],
                                                          ps_r2[:, 0:512]))
                P.emit("v", lambda e, dc=dc: e.tensor_scalar(
                    sink(dc), self.tf32[:, 0:512], self.normw_sb[:, dc:dc + 1], None,
                    ALU.mult), inc=[("ev", 1)])
            else:
                P.emit("v", lambda e, dc=dc: e.tensor_mul(sink(dc), self.tf32[:, 0:512],
                                                          ps_r2[:, 0:512]),
                       inc=[("ev", 1)])
        self.u_done = self.cnt["ev"]

    def _ffn_call(self, src, dst, first):
        b = self
        nc = self.nc
        with contextlib.ExitStack() as fs:
            s = lambda n, sh, d: fs.enter_context(nc.sbuf_tensor(self._nm(n), sh, d))
            self.xh = s("xh", [128, ND * HALF], F32)
            self.uh = s("uh", [128, ND * HALF], BF16)
            self.gp_b = s("gp_b", [128, NHC * HALF], BF16)
            self.wbuf = s("wbuf", [128, 2 * ND * 128], BF16)
            self.wrep = s("wrep", [128, HALF], F32)
            self.xsq = s("xsq", [128, ND * 512], BF16)
            self.tf32 = s("tf32", [128, 512], F32)
            self.rowa = s("rowa", [1, 512], F32)
            self.rowb = s("rowb", [1, 512], F32)
            self.rowc = s("rowc", [1, 512], F32)
            self.rowd = s("rowd", [1, 512], F32)
            self.oh_sb = s("oh_sb", [E, E * 128], F32)
            self.wTh = s("wTh", [E, HALF], F32)
            self.probs = s("probs", [E, 512], F32)
            self.ptm = s("ptm", [128, 8], F32)
            self.maxs = s("maxs", [128, 8], F32)
            self.wtm = s("wtm", [128, 8], F32)
            self.v12 = s("v12", [128, 2], F32)

            P = Prog(b)
            P.emit("s", lambda e: e.dma_start(out=self.oh_sb[:, :], in_=self.oh_in),
                   inc=[("ld", 16)])
            P.wait("s", "ld")
            P.flush()

            for ha in range(2):
                self._ffn_half(src, dst, ha, first)

    def _ffn_half(self, src, dst, ha, first):
        b = self
        nc = self.nc
        t0 = ha * HALF
        P = Prog(b)
        for dc in range(ND):
            P.emit("s", lambda e, dc=dc: e.dma_start(
                out=self.xh[:, dc * HALF:(dc + 1) * HALF],
                in_=src[dc * 128:(dc + 1) * 128, t0:t0 + HALF]), inc=[("ld", 16)])
        P.wait("s", "ld")
        P.flush()

        with contextlib.ExitStack() as pes:
            ps_a0 = self.psum(pes, "ps_a0", [128, 512])
            ps_a1 = self.psum(pes, "ps_a1", [128, 512])
            ps_sx = self.psum(pes, "ps_sx", [128, 512])
            ps_sq = self.psum(pes, "ps_sq", [128, 512])
            ps_r1 = self.psum(pes, "ps_r1", [128, 512])
            ps_r2 = self.psum(pes, "ps_r2", [128, 512])
            ps_x = self.psum(pes, "ps_x", [128, 512])
            psb = (ps_sx, ps_sq, ps_r1, ps_r2)
            ps_mm = [ps_a0, ps_a1]

            P = Prog(b)
            u_tile_done = []
            for ti in range(2):
                xsrc = lambda dc, ti=ti: self.xh[:, dc * HALF + ti * 512:
                                                 dc * HALF + (ti + 1) * 512]
                sink = lambda dc, ti=ti: self.uh[:, dc * HALF + ti * 512:
                                                 dc * HALF + (ti + 1) * 512]
                self._ln_tile(P, xsrc, sink, t0 + ti * 512, psb, mode="ln")
                u_tile_done.append(self.u_done)
                P.wait("t", "ev")
                for dc in range(ND):
                    P.emit("t", lambda e, dc=dc, ti=ti: e.matmul(
                        ps_x[0:8, 0:512], self.wr_sb[:, dc * E:(dc + 1) * E],
                        self.uh[:, dc * HALF + ti * 512:dc * HALF + (ti + 1) * 512],
                        start=(dc == 0), stop=(dc == ND - 1)), inc=(
                        [("mm", 1)] if dc == ND - 1 else None))
                P.wait("a", "mm")
                P.emit("a", lambda e: e.activation(self.probs[:, 0:512], ps_x[0:8, 0:512],
                                                   AF.Sigmoid, bias=self.biasr[:, :]),
                       inc=[("ac", 1)])
                for j in range(4):
                    P.wait("t", "ac")
                    if j > 0 or ti > 0:
                        P.wait("t", "ev")
                    P.emit("t", lambda e, j=j: e.transpose(
                        ps_x[0:128, 0:8] if False else ps_x[:, 0:8],
                        self.probs[0:8, j * 128:(j + 1) * 128], self.idf[0:8, 0:8]),
                        inc=[("mm", 1)])
                    P.wait("v", "mm")
                    for fn in [
                        lambda e: e.tensor_copy(self.ptm[:, :], ps_x[:, 0:8]),
                        lambda e: e.max(out=self.maxs[:, :], in_=self.ptm[:, :]),
                        lambda e: e.tensor_add(self.v12[:, 0:1], self.maxs[:, 0:1],
                                               self.maxs[:, 1:2]),
                        lambda e: e.reciprocal(self.v12[:, 1:2], self.v12[:, 0:1]),
                        lambda e: e.tensor_scalar(self.wtm[:, :], self.ptm[:, :],
                                                  self.maxs[:, 1:2], None, ALU.is_ge),
                        lambda e: e.tensor_mul(self.wtm[:, :], self.wtm[:, :],
                                               self.ptm[:, :]),
                        lambda e: e.tensor_scalar(self.wtm[:, :], self.wtm[:, :],
                                                  self.v12[:, 1:2], None, ALU.mult),
                    ]:
                        P.emit("v", fn, inc=[("ev", 1)])
                        P.wait("v", "ev")
                    P.wait("t", "ev")
                    P.emit("t", lambda e, j=j: e.transpose(
                        ps_x[0:8, 8:136], self.wtm[:, :], self.idf[:, :]),
                        inc=[("mm", 1)])
                    P.wait("v", "mm")
                    P.emit("v", lambda e, j=j, ti=ti: e.tensor_copy(
                        self.wTh[0:8, ti * 512 + j * 128:ti * 512 + (j + 1) * 128],
                        ps_x[0:8, 8:136]), inc=[("ev", 1)])
            self.wth_done = self.cnt["ev"]
            P.flush()

            for e_ in range(9):
                self._ffn_expert(e_, ha, first, ps_mm, ps_x, (ps_r1, ps_r2))

        P = Prog(b)
        P.wait("s", "ev")
        for dc in range(ND):
            P.emit("s", lambda e, dc=dc: e.dma_start(
                out=dst[dc * 128:(dc + 1) * 128, t0:t0 + HALF],
                in_=self.xh[:, dc * HALF:(dc + 1) * HALF]), inc=[("st", 16)])
        P.wait("s", "st")
        P.flush()

    def _ffn_expert(self, e_, ha, first, ps_mm, ps_x, ps_wreps):
        b = self
        P = Prog(b)
        shared = (e_ == 0)
        if shared:
            wg, wu, wd = self.wg_s_bf.ap(), self.wu_s_bf.ap(), self.wd_s_bf.ap()
            castkeys = ("wg_s", "wu_s", "wd_s")
        else:
            wg, wu, wd = (self.wg_bf.ap()[e_ - 1], self.wu_bf.ap()[e_ - 1],
                          self.wd_bf.ap()[e_ - 1])
            castkeys = (f"wg{e_ - 1}", f"wu{e_ - 1}", f"wd{e_ - 1}")

        if not shared:
            P.wait("t", "ev")
            for ti in range(2):
                pw = ps_wreps[ti % 2]
                P.emit("t", lambda e, ti=ti, pw=pw: e.matmul(
                    pw[:, 0:512], self.oh_sb[:, (e_ - 1) * 128:e_ * 128],
                    self.wTh[0:8, ti * 512:(ti + 1) * 512], start=True, stop=True),
                    inc=[("mm", 1)])
                P.wait("v", "mm")
                P.emit("v", lambda e, ti=ti, pw=pw: e.tensor_copy(
                    self.wrep[:, ti * 512:(ti + 1) * 512], pw[:, 0:512]),
                    inc=[("ev", 1)])
        self.wrep_done = self.cnt["ev"]

        nbuf = ND * 128
        cols = [("g", i) for i in range(NHC)] + [("u", i) for i in range(NHC)] + \
               [("d", i) for i in range(ND)]
        if first:
            grp = "c0" if e_ <= 1 else ("c1" if e_ <= 4 else "c2")
            P.wait("s", grp, self.castgrp[grp])

        mm_of_tile = {}
        ev_free = []
        if not hasattr(self, "mm_after_col"):
            self.mm_after_col = []
        tidx = 0
        P.wait("t", "ev")
        P.wait("t", "ac")
        for ci, (kind, col) in enumerate(cols):
            gci = self.wf_hist
            if gci >= 2:
                P.wait("s", "mm", self.mm_after_col[gci - 2])
            self.wf_hist += 1
            if kind == "g":
                srcw = wg[:, col * 128:(col + 1) * 128]
            elif kind == "u":
                srcw = wu[:, col * 128:(col + 1) * 128]
            else:
                srcw = wd[:, col * 128:(col + 1) * 128]
            buf = self.wf_hist % 2
            wsem = "w0" if buf == 0 else "w1"
            nkw = ND if kind != "d" else NHC
            P.emit("s", lambda e, srcw=srcw, buf=buf, nkw=nkw: e.dma_start(
                out=self.wbuf[:, buf * nbuf:buf * nbuf + nkw * 128].rearrange(
                    "p (c n) -> p c n", n=128),
                in_=srcw.rearrange("(c p) n -> p c n", p=128)), inc=[(wsem, 16)])

            P.wait("t", wsem)
            nk = ND if kind != "d" else NHC
            for ti in range(2):
                pp = ps_mm[tidx % 2]
                if tidx >= 2:
                    P.wait("t", "ev" if ev_free[tidx - 2][0] == "ev" else "ac",
                           ev_free[tidx - 2][1])
                for k in range(nk):
                    if kind == "d":
                        rhs = self.gp_b[:, k * HALF + ti * 512:k * HALF + (ti + 1) * 512]
                    else:
                        rhs = self.uh[:, k * HALF + ti * 512:k * HALF + (ti + 1) * 512]
                    last = (k == nk - 1)
                    incs = [("mm", 1)] if last else None
                    P.emit("t", lambda e, pp=pp, k=k, rhs=rhs, buf=buf, kk=(k == 0),
                           st=last: e.matmul(
                        pp[:, 0:512],
                        self.wbuf[:, buf * nbuf + k * 128:buf * nbuf + (k + 1) * 128],
                        rhs, start=kk, stop=st), inc=incs)
                mm_of_tile[(kind, col, ti)] = self.cnt["mm"]

                gslice = self.gp_b[:, col * HALF + ti * 512:col * HALF + (ti + 1) * 512] \
                    if kind != "d" else None
                if kind == "g":
                    P.wait("a", "mm", mm_of_tile[(kind, col, ti)])
                    P.emit("a", lambda e, pp=pp, gs=gslice: e.activation(
                        gs, pp[:, 0:512], AF.Silu), inc=[("ac", 1)])
                    ev_free.append(("ac", self.cnt["ac"]))
                elif kind == "u":
                    P.wait("v", "mm", mm_of_tile[(kind, col, ti)])
                    if not shared:
                        P.emit("v", lambda e, pp=pp, gs=gslice: e.tensor_mul(
                            self.tf32[:, 0:512], pp[:, 0:512], gs))
                        P.emit("v", lambda e, gs=gslice, ti=ti: e.tensor_mul(
                            gs, self.tf32[:, 0:512],
                            self.wrep[:, ti * 512:(ti + 1) * 512]), inc=[("ev", 1)])
                    else:
                        P.emit("v", lambda e, pp=pp, gs=gslice: e.tensor_mul(
                            gs, pp[:, 0:512], gs), inc=[("ev", 1)])
                    ev_free.append(("ev", self.cnt["ev"]))
                else:
                    xsl = self.xh[:, col * HALF + ti * 512:col * HALF + (ti + 1) * 512]
                    P.wait("v", "mm", mm_of_tile[(kind, col, ti)])
                    P.emit("v", lambda e, pp=pp, xsl=xsl: e.scalar_tensor_tensor(
                        xsl, pp[:, 0:512], 0.5, xsl, ALU.mult, ALU.add),
                        inc=[("ev", 1)])
                    ev_free.append(("ev", self.cnt["ev"]))
                tidx += 1
            self.mm_after_col.append(self.cnt["mm"])
        P.flush()

    def _copy_out(self, src):
        P = Prog(self)
        for dc in range(ND):
            P.emit("s", lambda e, dc=dc: e.dma_start(
                out=self.y_out[dc * 128:(dc + 1) * 128, :],
                in_=src[dc * 128:(dc + 1) * 128, :]), inc=[("st", 16)])
        P.wait("s", "st")
        P.flush()

    def _final_wait(self):
        P = Prog(self)
        P.wait("g", "st")
        P.flush()

    def _mamba(self, src, dst):
        emit_mamba(self, src, dst)




def emit_mamba(b, src, dst):
    nc = b.nc
    b.ev_free_m = {}
    b.ev_free_o = {}
    with contextlib.ExitStack() as ms:
        s = lambda n, sh, d: ms.enter_context(nc.sbuf_tensor(b._nm(n), sh, d))
        psum = lambda es, n, sh, d=F32: b.psum(es, n, sh, d)
        b.mrow = s("mrow", [1, TOK], F32)
        b.srow = s("srow", [1, TOK], F32)
        b.innw_rep = s("innw_rep", [128, DI], BF16)
        b.b_bf = s("b_bf", [128, TOK], BF16)
        b.c_bf = s("c_bf", [128, TOK], BF16)
        b.dtraw = s("dtraw", [NH, TOK], F32)
        b.halo = s("halo", [128, NKI * 3], BF16)
        b.convinit_sb = s("convinit_sb", [128, NKI * 3], F32)
        b.tot = s("tot", [NH, NCH + 1], F32)
        b.deT = s("deT", [128, NCH * NH], F32)
        b.prow = s("prow", [1, (NCH + 1) * NH], F32)
        b.pcols = s("pcols", [128, (NCH + 1) * NH], F32)
        b.runall = s("runall", [128, NCH * NH], F32)
        b.bT = s("bT", [128, NCH * 128], BF16)


        P = Prog(b)
        P.emit("g", lambda e: e.dma_start(out=b.innw_rep[:, :],
                                          in_=b.innw_in.broadcast_to([128, DI])),
               inc=[("gp", 16)])
        P.emit("s", lambda e: e.dma_start(out=b.convinit_sb[:, :], in_=b.convinit_in),
               inc=[("ld", 16)])
        P.emit("v", lambda e: e.tensor_scalar(b.cm128i[:, :], b.cm128[:, :], -1.0, 1.0,
                                              ALU.mult, ALU.add))
        P.wait("g", "gp")
        P.wait("s", "ld")
        P.flush()

        _m1_u_inproj(b, ms, src)
        _m2_prep(b, ms)
        with contextlib.ExitStack() as bs:
            b.big = bs.enter_context(nc.sbuf_tensor(
                b._nm("big"), [128, NKI * NCH * 128], BF16))
            b.hst = bs.enter_context(nc.sbuf_tensor(
                b._nm("hst"), [128, NH * PH], F32))
            _m3_conv(b, ms)
            _m4_pass1_exchange(b, ms)
            _m5_pass2_gate(b, ms)
        _m6_outproj(b, ms, src, dst)


def _m1_u_inproj(b, ms, src):
    nc = b.nc
    with contextlib.ExitStack() as fs:
        s = lambda n, sh, d: fs.enter_context(nc.sbuf_tensor(b._nm(n), sh, d))
        b.uh = s("uh", [128, ND * TOK], BF16)
        b.xh = None
        xt = s("xt", [128, ND * 512], F32)
        b.xsq = s("xsq", [128, ND * 512], BF16)
        b.tf32 = s("tf32", [128, 512], F32)
        b.rowa = s("rowa", [1, 512], F32)
        b.rowb = s("rowb", [1, 512], F32)
        b.rowc = s("rowc", [1, 512], F32)
        b.rowd = s("rowd", [1, 512], F32)
        wbuf = s("wbufm", [128, 2 * ND * 128], BF16)
        zst = s("zst", [128, 2 * 512], BF16)

        with contextlib.ExitStack() as pes:
            ps_a0 = b.psum(pes, "ps_a0", [128, 512])
            ps_a1 = b.psum(pes, "ps_a1", [128, 512])
            ps_sx = b.psum(pes, "ps_sx", [128, 512])
            ps_sq = b.psum(pes, "ps_sq", [128, 512])
            ps_r1 = b.psum(pes, "ps_r1", [128, 512])
            ps_r2 = b.psum(pes, "ps_r2", [128, 512])
            psb = (ps_sx, ps_sq, ps_r1, ps_r2)
            ps_mm = [ps_a0, ps_a1]

            P = Prog(b)
            for tau in range(4):
                P.wait("s", "mm")
                P.wait("s", "ev")
                for dc in range(ND):
                    P.emit("s", lambda e, dc=dc, tau=tau: e.dma_start(
                        out=xt[:, dc * 512:(dc + 1) * 512],
                        in_=src[dc * 128:(dc + 1) * 128, tau * 512:(tau + 1) * 512]),
                        inc=[("ld", 16)])
                P.wait("s", "ld")
                P.flush()
                P = Prog(b)
                xsrc = lambda dc: xt[:, dc * 512:(dc + 1) * 512]
                sink = lambda dc, tau=tau: b.uh[:, dc * TOK + tau * 512:
                                                dc * TOK + (tau + 1) * 512]
                b._ln_tile(P, xsrc, sink, tau * 512, psb, mode="mamba")
                P.flush()
                P = Prog(b)

            P.wait("t", "ev")
            P.wait("t", "ac")
            if "ffn1" not in b.phases:
                P.wait("s", "c3", b.castgrp["c3"])
            zpar_cnt = [0, 0]
            zpar_ev = [0, 0]
            for oc in range(50):
                gci = b.wf_hist
                if gci >= 2:
                    P.wait("s", "mm", b.mm_after_col[gci - 2])
                b.wf_hist += 1
                buf = b.wf_hist % 2
                wsem = "w0" if buf == 0 else "w1"
                srcw = b.inprojT_bf.ap()[:, oc * 128:(oc + 1) * 128]
                P.emit("s", lambda e, srcw=srcw, buf=buf: e.dma_start(
                    out=wbuf[:, buf * ND * 128:(buf + 1) * ND * 128].rearrange(
                        "p (c n) -> p c n", n=128),
                    in_=srcw.rearrange("(c p) n -> p c n", p=128)), inc=[(wsem, 16)])
                P.wait("t", wsem)
                for tau in range(4):
                    ti = oc * 4 + tau
                    pp = ps_mm[ti % 2]
                    if ti >= 2:
                        kind_prev, cnt_prev = b.ev_free_m[ti - 2]
                        P.wait("t", kind_prev, cnt_prev)
                    for k in range(ND):
                        P.emit("t", lambda e, pp=pp, k=k, buf=buf, tau=tau: e.matmul(
                            pp[:, 0:512],
                            wbuf[:, buf * ND * 128 + k * 128:buf * ND * 128 + (k + 1) * 128],
                            b.uh[:, k * TOK + tau * 512:k * TOK + (tau + 1) * 512],
                            start=(k == 0), stop=(k == ND - 1)),
                            inc=([("mm", 1)] if k == ND - 1 else None))
                    mmv = b.cnt["mm"]
                    if oc < 48:
                        zp = ti % 2
                        zsem = "z0" if zp == 0 else "z1"
                        if zpar_cnt[zp] > 0:
                            P.wait("a", zsem, zpar_cnt[zp])
                        if zpar_ev[zp] > 0:
                            P.wait("a", "ev", zpar_ev[zp])
                        P.wait("a", "mm", mmv)
                        P.emit("a", lambda e, pp=pp, zp=zp: e.activation(
                            zst[:, zp * 512:(zp + 1) * 512], pp[:, 0:512], AF.Copy),
                            inc=[("ac", 1)])
                        b.ev_free_m[ti] = ("ac", b.cnt["ac"])
                        P.wait("s", "ac", b.cnt["ac"])
                        dstt = b.z_d.ap() if oc < 24 else b.xc_d.ap()
                        r0 = (oc % 24) * 128
                        P.emit("s", lambda e, dstt=dstt, r0=r0, tau=tau, zp=zp:
                               e.dma_start(out=dstt[r0:r0 + 128,
                                                    tau * 512:(tau + 1) * 512],
                                           in_=zst[:, zp * 512:(zp + 1) * 512]),
                               inc=[(zsem, 16)])
                        zpar_cnt[zp] = b.cnt[zsem]
                        if oc >= 24 and tau == 3:
                            P.wait("v", "ac", b.ev_free_m[ti][1])
                            P.emit("v", lambda e, oc=oc, zp=zp: e.tensor_copy(
                                b.halo[:, (oc - 24) * 3:(oc - 23) * 3],
                                zst[:, zp * 512 + 509:zp * 512 + 512]), inc=[("ev", 1)])
                            zpar_ev[zp] = b.cnt["ev"]
                    elif oc == 48:
                        P.wait("v", "mm", mmv)
                        P.emit("v", lambda e, pp=pp, tau=tau: e.tensor_copy(
                            b.b_bf[:, tau * 512:(tau + 1) * 512], pp[:, 0:512]),
                            inc=[("ev", 1)])
                        b.ev_free_m[ti] = ("ev", b.cnt["ev"])
                    else:
                        P.wait("v", "mm", mmv)
                        P.emit("v", lambda e, pp=pp, tau=tau: e.tensor_copy(
                            b.c_bf[:, tau * 512:(tau + 1) * 512], pp[:, 0:512]),
                            inc=[("ev", 1)])
                        b.ev_free_m[ti] = ("ev", b.cnt["ev"])
                b.mm_after_col.append(b.cnt["mm"])
            for tau in range(4):
                ti = 200 + tau
                pp = ps_mm[tau % 2]
                kind_prev, cnt_prev = b.ev_free_m[198 + tau if tau < 2 else 200 + tau - 2]
                P.wait("t", kind_prev, cnt_prev)
                for k in range(ND):
                    P.emit("t", lambda e, pp=pp, k=k, tau=tau: e.matmul(
                        pp[0:NH, 0:512], b.dtlhs[:, k * NH:(k + 1) * NH],
                        b.uh[:, k * TOK + tau * 512:k * TOK + (tau + 1) * 512],
                        start=(k == 0), stop=(k == ND - 1)),
                        inc=([("mm", 1)] if k == ND - 1 else None))
                P.wait("v", "mm")
                P.emit("v", lambda e, pp=pp, tau=tau: e.tensor_copy(
                    b.dtraw[:, tau * 512:(tau + 1) * 512], pp[0:NH, 0:512]),
                    inc=[("ev", 1)])
                b.ev_free_m[200 + tau] = ("ev", b.cnt["ev"])
            P.wait("s", "ev")
            P.emit("s", lambda e: e.dma_start(out=b.cc_halo_in.ap(), in_=b.halo[:, :]),
                   inc=[("st", 16)])
            P.wait("s", "st")
            P.flush()

        P = Prog(b)
        P.emit("g", lambda e: e.collective_compute(
            "AllGather", ALU.bypass, replica_groups=[list(range(NCORES))],
            ins=[b.cc_halo_in.ap().opt()], outs=[b.cc_halo_out.ap().opt()]),
            inc=[("cc", 1)])
        P.wait("g", "cc")
        P.flush()


def _m2_prep(b, ms):
    nc = b.nc
    with contextlib.ExitStack() as fs:
        s = lambda n, sh, d: fs.enter_context(nc.sbuf_tensor(b._nm(n), sh, d))
        la = s("la", [NH, TOK], F32)
        afm = s("afm", [NH, TOK], F32)
        cum = s("cum", [NH, TOK], F32)
        sc48 = s("sc48", [NH, TOK], F32)
        sc48b = s("sc48b", [NH, TOK], F32)
        agh = s("agh", [128, NKI * 3], BF16)

        P = Prog(b)
        P.emit("a", lambda e: e.activation(sc48[:, :], b.dtraw[:, :], AF.Abs,
                                           bias=b.dtb[:, :], scale=1.0))
        P.emit("a", lambda e: e.activation(sc48[:, :], sc48[:, :], AF.Exp, scale=-1.0))
        P.emit("a", lambda e: e.activation(sc48[:, :], sc48[:, :], AF.Ln, bias=1.0),
               inc=[("ac", 1)])
        P.emit("a", lambda e: e.activation(b.acol[:, :], b.acol[:, :], AF.Exp),
               inc=[("ac", 1)])
        P.wait("v", "ac")
        P.emit("v", lambda e: e.tensor_scalar(sc48b[:, :], b.dtraw[:, :], b.dtb[:, :],
                                              0.0, ALU.add, ALU.max), inc=[("ev", 1)])
        P.emit("v", lambda e: e.tensor_add(sc48b[:, :], sc48b[:, :], sc48[:, :]),
               inc=[("ev", 1)])
        P.emit("v", lambda e: e.tensor_scalar(la[:, :], sc48b[:, :], b.acol[:, :],
                                              -1.0, ALU.mult, ALU.mult), inc=[("ev", 1)])
        P.wait("a", "ev")
        P.emit("a", lambda e: e.activation(afm[:, :], la[:, :], AF.Exp), inc=[("ac", 1)])
        P.wait("v", "ac")
        for c in range(NCH):
            sl = slice(c * 128, (c + 1) * 128)
            P.emit("v", lambda e, sl=sl: e.tensor_tensor_scan(
                cum[:, sl], la[:, sl], la[:, sl], 0.0, ALU.add, ALU.bypass))
            P.emit("v", lambda e, c=c, sl=sl: e.reduce_sum(
                b.tot[:, c:c + 1], la[:, sl], axis=AX.X), inc=[("ev", 1)])
            P.wait("v", "ev")
        P.emit("v", lambda e: e.reduce_sum(b.tot[:, NCH:NCH + 1], la[:, :], axis=AX.X),
               inc=[("ev", 1)])
        P.wait("v", "ev")
        P.wait("a", "ev")
        for c in range(NCH):
            sl = slice(c * 128, (c + 1) * 128)
            P.emit("a", lambda e, c=c, sl=sl: e.activation(
                sc48[:, sl], cum[:, sl], AF.Exp, bias=b.tot[:, c:c + 1], scale=-1.0))
        P.emit("a", lambda e: e.activation(sc48b[:, 0:NCH + 1], b.tot[:, 0:NCH + 1],
                                           AF.Exp), inc=[("ac", 1)])
        P.wait("s", "ac")
        P.emit("s", lambda e: e.dma_start(
            out=b.a_d.ap().rearrange("o (hh t) -> (o hh) t", t=TOK), in_=afm[:, :]),
               inc=[("st", 16)])
        P.emit("s", lambda e: e.dma_start(
            out=b.bounce.ap()[0:1, 0:NH * (NCH + 1)].rearrange(
                "o (hh c) -> (o hh) c", c=NCH + 1),
            in_=sc48b[:, 0:NCH + 1]), inc=[("st", 16)])
        P.wait("s", "st")
        for c in range(NCH + 1):
            P.emit("s", lambda e, c=c: e.dma_start(
                out=b.prow[0:1, c * NH:(c + 1) * NH],
                in_=b.bounce.ap()[0:1, 0:NH * (NCH + 1)].rearrange(
                    "o (hh c) -> o c hh", c=NCH + 1)[:, c, :],
                allow_slow_non_contiguous=True), inc=[("ld", 16)])
        P.wait("s", "ld")
        P.emit("s", lambda e: e.dma_start(
            out=agh[:, :].rearrange("p (k t) -> p k t", t=3) if False else agh[:, :],
            in_=b.cc_halo_out.ap()[0:128, :]), inc=[("ld", 16)])
        P.flush()

        with contextlib.ExitStack() as pes:
            ps_t = b.psum(pes, "ps_t", [128, 512])
            ps_tb = b.psum(pes, "ps_tb", [128, 1024], BF16)
            items = []
            for c in range(NCH):
                items.append((
                    (lambda t, c=c: t.transpose(ps_t[:, 0:NH],
                                                sc48[0:NH, c * 128:(c + 1) * 128],
                                                b.idf[0:NH, 0:NH])),
                    (lambda v, c=c: v.tensor_copy(b.deT[:, c * NH:(c + 1) * NH],
                                                  ps_t[:, 0:NH])),
                ))
            for c in range(NCH + 1):
                items.append((
                    (lambda t, c=c: t.matmul(ps_t[:, 0:NH], b.ones_rf[:, :],
                                             b.prow[0:1, c * NH:(c + 1) * NH],
                                             start=True, stop=True)),
                    (lambda v, c=c: v.tensor_copy(b.pcols[:, c * NH:(c + 1) * NH],
                                                  ps_t[:, 0:NH])),
                ))
            for c in range(NCH):
                items.append((
                    (lambda t, c=c: t.transpose(ps_tb[:, 0:128],
                                                b.b_bf[:, c * 128:(c + 1) * 128],
                                                b.idb[:, :])),
                    (lambda v, c=c: v.tensor_copy(b.bT[:, c * 128:(c + 1) * 128],
                                                  ps_tb[:, 0:128])),
                ))
            P = Prog(b)
            pe_b, dv_b = b.cnt["mm"], b.cnt["ev"]
            n = len(items)
            for i, (epe, _edv) in enumerate(items):
                if i > 0:
                    P.wait("t", "ev", dv_b + i)
                P.emit("t", epe, inc=[("mm", 1)])
            for i, (_epe, edv) in enumerate(items):
                P.wait("v", "mm", pe_b + i + 1)
                P.emit("v", edv, inc=[("ev", 1)])
            P.emit("g", lambda e: e.memset(b.runall[:, (NCH - 1) * NH:NCH * NH], 1.0))
            P.flush()
            P = Prog(b)
            for c in range(NCH - 2, -1, -1):
                P.emit("v", lambda e, c=c: e.tensor_mul(
                    b.runall[:, c * NH:(c + 1) * NH],
                    b.runall[:, (c + 1) * NH:(c + 2) * NH],
                    b.pcols[:, (c + 1) * NH:(c + 2) * NH]), inc=[("ev", 1)])
                P.wait("v", "ev")
            P.wait("v", "ld")
            P.emit("v", lambda e: e.tensor_scalar(
                b.halo[:, :], agh[:, :], b.hmask_sb[:, 0:1], None, ALU.mult),
                inc=[("ev", 1)])
            P.flush()
            P = Prog(b)
            for d in range(1, NCORES):
                P.emit("s", lambda e, d=d: e.dma_start(
                    out=agh[:, :], in_=b.cc_halo_out.ap()[d * 128:(d + 1) * 128, :]),
                    inc=[("ld", 16)])
                P.wait("s", "ld")
                P.flush()
                P = Prog(b)
                P.wait("v", "ld")
                P.emit("v", lambda e, d=d: e.scalar_tensor_tensor(
                    b.halo[:, :], agh[:, :], b.hmask_sb[:, d:d + 1], b.halo[:, :],
                    ALU.mult, ALU.add), inc=[("ev", 1)])
                P.flush()
                P = Prog(b)
            P.emit("v", lambda e: e.tensor_add(b.halo[:, :], b.halo[:, :],
                                               b.convinit_sb[:, :]), inc=[("ev", 1)])
            P.flush()


def _m3_conv(b, ms):
    nc = b.nc
    with contextlib.ExitStack() as fs:
        s = lambda n, sh, d: fs.enter_context(nc.sbuf_tensor(b._nm(n), sh, d))
        xck = s("xck", [128, 3 + TOK], BF16)
        convt = s("convt", [128, TOK], BF16)
        with contextlib.ExitStack() as pes:
            ps_c = b.psum(pes, "ps_c", [128, 512])
            ps_tb = b.psum(pes, "ps_tb", [128, 1024], BF16)
            for k in range(NKI):
                P = Prog(b)
                P.emit("s", lambda e, k=k: e.dma_start(
                    out=xck[:, 3:3 + TOK],
                    in_=b.xc_d.ap()[k * 128:(k + 1) * 128, :]), inc=[("ld", 16)])
                P.wait("s", "ld")
                P.flush()
                P = Prog(b)
                P.emit("v", lambda e, k=k: e.tensor_copy(
                    xck[:, 0:3], b.halo[:, k * 3:(k + 1) * 3]), inc=[("ev", 1)])
                P.wait("v", "ev")
                for tc in range(4):
                    o = tc * 512
                    P.emit("v", lambda e, o=o, k=k: e.tensor_scalar(
                        ps_c[:, 0:512], xck[:, o:o + 512],
                        b.convw_sb[:, k * 4:k * 4 + 1], b.convb_sb[:, k:k + 1],
                        ALU.mult, ALU.add))
                    for kk in range(1, 4):
                        P.emit("v", lambda e, o=o, k=k, kk=kk: e.scalar_tensor_tensor(
                            ps_c[:, 0:512], xck[:, o + kk:o + kk + 512],
                            b.convw_sb[:, k * 4 + kk:k * 4 + kk + 1],
                            ps_c[:, 0:512], ALU.mult, ALU.add))
                    P.emit("v", lambda e: e.engine_nop(), inc=[("ev", 1)])
                    P.wait("a", "ev")
                    P.emit("a", lambda e, o=o: e.activation(
                        convt[:, o:o + 512], ps_c[:, 0:512], AF.Silu), inc=[("ac", 1)])
                    P.wait("v", "ac")
                for c in range(NCH):
                    P.wait("t", "ac")
                    P.wait("t", "ev")
                    P.emit("t", lambda e, c=c: e.transpose(
                        ps_tb[:, 0:128], convt[:, c * 128:(c + 1) * 128], b.idb[:, :]),
                        inc=[("mm", 1)])
                    P.wait("v", "mm")
                    P.emit("v", lambda e, k=k, c=c: e.tensor_copy(
                        b.big[:, k * TOK + c * 128:k * TOK + (c + 1) * 128],
                        ps_tb[:, 0:128]), inc=[("ev", 1)])
                P.flush()


def xslice(b, hh, c):
    k, half = divmod(hh, 2)
    off = k * TOK + c * 128 + half * 64
    return b.big[:, off:off + 64]


def _m4_pass1_exchange(b, ms):
    nc = b.nc
    with contextlib.ExitStack() as fs:
        s = lambda n, sh, d: fs.enter_context(nc.sbuf_tensor(b._nm(n), sh, d))
        bhat = s("bhat", [128, 2 * 128], BF16)
        sstage = s("sstage", [128, NH * PH // 2], F32)
        with contextlib.ExitStack() as pes:
            ps_sc = b.psum(pes, "ps_sc", [128, NH * PH])
            P = Prog(b)
            pe_b, dv_b = b.cnt["mm"], b.cnt["ev"]
            i = 0
            for hh in range(NH):
                for ci, c in enumerate(reversed(range(NCH))):
                    P.wait("t", "ev", dv_b + i + 1)
                    P.emit("t", lambda e, hh=hh, c=c, ci=ci, i=i: e.matmul(
                        ps_sc[:, hh * PH:(hh + 1) * PH],
                        bhat[:, (i % 2) * 128:(i % 2) * 128 + 128],
                        xslice(b, hh, c),
                        start=(ci == 0), stop=(ci == NCH - 1)), inc=[("mm", 1)])
                    i += 1
            i = 0
            for hh in range(NH):
                for ci, c in enumerate(reversed(range(NCH))):
                    if i >= 2:
                        P.wait("v", "mm", pe_b + i - 1)
                    P.emit("v", lambda e, hh=hh, c=c, i=i: e.tensor_scalar(
                        bhat[:, (i % 2) * 128:(i % 2) * 128 + 128],
                        b.bT[:, c * 128:(c + 1) * 128],
                        b.deT[:, c * NH + hh:c * NH + hh + 1],
                        b.runall[:, c * NH + hh:c * NH + hh + 1],
                        ALU.mult, ALU.mult), inc=[("ev", 1)])
                    i += 1
            P.flush()
            P = Prog(b)
            P.emit("g", lambda e: e.memset(sstage[0:1, :], 0.0))
            P.emit("g", lambda e: e.dma_start(
                out=b.cc_sum_in.ap()[128:129, 0:1536], in_=sstage[0:1, :]),
                inc=[("gp", 16)])
            P.emit("g", lambda e: e.dma_start(
                out=b.cc_sum_in.ap()[128:129, 1536:3072], in_=sstage[0:1, :]),
                inc=[("gp", 16)])
            P.wait("g", "gp")
            P.flush()
            P = Prog(b)
            P.wait("v", "mm")
            for hf in range(2):
                if hf == 1:
                    P.wait("v", "st")
                P.emit("v", lambda e, hf=hf: e.tensor_copy(
                    sstage[:, :], ps_sc[:, hf * 1536:(hf + 1) * 1536]), inc=[("ev", 1)])
                P.wait("s", "ev")
                P.emit("s", lambda e, hf=hf: e.dma_start(
                    out=b.cc_sum_in.ap()[0:128, hf * 1536:(hf + 1) * 1536],
                    in_=sstage[:, :]), inc=[("st", 16)])
                P.wait("s", "st")
            P.emit("s", lambda e: e.dma_start(
                out=b.cc_sum_in.ap()[128:129, :].rearrange("o (a hh) -> o a hh", hh=NH)
                if False else b.cc_sum_in.ap()[128:129, 0:NH],
                in_=b.prow[0:1, NCH * NH:(NCH + 1) * NH]), inc=[("st", 16)])
            P.wait("s", "st")
            P.flush()

        P = Prog(b)
        P.emit("g", lambda e: e.collective_compute(
            "AllGather", ALU.bypass, replica_groups=[list(range(NCORES))],
            ins=[b.cc_sum_in.ap().opt()], outs=[b.cc_sum_out.ap().opt()]),
            inc=[("cc", 1)])
        P.wait("g", "cc")
        P.flush()

        with contextlib.ExitStack() as fs:
            s = lambda n, sh, d: fs.enter_context(nc.sbuf_tensor(b._nm(n), sh, d))
            S_sb = s("S_sb", [128, NH * PH], F32)
            pd_row = s("pd_row", [1, NH], F32)
            acols = s("acols", [128, NH], F32)
            with contextlib.ExitStack() as pes:
                ps_pc = b.psum(pes, "ps_pc", [128, 512])
                P = Prog(b)
                P.emit("s", lambda e: e.dma_start(out=b.hst[:, :], in_=b.h0_in),
                       inc=[("ld", 16)])
                P.wait("s", "ld")
                P.flush()
                for d in range(NCORES):
                    P = Prog(b)
                    P.emit("s", lambda e, d=d: e.dma_start(
                        out=pd_row[0:1, :],
                        in_=b.cc_sum_out.ap()[d * 129 + 128:d * 129 + 129, 0:NH]),
                        inc=[("ld", 16)])
                    P.emit("s", lambda e, d=d: e.dma_start(
                        out=S_sb[:, :],
                        in_=b.cc_sum_out.ap()[d * 129:d * 129 + 128, :]),
                        inc=[("ld", 16)])
                    P.wait("s", "ld")
                    P.flush()
                    P = Prog(b)
                    P.emit("t", lambda e: e.matmul(
                        ps_pc[:, 0:NH], b.ones_rf[:, :], pd_row[0:1, :],
                        start=True, stop=True), inc=[("mm", 1)])
                    P.wait("v", "mm")
                    P.emit("v", lambda e, d=d: e.tensor_scalar(
                        acols[:, :], ps_pc[:, 0:NH], b.cm128[:, d:d + 1],
                        None, ALU.mult), inc=[("ev", 1)])
                    P.wait("v", "ev")
                    P.emit("v", lambda e, d=d: e.tensor_scalar(
                        acols[:, :], acols[:, :], b.cm128i[:, d:d + 1],
                        None, ALU.add), inc=[("ev", 1)])
                    P.wait("v", "ev")
                    P.emit("v", lambda e, d=d: e.tensor_scalar(
                        S_sb[:, :], S_sb[:, :], b.cm128[:, d:d + 1], None, ALU.mult),
                        inc=[("ev", 1)])
                    P.wait("v", "ev")
                    for hh in range(NH):
                        P.emit("v", lambda e, hh=hh: e.scalar_tensor_tensor(
                            b.hst[:, hh * PH:(hh + 1) * PH],
                            b.hst[:, hh * PH:(hh + 1) * PH],
                            acols[:, hh:hh + 1],
                            S_sb[:, hh * PH:(hh + 1) * PH], ALU.mult, ALU.add),
                            inc=[("ev", 1)])
                    P.flush()


def _m5_pass2_gate(b, ms):
    nc = b.nc
    with contextlib.ExitStack() as fs:
        s = lambda n, sh, d: fs.enter_context(nc.sbuf_tensor(b._nm(n), sh, d))
        mask_sb = s("mask", [128, 128], BF16)
        coef_sb = s("coef", [128, 128], F32)
        gm_sb = s("gm", [128, 128], BF16)
        mc_sb = s("mc", [128, 128], F32)
        bhat_sb = s("bhat2", [128, 128], BF16)
        ytm = s("ytm", [128, DI], BF16)
        yz = s("yz", [128, DI], BF16)
        ygst = s("ygst", [128, DI], BF16)
        arow = s("arow", [1, NH * 128 // 2], BF16)
        ycol = s("ycol", [128, 8], F32)
        with contextlib.ExitStack() as pes:
            ps_a = b.psum(pes, "ps_a", [128, 512])
            ps_g = b.psum(pes, "ps_g", [128, 512])
            ps_y = b.psum(pes, "ps_y", [128, 512])
            ps_s = b.psum(pes, "ps_s", [128, 512])
            ygpar = [0, 0]
            for c in range(NCH):
                P = Prog(b)
                P.emit("g", lambda e, c=c: e.dma_start(
                    out=arow[0:1, :].rearrange("o (hh j) -> o hh j", j=128),
                    in_=b.a_d.ap().rearrange("o (hh t) -> o hh t", t=TOK)[
                        :, 0:24, c * 128:(c + 1) * 128]), inc=[("gp", 16)])
                P.wait("g", "gp")
                P.flush()

                pe0, dv0 = b.cnt["mm"], b.cnt["ev"]
                P = Prog(b)
                P.emit("t", lambda e, c=c: e.matmul(
                    ps_g[:, 0:128], b.b_bf[:, c * 128:(c + 1) * 128],
                    b.c_bf[:, c * 128:(c + 1) * 128], start=True, stop=True),
                    inc=[("mm", 1)])
                for hh in range(NH):
                    if hh > 0:
                        P.wait("t", "ev", dv0 + 7 * (hh - 1) + 2)
                    if hh == 24:
                        P.wait("g", "mm", pe0 + 1 + 3 * 23 + 1)
                        P.emit("g", lambda e, c=c: e.dma_start(
                            out=arow[0:1, :].rearrange("o (hh j) -> o hh j", j=128),
                            in_=b.a_d.ap().rearrange("o (hh t) -> o hh t", t=TOK)[
                                :, 24:48, c * 128:(c + 1) * 128]), inc=[("gp", 16)])
                        P.wait("t", "gp")
                    P.emit("t", lambda e, hh=hh: e.matmul(
                        ps_a[:, 0:128], b.ones_rb[:, :],
                        arow[0:1, (hh % 24) * 128:(hh % 24 + 1) * 128],
                        start=True, stop=True), inc=[("mm", 1)])
                    P.wait("t", "ev", dv0 + 7 * hh + 4)
                    P.emit("t", lambda e, hh=hh, c=c: e.matmul(
                        ps_y[:, 0:PH], gm_sb[:, :], xslice(b, hh, c),
                        start=True, stop=False))
                    P.emit("t", lambda e, hh=hh: e.matmul(
                        ps_y[:, 0:PH], mc_sb[:, :], b.hst[:, hh * PH:(hh + 1) * PH],
                        start=False, stop=True), inc=[("mm", 1)])
                    P.wait("t", "ev", dv0 + 7 * hh + 5)
                    P.emit("t", lambda e, hh=hh, c=c: e.matmul(
                        ps_s[:, 0:PH], bhat_sb[:, :], xslice(b, hh, c),
                        start=True, stop=True), inc=[("mm", 1)])

                for hh in range(NH):
                    P.wait("v", "mm", pe0 + 1 + 3 * hh + 1)
                    P.emit("v", lambda e: e.tensor_tensor_scan(
                        mask_sb[:, :], ps_a[:, 0:128], b.idf[:, :], 0.0,
                        ALU.mult, ALU.add), inc=[("ev", 1)])
                    P.emit("v", lambda e: e.tensor_tensor_scan(
                        coef_sb[:, :], ps_a[:, 0:128], b.zero128[:, :], 1.0,
                        ALU.mult, ALU.add), inc=[("ev", 1)])
                    P.emit("v", lambda e: e.tensor_mul(gm_sb[:, :], ps_g[:, 0:128],
                                                       mask_sb[:, :]), inc=[("ev", 1)])
                    P.emit("v", lambda e, c=c: e.tensor_mul(
                        mc_sb[:, :], b.c_bf[:, c * 128:(c + 1) * 128], coef_sb[:, :]),
                        inc=[("ev", 1)])
                    P.emit("v", lambda e, c=c, hh=hh: e.tensor_scalar(
                        bhat_sb[:, :], b.bT[:, c * 128:(c + 1) * 128],
                        b.deT[:, c * NH + hh:c * NH + hh + 1], None, ALU.mult),
                        inc=[("ev", 1)])
                    P.wait("v", "mm", pe0 + 1 + 3 * hh + 2)
                    P.emit("v", lambda e, hh=hh, c=c: e.scalar_tensor_tensor(
                        ytm[:, hh * PH:(hh + 1) * PH], xslice(b, hh, c),
                        b.dcols[:, hh:hh + 1], ps_y[:, 0:PH], ALU.mult, ALU.add),
                        inc=[("ev", 1)])
                    P.wait("v", "mm", pe0 + 1 + 3 * hh + 3)
                    P.emit("v", lambda e, hh=hh: e.scalar_tensor_tensor(
                        b.hst[:, hh * PH:(hh + 1) * PH], b.hst[:, hh * PH:(hh + 1) * PH],
                        coef_sb[:, 127:128], ps_s[:, 0:PH], ALU.mult, ALU.add),
                        inc=[("ev", 1)])
                P.flush()

                P = Prog(b)
                P.wait("a", "ev")
                P.emit("a", lambda e: e.activation(yz[:, :], ytm[:, :], AF.Square),
                       inc=[("ac", 1)])
                P.wait("v", "ac")
                P.emit("v", lambda e: e.reduce_sum(ycol[:, 0:1], yz[:, :], axis=AX.X),
                       inc=[("ev", 1)])
                P.wait("v", "ev")
                P.emit("v", lambda e: e.tensor_scalar(ycol[:, 1:2], ycol[:, 0:1],
                                                      1.0 / DI, None, ALU.mult),
                       inc=[("ev", 1)])
                P.wait("a", "ev")
                P.emit("a", lambda e: e.activation(ycol[:, 2:3], ycol[:, 1:2], AF.Ln,
                                                   bias=b.epsc[:, :]), inc=[("ac", 1)])
                P.wait("a", "ac")
                P.emit("a", lambda e: e.activation(ycol[:, 3:4], ycol[:, 2:3], AF.Exp,
                                                   scale=-0.5), inc=[("ac", 1)])
                P.wait("a", "ev")
                P.emit("a", lambda e, c=c: e.dma_start_transpose(
                    yz[:, :], b.z_d.ap()[:, c * 128:(c + 1) * 128]), inc=[("sc", 16)])
                P.wait("a", "sc")
                P.emit("a", lambda e: e.activation(yz[:, :], yz[:, :], AF.Silu),
                       inc=[("ac", 1)])
                P.wait("v", "ac")
                P.emit("v", lambda e: e.tensor_mul(yz[:, :], yz[:, :], b.innw_rep[:, :]),
                       inc=[("ev", 1)])
                if ygpar[0] > 0:
                    P.wait("v", "z0", ygpar[0])
                P.emit("v", lambda e: e.scalar_tensor_tensor(
                    ygst[:, :], ytm[:, :], ycol[:, 3:4],
                    yz[:, :], ALU.mult, ALU.mult), inc=[("ev", 1)])
                P.wait("s", "ev")
                P.emit("s", lambda e, c=c: e.dma_start(
                    out=b.yg_d.ap()[c * 128:(c + 1) * 128, :],
                    in_=ygst[:, :]), inc=[("z0", 16)])
                ygpar[0] = b.cnt["z0"]
                P.flush()
        P = Prog(b)
        P.wait("s", "z0", b.cnt["z0"])
        P.wait("s", "z1", b.cnt["z1"])
        P.flush()


def _m6_outproj(b, ms, src, dst):
    nc = b.nc
    with contextlib.ExitStack() as fs:
        s = lambda n, sh, d: fs.enter_context(nc.sbuf_tensor(b._nm(n), sh, d))
        ygr = s("ygr", [128, NKI * 512], BF16)
        wbuf = s("wbufo", [128, 2 * NKI * 128], BF16)
        x1t = s("x1t", [128, 2 * 512], F32)
        x2t = s("x2t", [128, 2 * 512], F32)
        reps = s("reps", [128, 2 * 512], F32)
        rw1 = s("rw1", [1, 512], F32)
        rw2 = s("rw2", [1, 512], F32)
        with contextlib.ExitStack() as pes:
            ps_m0 = b.psum(pes, "ps_m0", [128, 512])
            ps_m1 = b.psum(pes, "ps_m1", [128, 512])
            ps_r = b.psum(pes, "ps_r", [128, 1024])
            x1par = [0, 0]
            x2par = [0, 0]
            for tau in range(4):
                P = Prog(b)
                for k in range(NKI):
                    P.emit("a", lambda e, k=k, tau=tau: e.dma_start_transpose(
                        ygr[:, k * 512:(k + 1) * 512],
                        b.yg_d.ap()[tau * 512:(tau + 1) * 512, k * 128:(k + 1) * 128]),
                        inc=[("sc", 16)])
                P.wait("a", "sc")
                P.flush()
                P = Prog(b)
                P.emit("v", lambda e, tau=tau: e.tensor_scalar(
                    rw1[0:1, :], b.srow[0:1, tau * 512:(tau + 1) * 512], 1.0, None,
                    ALU.add), inc=[("ev", 1)])
                P.emit("v", lambda e, tau=tau: e.tensor_mul(
                    rw2[0:1, :], b.srow[0:1, tau * 512:(tau + 1) * 512],
                    b.mrow[0:1, tau * 512:(tau + 1) * 512]), inc=[("ev", 1)])
                P.wait("t", "ev")
                P.emit("t", lambda e: e.matmul(ps_r[:, 0:512], b.ones_rf[:, :],
                                               rw1[0:1, :], start=True, stop=True))
                P.emit("t", lambda e: e.matmul(ps_r[:, 512:1024], b.ones_rf[:, :],
                                               rw2[0:1, :], start=True, stop=True),
                       inc=[("mm", 1)])
                P.wait("v", "mm")
                P.emit("v", lambda e: e.tensor_copy(reps[:, 0:512], ps_r[:, 0:512]),
                       inc=[("ev", 1)])
                P.emit("v", lambda e: e.tensor_copy(reps[:, 512:1024],
                                                    ps_r[:, 512:1024]), inc=[("ev", 1)])
                P.flush()

                P = Prog(b)
                ps_mm = [ps_m0, ps_m1]
                for dc in range(ND):
                    gci = b.wf_hist
                    if gci >= 2:
                        P.wait("s", "mm", b.mm_after_col[gci - 2])
                    b.wf_hist += 1
                    buf = b.wf_hist % 2
                    wsem = "w0" if buf == 0 else "w1"
                    P.emit("s", lambda e, dc=dc, buf=buf: e.dma_start(
                        out=wbuf[:, buf * NKI * 128:(buf + 1) * NKI * 128].rearrange(
                            "p (c n) -> p c n", n=128),
                        in_=b.outprojT_bf.ap()[:, dc * 128:(dc + 1) * 128].rearrange(
                            "(c p) n -> p c n", p=128)), inc=[(wsem, 16)])
                    xp = dc % 2
                    xsem = "z0" if xp == 0 else "z1"
                    if x1par[xp] > 0:
                        P.wait("s", "ev", x1par[xp])
                    P.emit("s", lambda e, dc=dc, tau=tau, xp=xp: e.dma_start(
                        out=x1t[:, xp * 512:(xp + 1) * 512],
                        in_=src[dc * 128:(dc + 1) * 128, tau * 512:(tau + 1) * 512]),
                        inc=[(xsem, 16)])
                    P.wait("t", wsem)
                    ti = tau * ND + dc
                    pp = ps_mm[ti % 2]
                    if ti >= 2:
                        P.wait("t", "ev", b.ev_free_o[ti - 2])
                    for k in range(NKI):
                        P.emit("t", lambda e, pp=pp, k=k, buf=buf: e.matmul(
                            pp[:, 0:512],
                            wbuf[:, buf * NKI * 128 + k * 128:buf * NKI * 128 + (k + 1) * 128],
                            ygr[:, k * 512:(k + 1) * 512],
                            start=(k == 0), stop=(k == NKI - 1)),
                            inc=([("mm", 1)] if k == NKI - 1 else None))
                    b.mm_after_col.append(b.cnt["mm"])
                    yp = ti % 2
                    ysem = "y0" if yp == 0 else "y1"
                    x2s = slice(yp * 512, (yp + 1) * 512)
                    P.wait("v", "mm")
                    P.wait("v", xsem)
                    if x2par[yp] > 0:
                        P.wait("v", ysem, x2par[yp])
                    P.emit("v", lambda e, xp=xp, x2s=x2s: e.tensor_mul(
                        x2t[:, x2s], x1t[:, xp * 512:(xp + 1) * 512], reps[:, 0:512]))
                    P.emit("v", lambda e, x2s=x2s: e.tensor_sub(x2t[:, x2s], x2t[:, x2s],
                                                                reps[:, 512:1024]))
                    P.emit("v", lambda e, pp=pp, x2s=x2s: e.tensor_add(
                        x2t[:, x2s], x2t[:, x2s], pp[:, 0:512]), inc=[("ev", 1)])
                    b.ev_free_o[ti] = b.cnt["ev"]
                    x1par[xp] = b.cnt["ev"]
                    P.wait("s", "ev")
                    P.emit("s", lambda e, dc=dc, tau=tau, x2s=x2s: e.dma_start(
                        out=dst[dc * 128:(dc + 1) * 128, tau * 512:(tau + 1) * 512],
                        in_=x2t[:, x2s]), inc=[(ysem, 16)])
                    x2par[yp] = b.cnt[ysem]
                P.wait("s", "st")
                P.flush()


def make_inputs(inp, core):
    t0 = core * TOK
    x = np.asarray(inp["L0_attn_x"]).reshape(NCORES * TOK, D)[t0:t0 + TOK]
    bf = lambda a: np.ascontiguousarray(a)
    eye = np.eye(128, dtype=np.float32)
    oh = np.zeros((E, E * 128), dtype=np.float32)
    for e in range(E):
        oh[e, e * 128:(e + 1) * 128] = 1.0
    chain_start = core % 4 == 0
    prev = core - 1
    hmask = np.zeros((128, E), dtype=np.float32)
    if not chain_start:
        hmask[:, prev] = 1.0
    cm128 = np.zeros((128, E), dtype=np.float32)
    for d in range(E):
        if d // 4 == core // 4 and d < core:
            cm128[:, d] = 1.0
    conv_state = np.asarray(inp["L0_attn_conv_state"])[0]
    conv_init = (conv_state if chain_start else np.zeros_like(conv_state))
    ssm = np.asarray(inp["L0_attn_ssm_state"])[0]
    m = {
        "x_fm": np.ascontiguousarray(x.T),
        "wg_s": bf(inp["L0_ffn_Wg_s"]), "wu_s": bf(inp["L0_ffn_Wu_s"]),
        "wd_s": bf(inp["L0_ffn_Wd_s"]),
        "wg_e": bf(inp["L0_ffn_Wg"]), "wu_e": bf(inp["L0_ffn_Wu"]),
        "wd_e": bf(inp["L0_ffn_Wd"]),
        "wr": bf(inp["L0_ffn_Wr"]),
        "br_rb": np.ascontiguousarray(
            np.stack([inp["L0_ffn_br"], inp["L0_ffn_rb"]], axis=1)),
        "inproj": bf(inp["L0_attn_in_proj_w"]),
        "outproj": bf(inp["L0_attn_out_proj_w"]),
        "convw": np.ascontiguousarray(
            np.asarray(inp["L0_attn_conv_w"]).reshape(NKI, 128, 4)
            .transpose(1, 0, 2).reshape(128, NKI * 4)),
        "convb": np.ascontiguousarray(
            np.asarray(inp["L0_attn_conv_b"]).reshape(NKI, 128).T),
        "dtbias": np.asarray(inp["L0_attn_dt_bias"]).reshape(NH, 1),
        "alog": np.asarray(inp["L0_attn_A_log"]).reshape(NH, 1),
        "drow": np.asarray(inp["L0_attn_D_param"]).reshape(1, NH),
        "normw": np.ascontiguousarray(
            np.asarray(inp["L0_attn_norm_w"]).reshape(ND, 128).T),
        "innw": np.asarray(inp["L0_attn_inner_norm_w"]).reshape(1, DI),
        "h0_in": np.ascontiguousarray(ssm.transpose(1, 0, 2).reshape(128, NH * PH)),
        "conv_init": np.ascontiguousarray(
            conv_init.reshape(NKI, 128, 3).transpose(1, 0, 2).reshape(128, NKI * 3)),
        "flaginv": np.full((128, 1), 0.0 if chain_start else 1.0, dtype=np.float32),
        "hmask": hmask,
        "cmask128": cm128,
        "idf_in": eye,
        "idb_in": eye.astype(ml_dtypes.bfloat16),
        "oh_in": oh,
    }
    return {k: np.ascontiguousarray(np.asarray(v, dtype=(
        ml_dtypes.bfloat16 if k == "idb_in" else np.float32))) for k, v in m.items()}




def kernel(**inputs):
    mk_ = MK(("ffn1", "mamba", "ffn2"))
    nc = mk_.build()
    in_maps = [make_inputs(inputs, c) for c in range(NCORES)]
    res = run_bass_kernel_spmd(nc, in_maps, core_ids=list(range(NCORES)))
    outs = [np.asarray(r["y_fm"], dtype=np.float32).T for r in res.results]
    return np.concatenate(outs, axis=0).reshape(2, 8192, D)


# version 02
# speedup vs baseline: 1.1041x; 1.1041x over previous
import contextlib
import numpy as np
import ml_dtypes
import concourse.bass as bass
import concourse.mybir as mybir
from concourse.bass_utils import run_bass_kernel_spmd

F32 = mybir.dt.float32
BF16 = mybir.dt.bfloat16
AF = mybir.ActivationFunctionType
ALU = mybir.AluOpType
AX = mybir.AxisListType

NCORES = 8
TOK = 2048
HALF = 1024
D = 2048
ND = 16
H = 1408
NHC = 11
E = 8
DI = 3072
NKI = 24
NH = 48
PH = 64
NCH = 16
EPS = 1e-5


class Prog:

    ENGS = ("t", "v", "a", "s", "g")

    def __init__(self, b):
        self.b = b
        self.q = {e: [] for e in self.ENGS}

    def on(self, eng, fn):
        self.q[eng].append(fn)

    def wait(self, eng, sem, val=None):
        v = self.b.cnt[sem] if val is None else val
        self.q[eng].append(lambda e, s=self.b.sems[sem], v=v: e.wait_ge(s, v))

    def emit(self, eng, fn, inc=None):
        incs = []
        if inc:
            for sem, k in inc:
                self.b.cnt[sem] += k
                incs.append((self.b.sems[sem], k))

        def run(e, fn=fn, incs=incs):
            i = fn(e)
            for s, k in incs:
                i = i.then_inc(s, k)
        self.q[eng].append(run)
        return {s: self.b.cnt[s] for s in self.b.cnt} if inc else None

    def flush(self):
        nc = self.b.nc
        with nc.Block() as blk:
            if self.q["t"]:
                blk.tensor(lambda e, fns=self.q["t"]: [f(e) for f in fns])
            if self.q["v"]:
                blk.vector(lambda e, fns=self.q["v"]: [f(e) for f in fns])
            if self.q["a"]:
                blk.scalar(lambda e, fns=self.q["a"]: [f(e) for f in fns])
            if self.q["s"]:
                blk.sync(lambda e, fns=self.q["s"]: [f(e) for f in fns])
            if self.q["g"]:
                blk.gpsimd(lambda e, fns=self.q["g"]: [f(e) for f in fns])
        self.q = {e: [] for e in self.ENGS}


class MK:
    def __init__(self, phases=("ffn1", "mamba", "ffn2"), detect_races=False):
        self.phases = phases
        nc = bass.Bass("TRN2", target_bir_lowering=False, detect_race_conditions=detect_races)
        self.nc = nc
        self.cnt = {}
        self.sems = {}
        for name in ["ld", "st", "sc", "gp", "c0", "c1", "c2", "c3", "w0", "w1", "z0", "z1", "y0", "y1", "cc", "mm", "ev", "ac"]:
            self.sems[name] = nc.alloc_semaphore(name)
            self.cnt[name] = 0

        ap = lambda n, s, d=F32: nc.dram_tensor(n, s, d, kind="ExternalInput").ap()
        self.x_in = ap("x_fm", [D, TOK])
        self.wg_s = ap("wg_s", [D, H]);  self.wu_s = ap("wu_s", [D, H])
        self.wd_s = ap("wd_s", [H, D])
        self.wg_e = ap("wg_e", [E, D, H]); self.wu_e = ap("wu_e", [E, D, H])
        self.wd_e = ap("wd_e", [E, H, D])
        self.wr_in = ap("wr", [D, E])
        self.br_rb = ap("br_rb", [E, 2])
        self.inproj = ap("inproj", [6448, D])
        self.outproj = ap("outproj", [D, DI])
        self.convw_in = ap("convw", [128, NKI * 4])
        self.convb_in = ap("convb", [128, NKI])
        self.dtbias_in = ap("dtbias", [NH, 1])
        self.alog_in = ap("alog", [NH, 1])
        self.drow_in = ap("drow", [1, NH])
        self.normw_in = ap("normw", [128, ND])
        self.innw_in = ap("innw", [1, DI])
        self.h0_in = ap("h0_in", [128, NH * PH])
        self.convinit_in = ap("conv_init", [128, NKI * 3])
        self.flaginv_in = ap("flaginv", [128, 1])
        self.hmask_in = ap("hmask", [128, E])
        self.cm128_in = ap("cmask128", [128, E])
        self.idf_in = ap("idf_in", [128, 128])
        self.idb_in = nc.dram_tensor("idb_in", [128, 128], BF16, kind="ExternalInput").ap()
        self.oh_in = ap("oh_in", [E, E * 128])

        self.y_out = nc.dram_tensor("y_fm", [D, TOK], F32, kind="ExternalOutput").ap()
        self.dbg = False

        dr = lambda n, s, d: nc.dram_tensor(n, s, d, kind="Internal")
        self.x1_d = dr("x1_d", [D, TOK], F32)
        self.wr_bf = dr("wr_bf", [D, E], BF16)
        self.inproj_bf = dr("inproj_bf", [6448, D], BF16)
        self.inprojT_bf = dr("inprojT_bf", [D, 6400], BF16)
        self.outproj_bf = dr("outproj_bf", [D, DI], BF16)
        self.outprojT_bf = dr("outprojT_bf", [DI, D], BF16)
        self.z_d = dr("z_d", [DI, TOK], BF16)
        self.xc_d = dr("xc_d", [DI, TOK], BF16)
        self.yg_d = dr("yg_d", [TOK, DI], BF16)
        self.a_d = dr("a_d", [1, NH * TOK], F32)
        self.bounce = dr("bounce", [1, 8192], F32)

        self.cc_halo_in = dr("cc_halo_in", [128, NKI * 3], BF16)
        self.cc_halo_out = dr("cc_halo_out", [NCORES * 128, NKI * 3], BF16)
        self.cc_sum_in = dr("cc_sum_in", [129, NH * PH], F32)
        self.cc_sum_out = dr("cc_sum_out", [NCORES * 129, NH * PH], F32)

        self.ctx = contextlib.ExitStack()
        self.uid = 0
        self.wf_hist = 0

    def _nm(self, n):
        self.uid += 1
        return f"{n}_{self.uid}"

    def sb(self, n, sh, d):
        return self.ctx.enter_context(self.nc.sbuf_tensor(self._nm(n), sh, d))

    def psum(self, es, n, sh, d=F32):
        return es.enter_context(self.nc.psum_tensor(self._nm(n), sh, d))

    def build(self):
        with self.ctx:
            self._alloc_globals()
            self._p0()
            src = self.x_in
            if "ffn1" in self.phases:
                self._ffn_call(src, self.x1_d.ap(), first=True)
                src = self.x1_d.ap()
            if "mamba" in self.phases:
                self._mamba(src, self.x1_d.ap())
                src = self.x1_d.ap()
            if "ffn2" in self.phases:
                self._ffn_call(src, self.y_out, first=False)
            else:
                self._copy_out(src)
            self._final_wait()
        return self.nc

    def _alloc_globals(self):
        s = self.sb
        self.idf = s("idf", [128, 128], F32)
        self.idb = s("idb", [128, 128], BF16)
        self.ones_rf = s("ones_rf", [1, 128], F32)
        self.ones_cb = s("ones_cb", [128, 1], BF16)
        self.ones_rb = s("ones_rb", [1, 128], BF16)
        self.zero128 = s("zero128", [128, 128], F32)
        self.epsc = s("epsc", [128, 1], F32)
        self.wr_sb = s("wr_sb", [128, ND * E], BF16)
        self.biasr = s("biasr", [E, 1], F32)
        self.normw_sb = s("normw_sb", [128, ND], F32)
        self.dcols = s("dcols", [128, NH], F32)
        self.rowtmp = s("rowtmp", [1, 64], F32)
        self.convw_sb = s("convw_sb", [128, NKI * 4], F32)
        self.convb_sb = s("convb_sb", [128, NKI], F32)
        self.flaginv_sb = s("flaginv_sb", [128, 1], F32)
        self.hmask_sb = s("hmask_sb", [128, E], F32)
        self.dtb = s("dtb", [NH, 1], F32)
        self.acol = s("acol", [NH, 1], F32)
        self.cm128 = s("cm128", [128, E], F32)
        self.cm128i = s("cm128i", [128, E], F32)
        self.brrb = s("brrb", [E, 2], F32)
        self.dtlhs = s("dtlhs", [128, ND * NH], BF16)

    def _p0(self):
        b = self
        P = Prog(b)
        loads = [
            (self.idf[:, :], self.idf_in), (self.idb[:, :], self.idb_in),
            (self.normw_sb[:, :], self.normw_in),
            (self.convw_sb[:, :], self.convw_in), (self.convb_sb[:, :], self.convb_in),
            (self.flaginv_sb[:, :], self.flaginv_in), (self.hmask_sb[:, :], self.hmask_in),
            (self.dtb[:, :], self.dtbias_in), (self.acol[:, :], self.alog_in),
            (self.cm128[:, :], self.cm128_in), (self.brrb[:, :], self.br_rb),
        ]
        for dst, srcc in loads:
            P.emit("s", lambda e, d=dst, x=srcc: e.dma_start(out=d, in_=x),
                   inc=[("ld", 16)])
        P.emit("g", lambda e: e.dma_start(out=self.wr_bf.ap(), in_=self.wr_in),
               inc=[("gp", 16)])
        P.wait("g", "gp")
        P.emit("g", lambda e: e.dma_start(
            out=self.wr_sb[:, :].rearrange("p (c ee) -> p c ee", ee=E),
            in_=self.wr_bf.ap().rearrange("(c p) ee -> p c ee", p=128)),
            inc=[("gp", 16)])
        P.emit("g", lambda e: e.memset(self.ones_rf[:, :], 1.0))
        P.emit("g", lambda e: e.memset(self.ones_cb[:, :], 1.0))
        P.emit("g", lambda e: e.memset(self.ones_rb[:, :], 1.0))
        P.emit("g", lambda e: e.memset(self.zero128[:, :], 0.0))
        P.emit("g", lambda e: e.memset(self.epsc[:, :], EPS))
        casts = [("c3", self.inproj_bf.ap(), self.inproj),
                 ("c3", self.outproj_bf.ap(), self.outproj)]
        for grp, dst, srcc in casts:
            P.emit("g", lambda e, d=dst, x=srcc: e.dma_start(out=d, in_=x),
                   inc=[(grp, 16)])
        self.castgrp = {"c3": self.cnt["c3"]}
        P.wait("s", "ld")
        P.flush()

        P = Prog(b)
        P.emit("v", lambda e: e.tensor_add(self.biasr[:, :], self.brrb[:, 0:1],
                                           self.brrb[:, 1:2]))
        P.emit("s", lambda e: e.dma_start(out=self.rowtmp[0:1, 0:NH], in_=self.drow_in),
               inc=[("ld", 16)])
        P.wait("t", "ld")
        with self.nc.psum_tensor(self._nm("ps0_p0"), [128, 512], F32) as ps0:
            P.emit("t", lambda e, p=ps0: e.matmul(p[:, 0:NH], self.ones_rf[:, :],
                                                  self.rowtmp[0:1, 0:NH],
                                                  start=True, stop=True),
                   inc=[("mm", 1)])
            P.wait("v", "mm")
            P.emit("v", lambda e, p=ps0: e.tensor_copy(self.dcols[:, :], p[:, 0:NH]))
            P.flush()

        self.stg = self.sb("p0stage", [128, 6400], BF16)
        self.prep_sc_done = None
        if "ffn1" not in self.phases:
            self._emit_prep_chain()

    def _emit_prep_chain(self):
        P = Prog(self)
        stg = self.stg
        P.wait("s", "c3", self.castgrp["c3"])
        for dc in range(ND):
            P.emit("s", lambda e, dc=dc, stg=stg: e.dma_start_transpose(
                stg[:, 0:6400], self.inproj_bf.ap()[0:6400, dc * 128:(dc + 1) * 128]),
                inc=[("sc", 16)])
            P.wait("s", "sc")
            P.emit("s", lambda e, dc=dc, stg=stg: e.dma_start(
                out=self.inprojT_bf.ap()[dc * 128:(dc + 1) * 128, :], in_=stg[:, 0:6400]),
                inc=[("sc", 16)])
            P.wait("s", "sc")
        for kc in range(NKI):
            P.emit("s", lambda e, kc=kc, stg=stg: e.dma_start_transpose(
                stg[:, 0:2048], self.outproj_bf.ap()[:, kc * 128:(kc + 1) * 128]),
                inc=[("sc", 16)])
            P.wait("s", "sc")
            P.emit("s", lambda e, kc=kc, stg=stg: e.dma_start(
                out=self.outprojT_bf.ap()[kc * 128:(kc + 1) * 128, :], in_=stg[:, 0:2048]),
                inc=[("sc", 16)])
            P.wait("s", "sc")
        P.emit("s", lambda e, stg=stg: e.dma_start(
            out=stg[0:48, 0:2048], in_=self.inproj_bf.ap()[6400:6448, :]),
            inc=[("sc", 16)])
        self.prep_sc_done = self.cnt["sc"]
        P.flush()

    def _emit_dtlhs(self):
        P = Prog(self)
        stg = self.stg
        P.wait("t", "sc", self.prep_sc_done)
        with self.nc.psum_tensor(self._nm("ps_tp0"), [128, 1024], BF16) as pstb:
            for dc in range(ND):
                if dc > 0:
                    P.wait("t", "ev")
                P.emit("t", lambda e, dc=dc, p=pstb, stg=stg: e.transpose(
                    p[:, 0:48], stg[0:48, dc * 128:(dc + 1) * 128], self.idb[0:48, 0:48]),
                    inc=[("mm", 1)])
                P.wait("v", "mm")
                P.emit("v", lambda e, dc=dc, p=pstb: e.tensor_copy(
                    self.dtlhs[:, dc * NH:(dc + 1) * NH], p[:, 0:48]), inc=[("ev", 1)])
            P.flush()

    def _ln_tile(self, P, xsrc, sink, tglob, psb, mode="ln"):
        b = self
        ps_sx, ps_sq, ps_r1, ps_r2 = psb
        P.wait("t", "ev")
        P.wait("t", "ac")
        P.wait("a", "mm")
        P.wait("v", "mm")
        for dc in range(ND):
            P.emit("v", lambda e, dc=dc: e.tensor_copy(sink(dc), xsrc(dc)),
                   inc=[("ev", 1)])
            P.wait("a", "ev")
            P.emit("a", lambda e, dc=dc: e.activation(
                self.xsq[:, dc * 512:(dc + 1) * 512], sink(dc),
                AF.Square), inc=[("ac", 1)])
            P.wait("t", "ev")
            P.emit("t", lambda e, dc=dc: e.matmul(
                ps_sx[0:1, 0:512], self.ones_cb[:, :], sink(dc),
                start=(dc == 0), stop=(dc == ND - 1)), inc=([("mm", 1)] if dc == ND - 1 else None))
            P.wait("t", "ac")
            P.emit("t", lambda e, dc=dc: e.matmul(
                ps_sq[0:1, 0:512], self.ones_cb[:, :],
                self.xsq[:, dc * 512:(dc + 1) * 512],
                start=(dc == 0), stop=(dc == ND - 1)), inc=([("mm", 1)] if dc == ND - 1 else None))
        P.wait("v", "mm")
        P.emit("v", lambda e: e.tensor_scalar(self.rowa[0:1, :], ps_sx[0:1, 0:512],
                                              1.0 / D, None, ALU.mult), inc=[("ev", 1)])
        P.emit("v", lambda e: e.tensor_scalar(self.rowb[0:1, :], ps_sq[0:1, 0:512],
                                              1.0 / D, None, ALU.mult), inc=[("ev", 1)])
        P.wait("a", "ev")
        P.emit("a", lambda e: e.activation(self.rowc[0:1, :], self.rowa[0:1, :],
                                           AF.Square), inc=[("ac", 1)])
        P.wait("v", "ac")
        P.emit("v", lambda e: e.tensor_sub(self.rowb[0:1, :], self.rowb[0:1, :],
                                           self.rowc[0:1, :]), inc=[("ev", 1)])
        P.wait("a", "ev")
        P.emit("a", lambda e: e.activation(self.rowc[0:1, :], self.rowb[0:1, :],
                                           AF.Ln, bias=self.epsc[0:1, :]))
        P.emit("a", lambda e: e.activation(self.rowc[0:1, :], self.rowc[0:1, :],
                                           AF.Exp, scale=-0.5), inc=[("ac", 1)])
        scale_row = self.rowc
        if mode == "mamba":
            P.wait("v", "ac")
            P.emit("v", lambda e: e.tensor_copy(self.mrow[0:1, tglob:tglob + 512],
                                                self.rowa[0:1, :]))
            P.emit("v", lambda e: e.tensor_copy(self.srow[0:1, tglob:tglob + 512],
                                                self.rowc[0:1, :]))
            P.emit("v", lambda e: e.tensor_mul(self.rowd[0:1, :], self.rowc[0:1, :],
                                               self.rowc[0:1, :]))
            P.emit("v", lambda e: e.tensor_mul(self.rowd[0:1, :], self.rowd[0:1, :],
                                               self.rowb[0:1, :]), inc=[("ev", 1)])
            P.wait("a", "ev")
            P.emit("a", lambda e: e.activation(self.rowd[0:1, :], self.rowd[0:1, :],
                                               AF.Ln, bias=self.epsc[0:1, :]))
            P.emit("a", lambda e: e.activation(self.rowd[0:1, :], self.rowd[0:1, :],
                                               AF.Exp, scale=-0.5), inc=[("ac", 1)])
            P.wait("v", "ac")
            P.emit("v", lambda e: e.tensor_mul(self.rowd[0:1, :], self.rowd[0:1, :],
                                               self.rowc[0:1, :]), inc=[("ev", 1)])
            scale_row = self.rowd
        P.wait("t", "ev" if mode == "mamba" else "ac")
        P.emit("t", lambda e: e.matmul(ps_r1[:, 0:512], self.ones_rf[:, :],
                                       self.rowa[0:1, :], start=True, stop=True))
        P.emit("t", lambda e, sr=scale_row: e.matmul(ps_r2[:, 0:512], self.ones_rf[:, :],
                                                     sr[0:1, :], start=True, stop=True),
               inc=[("mm", 1)])
        P.wait("v", "mm")
        for dc in range(ND):
            P.emit("v", lambda e, dc=dc: e.tensor_sub(self.tf32[:, 0:512], xsrc(dc),
                                                      ps_r1[:, 0:512]))
            if mode == "mamba":
                P.emit("v", lambda e, dc=dc: e.tensor_mul(self.tf32[:, 0:512],
                                                          self.tf32[:, 0:512],
                                                          ps_r2[:, 0:512]))
                P.emit("v", lambda e, dc=dc: e.tensor_scalar(
                    sink(dc), self.tf32[:, 0:512], self.normw_sb[:, dc:dc + 1], None,
                    ALU.mult), inc=[("ev", 1)])
            else:
                P.emit("v", lambda e, dc=dc: e.tensor_mul(sink(dc), self.tf32[:, 0:512],
                                                          ps_r2[:, 0:512]),
                       inc=[("ev", 1)])
        self.u_done = self.cnt["ev"]

    def _ffn_call(self, src, dst, first):
        b = self
        nc = self.nc
        with contextlib.ExitStack() as fs:
            s = lambda n, sh, d: fs.enter_context(nc.sbuf_tensor(self._nm(n), sh, d))
            self.xh = s("xh", [128, ND * HALF], F32)
            self.uh = s("uh", [128, ND * HALF], BF16)
            self.gp_b = s("gp_b", [128, NHC * HALF], BF16)
            self.wbuf = s("wbuf", [128, 2 * ND * 128], BF16)
            self.wrep = s("wrep", [128, HALF], F32)
            self.xsq = s("xsq", [128, ND * 512], BF16)
            self.tf32 = s("tf32", [128, 512], F32)
            self.rowa = s("rowa", [1, 512], F32)
            self.rowb = s("rowb", [1, 512], F32)
            self.rowc = s("rowc", [1, 512], F32)
            self.rowd = s("rowd", [1, 512], F32)
            self.oh_sb = s("oh_sb", [E, E * 128], F32)
            self.wTh = s("wTh", [E, HALF], F32)
            self.probs = s("probs", [E, 512], F32)
            self.ptm = s("ptm", [128, 8], F32)
            self.maxs = s("maxs", [128, 8], F32)
            self.wtm = s("wtm", [128, 8], F32)
            self.v12 = s("v12", [128, 2], F32)

            P = Prog(b)
            P.emit("s", lambda e: e.dma_start(out=self.oh_sb[:, :], in_=self.oh_in),
                   inc=[("ld", 16)])
            P.wait("s", "ld")
            P.flush()

            for ha in range(2):
                self._ffn_half(src, dst, ha, first)

    def _ffn_half(self, src, dst, ha, first):
        b = self
        nc = self.nc
        t0 = ha * HALF
        P = Prog(b)
        for dc in range(ND):
            P.emit("s", lambda e, dc=dc: e.dma_start(
                out=self.xh[:, dc * HALF:(dc + 1) * HALF],
                in_=src[dc * 128:(dc + 1) * 128, t0:t0 + HALF]), inc=[("ld", 16)])
        P.wait("s", "ld")
        P.flush()
        if first and ha == 1:
            self._emit_prep_chain()

        with contextlib.ExitStack() as pes:
            ps_a0 = self.psum(pes, "ps_a0", [128, 512])
            ps_a1 = self.psum(pes, "ps_a1", [128, 512])
            ps_sx = self.psum(pes, "ps_sx", [128, 512])
            ps_sq = self.psum(pes, "ps_sq", [128, 512])
            ps_r1 = self.psum(pes, "ps_r1", [128, 512])
            ps_r2 = self.psum(pes, "ps_r2", [128, 512])
            ps_x = self.psum(pes, "ps_x", [128, 512])
            psb = (ps_sx, ps_sq, ps_r1, ps_r2)
            ps_mm = [ps_a0, ps_a1]

            P = Prog(b)
            u_tile_done = []
            for ti in range(2):
                xsrc = lambda dc, ti=ti: self.xh[:, dc * HALF + ti * 512:
                                                 dc * HALF + (ti + 1) * 512]
                sink = lambda dc, ti=ti: self.uh[:, dc * HALF + ti * 512:
                                                 dc * HALF + (ti + 1) * 512]
                self._ln_tile(P, xsrc, sink, t0 + ti * 512, psb, mode="ln")
                u_tile_done.append(self.u_done)
                P.wait("t", "ev")
                for dc in range(ND):
                    P.emit("t", lambda e, dc=dc, ti=ti: e.matmul(
                        ps_x[0:8, 0:512], self.wr_sb[:, dc * E:(dc + 1) * E],
                        self.uh[:, dc * HALF + ti * 512:dc * HALF + (ti + 1) * 512],
                        start=(dc == 0), stop=(dc == ND - 1)), inc=(
                        [("mm", 1)] if dc == ND - 1 else None))
                P.wait("a", "mm")
                P.emit("a", lambda e: e.activation(self.probs[:, 0:512], ps_x[0:8, 0:512],
                                                   AF.Sigmoid, bias=self.biasr[:, :]),
                       inc=[("ac", 1)])
                for j in range(4):
                    P.wait("t", "ac")
                    if j > 0 or ti > 0:
                        P.wait("t", "ev")
                    P.emit("t", lambda e, j=j: e.transpose(
                        ps_x[0:128, 0:8] if False else ps_x[:, 0:8],
                        self.probs[0:8, j * 128:(j + 1) * 128], self.idf[0:8, 0:8]),
                        inc=[("mm", 1)])
                    P.wait("v", "mm")
                    for fn in [
                        lambda e: e.tensor_copy(self.ptm[:, :], ps_x[:, 0:8]),
                        lambda e: e.max(out=self.maxs[:, :], in_=self.ptm[:, :]),
                        lambda e: e.tensor_add(self.v12[:, 0:1], self.maxs[:, 0:1],
                                               self.maxs[:, 1:2]),
                        lambda e: e.reciprocal(self.v12[:, 1:2], self.v12[:, 0:1]),
                        lambda e: e.tensor_scalar(self.wtm[:, :], self.ptm[:, :],
                                                  self.maxs[:, 1:2], None, ALU.is_ge),
                        lambda e: e.tensor_mul(self.wtm[:, :], self.wtm[:, :],
                                               self.ptm[:, :]),
                        lambda e: e.tensor_scalar(self.wtm[:, :], self.wtm[:, :],
                                                  self.v12[:, 1:2], None, ALU.mult),
                    ]:
                        P.emit("v", fn, inc=[("ev", 1)])
                        P.wait("v", "ev")
                    P.wait("t", "ev")
                    P.emit("t", lambda e, j=j: e.transpose(
                        ps_x[0:8, 8:136], self.wtm[:, :], self.idf[:, :]),
                        inc=[("mm", 1)])
                    P.wait("v", "mm")
                    P.emit("v", lambda e, j=j, ti=ti: e.tensor_copy(
                        self.wTh[0:8, ti * 512 + j * 128:ti * 512 + (j + 1) * 128],
                        ps_x[0:8, 8:136]), inc=[("ev", 1)])
            self.wth_done = self.cnt["ev"]
            P.flush()

            for e_ in range(9):
                self._ffn_expert(e_, ha, first, ps_mm, ps_x, (ps_r1, ps_r2))

        P = Prog(b)
        P.wait("s", "ev")
        for dc in range(ND):
            P.emit("s", lambda e, dc=dc: e.dma_start(
                out=dst[dc * 128:(dc + 1) * 128, t0:t0 + HALF],
                in_=self.xh[:, dc * HALF:(dc + 1) * HALF]), inc=[("st", 16)])
        P.wait("s", "st")
        P.flush()

    def _ffn_expert(self, e_, ha, first, ps_mm, ps_x, ps_wreps):
        b = self
        P = Prog(b)
        shared = (e_ == 0)
        if shared:
            wg, wu, wd = self.wg_s, self.wu_s, self.wd_s
        else:
            wg, wu, wd = self.wg_e[e_ - 1], self.wu_e[e_ - 1], self.wd_e[e_ - 1]

        if not shared:
            P.wait("t", "ev")
            for ti in range(2):
                pw = ps_wreps[ti % 2]
                P.emit("t", lambda e, ti=ti, pw=pw: e.matmul(
                    pw[:, 0:512], self.oh_sb[:, (e_ - 1) * 128:e_ * 128],
                    self.wTh[0:8, ti * 512:(ti + 1) * 512], start=True, stop=True),
                    inc=[("mm", 1)])
                P.wait("v", "mm")
                P.emit("v", lambda e, ti=ti, pw=pw: e.tensor_copy(
                    self.wrep[:, ti * 512:(ti + 1) * 512], pw[:, 0:512]),
                    inc=[("ev", 1)])
        self.wrep_done = self.cnt["ev"]

        nbuf = ND * 128
        cols = [("g", i) for i in range(NHC)] + [("u", i) for i in range(NHC)] + \
               [("d", i) for i in range(ND)]

        mm_of_tile = {}
        ev_free = []
        if not hasattr(self, "mm_after_col"):
            self.mm_after_col = []
        tidx = 0
        P.wait("t", "ev")
        P.wait("t", "ac")
        for ci, (kind, col) in enumerate(cols):
            gci = self.wf_hist
            if gci >= 2:
                P.wait("g", "mm", self.mm_after_col[gci - 2])
            self.wf_hist += 1
            if kind == "g":
                srcw = wg[:, col * 128:(col + 1) * 128]
            elif kind == "u":
                srcw = wu[:, col * 128:(col + 1) * 128]
            else:
                srcw = wd[:, col * 128:(col + 1) * 128]
            buf = self.wf_hist % 2
            wsem = "w0" if buf == 0 else "w1"
            nkw = ND if kind != "d" else NHC
            P.emit("g", lambda e, srcw=srcw, buf=buf, nkw=nkw: e.dma_start(
                out=self.wbuf[:, buf * nbuf:buf * nbuf + nkw * 128].rearrange(
                    "p (c n) -> p c n", n=128),
                in_=srcw.rearrange("(c p) n -> p c n", p=128)), inc=[(wsem, 16)])

            P.wait("t", wsem)
            nk = ND if kind != "d" else NHC
            for ti in range(2):
                pp = ps_mm[tidx % 2]
                if tidx >= 2:
                    P.wait("t", "ev" if ev_free[tidx - 2][0] == "ev" else "ac",
                           ev_free[tidx - 2][1])
                for k in range(nk):
                    if kind == "d":
                        rhs = self.gp_b[:, k * HALF + ti * 512:k * HALF + (ti + 1) * 512]
                    else:
                        rhs = self.uh[:, k * HALF + ti * 512:k * HALF + (ti + 1) * 512]
                    last = (k == nk - 1)
                    incs = [("mm", 1)] if last else None
                    P.emit("t", lambda e, pp=pp, k=k, rhs=rhs, buf=buf, kk=(k == 0),
                           st=last: e.matmul(
                        pp[:, 0:512],
                        self.wbuf[:, buf * nbuf + k * 128:buf * nbuf + (k + 1) * 128],
                        rhs, start=kk, stop=st), inc=incs)
                mm_of_tile[(kind, col, ti)] = self.cnt["mm"]

                gslice = self.gp_b[:, col * HALF + ti * 512:col * HALF + (ti + 1) * 512] \
                    if kind != "d" else None
                if kind == "g":
                    P.wait("a", "mm", mm_of_tile[(kind, col, ti)])
                    P.emit("a", lambda e, pp=pp, gs=gslice: e.activation(
                        gs, pp[:, 0:512], AF.Silu), inc=[("ac", 1)])
                    ev_free.append(("ac", self.cnt["ac"]))
                elif kind == "u":
                    P.wait("v", "mm", mm_of_tile[(kind, col, ti)])
                    if not shared:
                        P.emit("v", lambda e, pp=pp, gs=gslice: e.tensor_mul(
                            self.tf32[:, 0:512], pp[:, 0:512], gs))
                        P.emit("v", lambda e, gs=gslice, ti=ti: e.tensor_mul(
                            gs, self.tf32[:, 0:512],
                            self.wrep[:, ti * 512:(ti + 1) * 512]), inc=[("ev", 1)])
                    else:
                        P.emit("v", lambda e, pp=pp, gs=gslice: e.tensor_mul(
                            gs, pp[:, 0:512], gs), inc=[("ev", 1)])
                    ev_free.append(("ev", self.cnt["ev"]))
                else:
                    xsl = self.xh[:, col * HALF + ti * 512:col * HALF + (ti + 1) * 512]
                    P.wait("v", "mm", mm_of_tile[(kind, col, ti)])
                    P.emit("v", lambda e, pp=pp, xsl=xsl: e.scalar_tensor_tensor(
                        xsl, pp[:, 0:512], 0.5, xsl, ALU.mult, ALU.add),
                        inc=[("ev", 1)])
                    ev_free.append(("ev", self.cnt["ev"]))
                tidx += 1
            self.mm_after_col.append(self.cnt["mm"])
        P.flush()

    def _copy_out(self, src):
        P = Prog(self)
        for dc in range(ND):
            P.emit("s", lambda e, dc=dc: e.dma_start(
                out=self.y_out[dc * 128:(dc + 1) * 128, :],
                in_=src[dc * 128:(dc + 1) * 128, :]), inc=[("st", 16)])
        P.wait("s", "st")
        P.flush()

    def _final_wait(self):
        P = Prog(self)
        P.wait("g", "st")
        P.flush()

    def _mamba(self, src, dst):
        emit_mamba(self, src, dst)




def emit_mamba(b, src, dst):
    nc = b.nc
    b.ev_free_m = {}
    b.ev_free_o = {}
    with contextlib.ExitStack() as ms:
        s = lambda n, sh, d: ms.enter_context(nc.sbuf_tensor(b._nm(n), sh, d))
        psum = lambda es, n, sh, d=F32: b.psum(es, n, sh, d)
        b.mrow = s("mrow", [1, TOK], F32)
        b.srow = s("srow", [1, TOK], F32)
        b.innw_rep = s("innw_rep", [128, DI], BF16)
        b.b_bf = s("b_bf", [128, TOK], BF16)
        b.c_bf = s("c_bf", [128, TOK], BF16)
        b.dtraw = s("dtraw", [NH, TOK], F32)
        b.halo = s("halo", [128, NKI * 3], BF16)
        b.convinit_sb = s("convinit_sb", [128, NKI * 3], F32)
        b.tot = s("tot", [NH, NCH + 1], F32)
        b.deT = s("deT", [128, NCH * NH], F32)
        b.prow = s("prow", [1, (NCH + 1) * NH], F32)
        b.pcols = s("pcols", [128, (NCH + 1) * NH], F32)
        b.runall = s("runall", [128, NCH * NH], F32)
        b.bT = s("bT", [128, NCH * 128], BF16)


        P = Prog(b)
        P.emit("g", lambda e: e.dma_start(out=b.innw_rep[:, :],
                                          in_=b.innw_in.broadcast_to([128, DI])),
               inc=[("gp", 16)])
        P.emit("s", lambda e: e.dma_start(out=b.convinit_sb[:, :], in_=b.convinit_in),
               inc=[("ld", 16)])
        P.emit("v", lambda e: e.tensor_scalar(b.cm128i[:, :], b.cm128[:, :], -1.0, 1.0,
                                              ALU.mult, ALU.add))
        P.wait("g", "gp")
        P.wait("s", "ld")
        P.flush()

        b._emit_dtlhs()
        _m1_u_inproj(b, ms, src)
        _m2_prep(b, ms)
        with contextlib.ExitStack() as bs:
            b.big = bs.enter_context(nc.sbuf_tensor(
                b._nm("big"), [128, NKI * NCH * 128], BF16))
            b.hst = bs.enter_context(nc.sbuf_tensor(
                b._nm("hst"), [128, NH * PH], F32))
            _m3_conv(b, ms)
            _m4_pass1_exchange(b, ms)
            _m5_pass2_gate(b, ms)
        _m6_outproj(b, ms, src, dst)


def _m1_u_inproj(b, ms, src):
    nc = b.nc
    with contextlib.ExitStack() as fs:
        s = lambda n, sh, d: fs.enter_context(nc.sbuf_tensor(b._nm(n), sh, d))
        b.uh = s("uh", [128, ND * TOK], BF16)
        b.xh = None
        xt = s("xt", [128, ND * 512], F32)
        b.xsq = s("xsq", [128, ND * 512], BF16)
        b.tf32 = s("tf32", [128, 512], F32)
        b.rowa = s("rowa", [1, 512], F32)
        b.rowb = s("rowb", [1, 512], F32)
        b.rowc = s("rowc", [1, 512], F32)
        b.rowd = s("rowd", [1, 512], F32)
        wbuf = s("wbufm", [128, 2 * ND * 128], BF16)
        zst = s("zst", [128, 2 * 512], BF16)

        with contextlib.ExitStack() as pes:
            ps_a0 = b.psum(pes, "ps_a0", [128, 512])
            ps_a1 = b.psum(pes, "ps_a1", [128, 512])
            ps_sx = b.psum(pes, "ps_sx", [128, 512])
            ps_sq = b.psum(pes, "ps_sq", [128, 512])
            ps_r1 = b.psum(pes, "ps_r1", [128, 512])
            ps_r2 = b.psum(pes, "ps_r2", [128, 512])
            psb = (ps_sx, ps_sq, ps_r1, ps_r2)
            ps_mm = [ps_a0, ps_a1]

            P = Prog(b)
            for tau in range(4):
                P.wait("s", "mm")
                P.wait("s", "ev")
                for dc in range(ND):
                    P.emit("s", lambda e, dc=dc, tau=tau: e.dma_start(
                        out=xt[:, dc * 512:(dc + 1) * 512],
                        in_=src[dc * 128:(dc + 1) * 128, tau * 512:(tau + 1) * 512]),
                        inc=[("ld", 16)])
                P.wait("s", "ld")
                P.flush()
                P = Prog(b)
                xsrc = lambda dc: xt[:, dc * 512:(dc + 1) * 512]
                sink = lambda dc, tau=tau: b.uh[:, dc * TOK + tau * 512:
                                                dc * TOK + (tau + 1) * 512]
                b._ln_tile(P, xsrc, sink, tau * 512, psb, mode="mamba")
                P.flush()
                P = Prog(b)

            P.wait("t", "ev")
            P.wait("t", "ac")
            P.wait("g", "sc", b.prep_sc_done)
            zpar_cnt = [0, 0]
            zpar_ev = [0, 0]
            for oc in range(50):
                gci = b.wf_hist
                if gci >= 2:
                    P.wait("g", "mm", b.mm_after_col[gci - 2])
                b.wf_hist += 1
                buf = b.wf_hist % 2
                wsem = "w0" if buf == 0 else "w1"
                srcw = b.inprojT_bf.ap()[:, oc * 128:(oc + 1) * 128]
                P.emit("g", lambda e, srcw=srcw, buf=buf: e.dma_start(
                    out=wbuf[:, buf * ND * 128:(buf + 1) * ND * 128].rearrange(
                        "p (c n) -> p c n", n=128),
                    in_=srcw.rearrange("(c p) n -> p c n", p=128)), inc=[(wsem, 16)])
                P.wait("t", wsem)
                for tau in range(4):
                    ti = oc * 4 + tau
                    pp = ps_mm[ti % 2]
                    if ti >= 2:
                        kind_prev, cnt_prev = b.ev_free_m[ti - 2]
                        P.wait("t", kind_prev, cnt_prev)
                    for k in range(ND):
                        P.emit("t", lambda e, pp=pp, k=k, buf=buf, tau=tau: e.matmul(
                            pp[:, 0:512],
                            wbuf[:, buf * ND * 128 + k * 128:buf * ND * 128 + (k + 1) * 128],
                            b.uh[:, k * TOK + tau * 512:k * TOK + (tau + 1) * 512],
                            start=(k == 0), stop=(k == ND - 1)),
                            inc=([("mm", 1)] if k == ND - 1 else None))
                    mmv = b.cnt["mm"]
                    if oc < 48:
                        zp = ti % 2
                        zsem = "z0" if zp == 0 else "z1"
                        if zpar_cnt[zp] > 0:
                            P.wait("a", zsem, zpar_cnt[zp])
                        if zpar_ev[zp] > 0:
                            P.wait("a", "ev", zpar_ev[zp])
                        P.wait("a", "mm", mmv)
                        P.emit("a", lambda e, pp=pp, zp=zp: e.activation(
                            zst[:, zp * 512:(zp + 1) * 512], pp[:, 0:512], AF.Copy),
                            inc=[("ac", 1)])
                        b.ev_free_m[ti] = ("ac", b.cnt["ac"])
                        P.wait("s", "ac", b.cnt["ac"])
                        dstt = b.z_d.ap() if oc < 24 else b.xc_d.ap()
                        r0 = (oc % 24) * 128
                        P.emit("s", lambda e, dstt=dstt, r0=r0, tau=tau, zp=zp:
                               e.dma_start(out=dstt[r0:r0 + 128,
                                                    tau * 512:(tau + 1) * 512],
                                           in_=zst[:, zp * 512:(zp + 1) * 512]),
                               inc=[(zsem, 16)])
                        zpar_cnt[zp] = b.cnt[zsem]
                        if oc >= 24 and tau == 3:
                            P.wait("v", "ac", b.ev_free_m[ti][1])
                            P.emit("v", lambda e, oc=oc, zp=zp: e.tensor_copy(
                                b.halo[:, (oc - 24) * 3:(oc - 23) * 3],
                                zst[:, zp * 512 + 509:zp * 512 + 512]), inc=[("ev", 1)])
                            zpar_ev[zp] = b.cnt["ev"]
                    elif oc == 48:
                        P.wait("v", "mm", mmv)
                        P.emit("v", lambda e, pp=pp, tau=tau: e.tensor_copy(
                            b.b_bf[:, tau * 512:(tau + 1) * 512], pp[:, 0:512]),
                            inc=[("ev", 1)])
                        b.ev_free_m[ti] = ("ev", b.cnt["ev"])
                    else:
                        P.wait("v", "mm", mmv)
                        P.emit("v", lambda e, pp=pp, tau=tau: e.tensor_copy(
                            b.c_bf[:, tau * 512:(tau + 1) * 512], pp[:, 0:512]),
                            inc=[("ev", 1)])
                        b.ev_free_m[ti] = ("ev", b.cnt["ev"])
                b.mm_after_col.append(b.cnt["mm"])
            for tau in range(4):
                ti = 200 + tau
                pp = ps_mm[tau % 2]
                kind_prev, cnt_prev = b.ev_free_m[198 + tau if tau < 2 else 200 + tau - 2]
                P.wait("t", kind_prev, cnt_prev)
                for k in range(ND):
                    P.emit("t", lambda e, pp=pp, k=k, tau=tau: e.matmul(
                        pp[0:NH, 0:512], b.dtlhs[:, k * NH:(k + 1) * NH],
                        b.uh[:, k * TOK + tau * 512:k * TOK + (tau + 1) * 512],
                        start=(k == 0), stop=(k == ND - 1)),
                        inc=([("mm", 1)] if k == ND - 1 else None))
                P.wait("v", "mm")
                P.emit("v", lambda e, pp=pp, tau=tau: e.tensor_copy(
                    b.dtraw[:, tau * 512:(tau + 1) * 512], pp[0:NH, 0:512]),
                    inc=[("ev", 1)])
                b.ev_free_m[200 + tau] = ("ev", b.cnt["ev"])
            P.wait("s", "ev")
            P.emit("s", lambda e: e.dma_start(out=b.cc_halo_in.ap(), in_=b.halo[:, :]),
                   inc=[("st", 16)])
            P.wait("s", "st")
            P.flush()

        P = Prog(b)
        P.emit("g", lambda e: e.collective_compute(
            "AllGather", ALU.bypass, replica_groups=[list(range(NCORES))],
            ins=[b.cc_halo_in.ap().opt()], outs=[b.cc_halo_out.ap().opt()]),
            inc=[("cc", 1)])
        P.wait("g", "cc")
        P.flush()


def _m2_prep(b, ms):
    nc = b.nc
    with contextlib.ExitStack() as fs:
        s = lambda n, sh, d: fs.enter_context(nc.sbuf_tensor(b._nm(n), sh, d))
        la = s("la", [NH, TOK], F32)
        afm = s("afm", [NH, TOK], F32)
        cum = s("cum", [NH, TOK], F32)
        sc48 = s("sc48", [NH, TOK], F32)
        sc48b = s("sc48b", [NH, TOK], F32)
        agh = s("agh", [128, NKI * 3], BF16)

        P = Prog(b)
        P.emit("a", lambda e: e.activation(sc48[:, :], b.dtraw[:, :], AF.Abs,
                                           bias=b.dtb[:, :], scale=1.0))
        P.emit("a", lambda e: e.activation(sc48[:, :], sc48[:, :], AF.Exp, scale=-1.0))
        P.emit("a", lambda e: e.activation(sc48[:, :], sc48[:, :], AF.Ln, bias=1.0),
               inc=[("ac", 1)])
        P.emit("a", lambda e: e.activation(b.acol[:, :], b.acol[:, :], AF.Exp),
               inc=[("ac", 1)])
        P.wait("v", "ac")
        P.emit("v", lambda e: e.tensor_scalar(sc48b[:, :], b.dtraw[:, :], b.dtb[:, :],
                                              0.0, ALU.add, ALU.max), inc=[("ev", 1)])
        P.emit("v", lambda e: e.tensor_add(sc48b[:, :], sc48b[:, :], sc48[:, :]),
               inc=[("ev", 1)])
        P.emit("v", lambda e: e.tensor_scalar(la[:, :], sc48b[:, :], b.acol[:, :],
                                              -1.0, ALU.mult, ALU.mult), inc=[("ev", 1)])
        P.wait("a", "ev")
        P.emit("a", lambda e: e.activation(afm[:, :], la[:, :], AF.Exp), inc=[("ac", 1)])
        P.wait("v", "ac")
        for c in range(NCH):
            sl = slice(c * 128, (c + 1) * 128)
            P.emit("v", lambda e, sl=sl: e.tensor_tensor_scan(
                cum[:, sl], la[:, sl], la[:, sl], 0.0, ALU.add, ALU.bypass))
            P.emit("v", lambda e, c=c, sl=sl: e.reduce_sum(
                b.tot[:, c:c + 1], la[:, sl], axis=AX.X), inc=[("ev", 1)])
            P.wait("v", "ev")
        P.emit("v", lambda e: e.reduce_sum(b.tot[:, NCH:NCH + 1], la[:, :], axis=AX.X),
               inc=[("ev", 1)])
        P.wait("v", "ev")
        P.wait("a", "ev")
        for c in range(NCH):
            sl = slice(c * 128, (c + 1) * 128)
            P.emit("a", lambda e, c=c, sl=sl: e.activation(
                sc48[:, sl], cum[:, sl], AF.Exp, bias=b.tot[:, c:c + 1], scale=-1.0))
        P.emit("a", lambda e: e.activation(sc48b[:, 0:NCH + 1], b.tot[:, 0:NCH + 1],
                                           AF.Exp), inc=[("ac", 1)])
        P.wait("s", "ac")
        P.emit("s", lambda e: e.dma_start(
            out=b.a_d.ap().rearrange("o (hh t) -> (o hh) t", t=TOK), in_=afm[:, :]),
               inc=[("st", 16)])
        P.emit("s", lambda e: e.dma_start(
            out=b.bounce.ap()[0:1, 0:NH * (NCH + 1)].rearrange(
                "o (hh c) -> (o hh) c", c=NCH + 1),
            in_=sc48b[:, 0:NCH + 1]), inc=[("st", 16)])
        P.wait("s", "st")
        for c in range(NCH + 1):
            P.emit("s", lambda e, c=c: e.dma_start(
                out=b.prow[0:1, c * NH:(c + 1) * NH],
                in_=b.bounce.ap()[0:1, 0:NH * (NCH + 1)].rearrange(
                    "o (hh c) -> o c hh", c=NCH + 1)[:, c, :],
                allow_slow_non_contiguous=True), inc=[("ld", 16)])
        P.wait("s", "ld")
        P.emit("s", lambda e: e.dma_start(
            out=agh[:, :].rearrange("p (k t) -> p k t", t=3) if False else agh[:, :],
            in_=b.cc_halo_out.ap()[0:128, :]), inc=[("ld", 16)])
        P.flush()

        with contextlib.ExitStack() as pes:
            ps_t = b.psum(pes, "ps_t", [128, 512])
            ps_tb = b.psum(pes, "ps_tb", [128, 1024], BF16)
            items = []
            for c in range(NCH):
                items.append((
                    (lambda t, c=c: t.transpose(ps_t[:, 0:NH],
                                                sc48[0:NH, c * 128:(c + 1) * 128],
                                                b.idf[0:NH, 0:NH])),
                    (lambda v, c=c: v.tensor_copy(b.deT[:, c * NH:(c + 1) * NH],
                                                  ps_t[:, 0:NH])),
                ))
            for c in range(NCH + 1):
                items.append((
                    (lambda t, c=c: t.matmul(ps_t[:, 0:NH], b.ones_rf[:, :],
                                             b.prow[0:1, c * NH:(c + 1) * NH],
                                             start=True, stop=True)),
                    (lambda v, c=c: v.tensor_copy(b.pcols[:, c * NH:(c + 1) * NH],
                                                  ps_t[:, 0:NH])),
                ))
            for c in range(NCH):
                items.append((
                    (lambda t, c=c: t.transpose(ps_tb[:, 0:128],
                                                b.b_bf[:, c * 128:(c + 1) * 128],
                                                b.idb[:, :])),
                    (lambda v, c=c: v.tensor_copy(b.bT[:, c * 128:(c + 1) * 128],
                                                  ps_tb[:, 0:128])),
                ))
            P = Prog(b)
            pe_b, dv_b = b.cnt["mm"], b.cnt["ev"]
            n = len(items)
            for i, (epe, _edv) in enumerate(items):
                if i > 0:
                    P.wait("t", "ev", dv_b + i)
                P.emit("t", epe, inc=[("mm", 1)])
            for i, (_epe, edv) in enumerate(items):
                P.wait("v", "mm", pe_b + i + 1)
                P.emit("v", edv, inc=[("ev", 1)])
            P.emit("g", lambda e: e.memset(b.runall[:, (NCH - 1) * NH:NCH * NH], 1.0))
            P.flush()
            P = Prog(b)
            for c in range(NCH - 2, -1, -1):
                P.emit("v", lambda e, c=c: e.tensor_mul(
                    b.runall[:, c * NH:(c + 1) * NH],
                    b.runall[:, (c + 1) * NH:(c + 2) * NH],
                    b.pcols[:, (c + 1) * NH:(c + 2) * NH]), inc=[("ev", 1)])
                P.wait("v", "ev")
            P.wait("v", "ld")
            P.emit("v", lambda e: e.tensor_scalar(
                b.halo[:, :], agh[:, :], b.hmask_sb[:, 0:1], None, ALU.mult),
                inc=[("ev", 1)])
            P.flush()
            P = Prog(b)
            for d in range(1, NCORES):
                P.emit("s", lambda e, d=d: e.dma_start(
                    out=agh[:, :], in_=b.cc_halo_out.ap()[d * 128:(d + 1) * 128, :]),
                    inc=[("ld", 16)])
                P.wait("s", "ld")
                P.flush()
                P = Prog(b)
                P.wait("v", "ld")
                P.emit("v", lambda e, d=d: e.scalar_tensor_tensor(
                    b.halo[:, :], agh[:, :], b.hmask_sb[:, d:d + 1], b.halo[:, :],
                    ALU.mult, ALU.add), inc=[("ev", 1)])
                P.flush()
                P = Prog(b)
            P.emit("v", lambda e: e.tensor_add(b.halo[:, :], b.halo[:, :],
                                               b.convinit_sb[:, :]), inc=[("ev", 1)])
            P.flush()


def _m3_conv(b, ms):
    nc = b.nc
    with contextlib.ExitStack() as fs:
        s = lambda n, sh, d: fs.enter_context(nc.sbuf_tensor(b._nm(n), sh, d))
        xck = s("xck", [128, 3 + TOK], BF16)
        convt = s("convt", [128, TOK], BF16)
        with contextlib.ExitStack() as pes:
            ps_c = b.psum(pes, "ps_c", [128, 512])
            ps_tb = b.psum(pes, "ps_tb", [128, 1024], BF16)
            for k in range(NKI):
                P = Prog(b)
                P.emit("s", lambda e, k=k: e.dma_start(
                    out=xck[:, 3:3 + TOK],
                    in_=b.xc_d.ap()[k * 128:(k + 1) * 128, :]), inc=[("ld", 16)])
                P.wait("s", "ld")
                P.flush()
                P = Prog(b)
                P.emit("v", lambda e, k=k: e.tensor_copy(
                    xck[:, 0:3], b.halo[:, k * 3:(k + 1) * 3]), inc=[("ev", 1)])
                P.wait("v", "ev")
                for tc in range(4):
                    o = tc * 512
                    P.emit("v", lambda e, o=o, k=k: e.tensor_scalar(
                        ps_c[:, 0:512], xck[:, o:o + 512],
                        b.convw_sb[:, k * 4:k * 4 + 1], b.convb_sb[:, k:k + 1],
                        ALU.mult, ALU.add))
                    for kk in range(1, 4):
                        P.emit("v", lambda e, o=o, k=k, kk=kk: e.scalar_tensor_tensor(
                            ps_c[:, 0:512], xck[:, o + kk:o + kk + 512],
                            b.convw_sb[:, k * 4 + kk:k * 4 + kk + 1],
                            ps_c[:, 0:512], ALU.mult, ALU.add))
                    P.emit("v", lambda e: e.engine_nop(), inc=[("ev", 1)])
                    P.wait("a", "ev")
                    P.emit("a", lambda e, o=o: e.activation(
                        convt[:, o:o + 512], ps_c[:, 0:512], AF.Silu), inc=[("ac", 1)])
                    P.wait("v", "ac")
                for c in range(NCH):
                    P.wait("t", "ac")
                    P.wait("t", "ev")
                    P.emit("t", lambda e, c=c: e.transpose(
                        ps_tb[:, 0:128], convt[:, c * 128:(c + 1) * 128], b.idb[:, :]),
                        inc=[("mm", 1)])
                    P.wait("v", "mm")
                    P.emit("v", lambda e, k=k, c=c: e.tensor_copy(
                        b.big[:, k * TOK + c * 128:k * TOK + (c + 1) * 128],
                        ps_tb[:, 0:128]), inc=[("ev", 1)])
                P.flush()


def xslice(b, hh, c):
    k, half = divmod(hh, 2)
    off = k * TOK + c * 128 + half * 64
    return b.big[:, off:off + 64]


def _m4_pass1_exchange(b, ms):
    nc = b.nc
    with contextlib.ExitStack() as fs:
        s = lambda n, sh, d: fs.enter_context(nc.sbuf_tensor(b._nm(n), sh, d))
        bhat = s("bhat", [128, 2 * 128], BF16)
        sstage = s("sstage", [128, NH * PH // 2], F32)
        with contextlib.ExitStack() as pes:
            ps_sc = b.psum(pes, "ps_sc", [128, NH * PH])
            P = Prog(b)
            pe_b, dv_b = b.cnt["mm"], b.cnt["ev"]
            i = 0
            for hh in range(NH):
                for ci, c in enumerate(reversed(range(NCH))):
                    P.wait("t", "ev", dv_b + i + 1)
                    P.emit("t", lambda e, hh=hh, c=c, ci=ci, i=i: e.matmul(
                        ps_sc[:, hh * PH:(hh + 1) * PH],
                        bhat[:, (i % 2) * 128:(i % 2) * 128 + 128],
                        xslice(b, hh, c),
                        start=(ci == 0), stop=(ci == NCH - 1)), inc=[("mm", 1)])
                    i += 1
            i = 0
            for hh in range(NH):
                for ci, c in enumerate(reversed(range(NCH))):
                    if i >= 2:
                        P.wait("v", "mm", pe_b + i - 1)
                    P.emit("v", lambda e, hh=hh, c=c, i=i: e.tensor_scalar(
                        bhat[:, (i % 2) * 128:(i % 2) * 128 + 128],
                        b.bT[:, c * 128:(c + 1) * 128],
                        b.deT[:, c * NH + hh:c * NH + hh + 1],
                        b.runall[:, c * NH + hh:c * NH + hh + 1],
                        ALU.mult, ALU.mult), inc=[("ev", 1)])
                    i += 1
            P.flush()
            P = Prog(b)
            P.emit("g", lambda e: e.memset(sstage[0:1, :], 0.0))
            P.emit("g", lambda e: e.dma_start(
                out=b.cc_sum_in.ap()[128:129, 0:1536], in_=sstage[0:1, :]),
                inc=[("gp", 16)])
            P.emit("g", lambda e: e.dma_start(
                out=b.cc_sum_in.ap()[128:129, 1536:3072], in_=sstage[0:1, :]),
                inc=[("gp", 16)])
            P.wait("g", "gp")
            P.flush()
            P = Prog(b)
            P.wait("v", "mm")
            for hf in range(2):
                if hf == 1:
                    P.wait("v", "st")
                P.emit("v", lambda e, hf=hf: e.tensor_copy(
                    sstage[:, :], ps_sc[:, hf * 1536:(hf + 1) * 1536]), inc=[("ev", 1)])
                P.wait("s", "ev")
                P.emit("s", lambda e, hf=hf: e.dma_start(
                    out=b.cc_sum_in.ap()[0:128, hf * 1536:(hf + 1) * 1536],
                    in_=sstage[:, :]), inc=[("st", 16)])
                P.wait("s", "st")
            P.emit("s", lambda e: e.dma_start(
                out=b.cc_sum_in.ap()[128:129, :].rearrange("o (a hh) -> o a hh", hh=NH)
                if False else b.cc_sum_in.ap()[128:129, 0:NH],
                in_=b.prow[0:1, NCH * NH:(NCH + 1) * NH]), inc=[("st", 16)])
            P.wait("s", "st")
            P.flush()

        P = Prog(b)
        P.emit("g", lambda e: e.collective_compute(
            "AllGather", ALU.bypass, replica_groups=[list(range(NCORES))],
            ins=[b.cc_sum_in.ap().opt()], outs=[b.cc_sum_out.ap().opt()]),
            inc=[("cc", 1)])
        P.wait("g", "cc")
        P.flush()

        with contextlib.ExitStack() as fs:
            s = lambda n, sh, d: fs.enter_context(nc.sbuf_tensor(b._nm(n), sh, d))
            S_sb = s("S_sb", [128, NH * PH], F32)
            pd_row = s("pd_row", [1, NH], F32)
            acols = s("acols", [128, NH], F32)
            with contextlib.ExitStack() as pes:
                ps_pc = b.psum(pes, "ps_pc", [128, 512])
                P = Prog(b)
                P.emit("s", lambda e: e.dma_start(out=b.hst[:, :], in_=b.h0_in),
                       inc=[("ld", 16)])
                P.wait("s", "ld")
                P.flush()
                for d in range(NCORES):
                    P = Prog(b)
                    P.emit("s", lambda e, d=d: e.dma_start(
                        out=pd_row[0:1, :],
                        in_=b.cc_sum_out.ap()[d * 129 + 128:d * 129 + 129, 0:NH]),
                        inc=[("ld", 16)])
                    P.emit("s", lambda e, d=d: e.dma_start(
                        out=S_sb[:, :],
                        in_=b.cc_sum_out.ap()[d * 129:d * 129 + 128, :]),
                        inc=[("ld", 16)])
                    P.wait("s", "ld")
                    P.flush()
                    P = Prog(b)
                    P.emit("t", lambda e: e.matmul(
                        ps_pc[:, 0:NH], b.ones_rf[:, :], pd_row[0:1, :],
                        start=True, stop=True), inc=[("mm", 1)])
                    P.wait("v", "mm")
                    P.emit("v", lambda e, d=d: e.tensor_scalar(
                        acols[:, :], ps_pc[:, 0:NH], b.cm128[:, d:d + 1],
                        None, ALU.mult), inc=[("ev", 1)])
                    P.wait("v", "ev")
                    P.emit("v", lambda e, d=d: e.tensor_scalar(
                        acols[:, :], acols[:, :], b.cm128i[:, d:d + 1],
                        None, ALU.add), inc=[("ev", 1)])
                    P.wait("v", "ev")
                    P.emit("v", lambda e, d=d: e.tensor_scalar(
                        S_sb[:, :], S_sb[:, :], b.cm128[:, d:d + 1], None, ALU.mult),
                        inc=[("ev", 1)])
                    P.wait("v", "ev")
                    for hh in range(NH):
                        P.emit("v", lambda e, hh=hh: e.scalar_tensor_tensor(
                            b.hst[:, hh * PH:(hh + 1) * PH],
                            b.hst[:, hh * PH:(hh + 1) * PH],
                            acols[:, hh:hh + 1],
                            S_sb[:, hh * PH:(hh + 1) * PH], ALU.mult, ALU.add),
                            inc=[("ev", 1)])
                    P.flush()


def _m5_pass2_gate(b, ms):
    nc = b.nc
    with contextlib.ExitStack() as fs:
        s = lambda n, sh, d: fs.enter_context(nc.sbuf_tensor(b._nm(n), sh, d))
        mask_sb = s("mask", [128, 128], BF16)
        coef_sb = s("coef", [128, 128], F32)
        gm_sb = s("gm", [128, 128], BF16)
        mc_sb = s("mc", [128, 128], F32)
        bhat_sb = s("bhat2", [128, 128], BF16)
        ytm = s("ytm", [128, DI], BF16)
        yz = s("yz", [128, DI], BF16)
        ygst = s("ygst", [128, DI], BF16)
        arow = s("arow", [1, NH * 128 // 2], BF16)
        ycol = s("ycol", [128, 8], F32)
        with contextlib.ExitStack() as pes:
            ps_a = b.psum(pes, "ps_a", [128, 512])
            ps_g = b.psum(pes, "ps_g", [128, 512])
            ps_y = b.psum(pes, "ps_y", [128, 512])
            ps_s = b.psum(pes, "ps_s", [128, 512])
            ygpar = [0, 0]
            for c in range(NCH):
                P = Prog(b)
                P.emit("g", lambda e, c=c: e.dma_start(
                    out=arow[0:1, :].rearrange("o (hh j) -> o hh j", j=128),
                    in_=b.a_d.ap().rearrange("o (hh t) -> o hh t", t=TOK)[
                        :, 0:24, c * 128:(c + 1) * 128]), inc=[("gp", 16)])
                P.wait("g", "gp")
                P.flush()

                pe0, dv0 = b.cnt["mm"], b.cnt["ev"]
                P = Prog(b)
                P.emit("t", lambda e, c=c: e.matmul(
                    ps_g[:, 0:128], b.b_bf[:, c * 128:(c + 1) * 128],
                    b.c_bf[:, c * 128:(c + 1) * 128], start=True, stop=True),
                    inc=[("mm", 1)])
                for hh in range(NH):
                    if hh > 0:
                        P.wait("t", "ev", dv0 + 7 * (hh - 1) + 2)
                    if hh == 24:
                        P.wait("g", "mm", pe0 + 1 + 3 * 23 + 1)
                        P.emit("g", lambda e, c=c: e.dma_start(
                            out=arow[0:1, :].rearrange("o (hh j) -> o hh j", j=128),
                            in_=b.a_d.ap().rearrange("o (hh t) -> o hh t", t=TOK)[
                                :, 24:48, c * 128:(c + 1) * 128]), inc=[("gp", 16)])
                        P.wait("t", "gp")
                    P.emit("t", lambda e, hh=hh: e.matmul(
                        ps_a[:, 0:128], b.ones_rb[:, :],
                        arow[0:1, (hh % 24) * 128:(hh % 24 + 1) * 128],
                        start=True, stop=True), inc=[("mm", 1)])
                    P.wait("t", "ev", dv0 + 7 * hh + 4)
                    P.emit("t", lambda e, hh=hh, c=c: e.matmul(
                        ps_y[:, 0:PH], gm_sb[:, :], xslice(b, hh, c),
                        start=True, stop=False))
                    P.emit("t", lambda e, hh=hh: e.matmul(
                        ps_y[:, 0:PH], mc_sb[:, :], b.hst[:, hh * PH:(hh + 1) * PH],
                        start=False, stop=True), inc=[("mm", 1)])
                    P.wait("t", "ev", dv0 + 7 * hh + 5)
                    P.emit("t", lambda e, hh=hh, c=c: e.matmul(
                        ps_s[:, 0:PH], bhat_sb[:, :], xslice(b, hh, c),
                        start=True, stop=True), inc=[("mm", 1)])

                for hh in range(NH):
                    P.wait("v", "mm", pe0 + 1 + 3 * hh + 1)
                    P.emit("v", lambda e: e.tensor_tensor_scan(
                        mask_sb[:, :], ps_a[:, 0:128], b.idf[:, :], 0.0,
                        ALU.mult, ALU.add), inc=[("ev", 1)])
                    P.emit("v", lambda e: e.tensor_tensor_scan(
                        coef_sb[:, :], ps_a[:, 0:128], b.zero128[:, :], 1.0,
                        ALU.mult, ALU.add), inc=[("ev", 1)])
                    P.emit("v", lambda e: e.tensor_mul(gm_sb[:, :], ps_g[:, 0:128],
                                                       mask_sb[:, :]), inc=[("ev", 1)])
                    P.emit("v", lambda e, c=c: e.tensor_mul(
                        mc_sb[:, :], b.c_bf[:, c * 128:(c + 1) * 128], coef_sb[:, :]),
                        inc=[("ev", 1)])
                    P.emit("v", lambda e, c=c, hh=hh: e.tensor_scalar(
                        bhat_sb[:, :], b.bT[:, c * 128:(c + 1) * 128],
                        b.deT[:, c * NH + hh:c * NH + hh + 1], None, ALU.mult),
                        inc=[("ev", 1)])
                    P.wait("v", "mm", pe0 + 1 + 3 * hh + 2)
                    P.emit("v", lambda e, hh=hh, c=c: e.scalar_tensor_tensor(
                        ytm[:, hh * PH:(hh + 1) * PH], xslice(b, hh, c),
                        b.dcols[:, hh:hh + 1], ps_y[:, 0:PH], ALU.mult, ALU.add),
                        inc=[("ev", 1)])
                    P.wait("v", "mm", pe0 + 1 + 3 * hh + 3)
                    P.emit("v", lambda e, hh=hh: e.scalar_tensor_tensor(
                        b.hst[:, hh * PH:(hh + 1) * PH], b.hst[:, hh * PH:(hh + 1) * PH],
                        coef_sb[:, 127:128], ps_s[:, 0:PH], ALU.mult, ALU.add),
                        inc=[("ev", 1)])
                P.flush()

                P = Prog(b)
                P.wait("a", "ev")
                P.emit("a", lambda e: e.activation(yz[:, :], ytm[:, :], AF.Square),
                       inc=[("ac", 1)])
                P.wait("v", "ac")
                P.emit("v", lambda e: e.reduce_sum(ycol[:, 0:1], yz[:, :], axis=AX.X),
                       inc=[("ev", 1)])
                P.wait("v", "ev")
                P.emit("v", lambda e: e.tensor_scalar(ycol[:, 1:2], ycol[:, 0:1],
                                                      1.0 / DI, None, ALU.mult),
                       inc=[("ev", 1)])
                P.wait("a", "ev")
                P.emit("a", lambda e: e.activation(ycol[:, 2:3], ycol[:, 1:2], AF.Ln,
                                                   bias=b.epsc[:, :]), inc=[("ac", 1)])
                P.wait("a", "ac")
                P.emit("a", lambda e: e.activation(ycol[:, 3:4], ycol[:, 2:3], AF.Exp,
                                                   scale=-0.5), inc=[("ac", 1)])
                P.wait("a", "ev")
                P.emit("a", lambda e, c=c: e.dma_start_transpose(
                    yz[:, :], b.z_d.ap()[:, c * 128:(c + 1) * 128]), inc=[("sc", 16)])
                P.wait("a", "sc")
                P.emit("a", lambda e: e.activation(yz[:, :], yz[:, :], AF.Silu),
                       inc=[("ac", 1)])
                P.wait("v", "ac")
                P.emit("v", lambda e: e.tensor_mul(yz[:, :], yz[:, :], b.innw_rep[:, :]),
                       inc=[("ev", 1)])
                if ygpar[0] > 0:
                    P.wait("v", "z0", ygpar[0])
                P.emit("v", lambda e: e.scalar_tensor_tensor(
                    ygst[:, :], ytm[:, :], ycol[:, 3:4],
                    yz[:, :], ALU.mult, ALU.mult), inc=[("ev", 1)])
                P.wait("s", "ev")
                P.emit("s", lambda e, c=c: e.dma_start(
                    out=b.yg_d.ap()[c * 128:(c + 1) * 128, :],
                    in_=ygst[:, :]), inc=[("z0", 16)])
                ygpar[0] = b.cnt["z0"]
                P.flush()
        P = Prog(b)
        P.wait("s", "z0", b.cnt["z0"])
        P.wait("s", "z1", b.cnt["z1"])
        P.flush()


def _m6_outproj(b, ms, src, dst):
    nc = b.nc
    with contextlib.ExitStack() as fs:
        s = lambda n, sh, d: fs.enter_context(nc.sbuf_tensor(b._nm(n), sh, d))
        ygr = s("ygr", [128, NKI * 512], BF16)
        wbuf = s("wbufo", [128, 2 * NKI * 128], BF16)
        x1t = s("x1t", [128, 2 * 512], F32)
        x2t = s("x2t", [128, 2 * 512], F32)
        reps = s("reps", [128, 2 * 512], F32)
        rw1 = s("rw1", [1, 512], F32)
        rw2 = s("rw2", [1, 512], F32)
        with contextlib.ExitStack() as pes:
            ps_m0 = b.psum(pes, "ps_m0", [128, 512])
            ps_m1 = b.psum(pes, "ps_m1", [128, 512])
            ps_r = b.psum(pes, "ps_r", [128, 1024])
            x1par = [0, 0]
            x2par = [0, 0]
            for tau in range(4):
                P = Prog(b)
                for k in range(NKI):
                    P.emit("a", lambda e, k=k, tau=tau: e.dma_start_transpose(
                        ygr[:, k * 512:(k + 1) * 512],
                        b.yg_d.ap()[tau * 512:(tau + 1) * 512, k * 128:(k + 1) * 128]),
                        inc=[("sc", 16)])
                P.wait("a", "sc")
                P.flush()
                P = Prog(b)
                P.emit("v", lambda e, tau=tau: e.tensor_scalar(
                    rw1[0:1, :], b.srow[0:1, tau * 512:(tau + 1) * 512], 1.0, None,
                    ALU.add), inc=[("ev", 1)])
                P.emit("v", lambda e, tau=tau: e.tensor_mul(
                    rw2[0:1, :], b.srow[0:1, tau * 512:(tau + 1) * 512],
                    b.mrow[0:1, tau * 512:(tau + 1) * 512]), inc=[("ev", 1)])
                P.wait("t", "ev")
                P.emit("t", lambda e: e.matmul(ps_r[:, 0:512], b.ones_rf[:, :],
                                               rw1[0:1, :], start=True, stop=True))
                P.emit("t", lambda e: e.matmul(ps_r[:, 512:1024], b.ones_rf[:, :],
                                               rw2[0:1, :], start=True, stop=True),
                       inc=[("mm", 1)])
                P.wait("v", "mm")
                P.emit("v", lambda e: e.tensor_copy(reps[:, 0:512], ps_r[:, 0:512]),
                       inc=[("ev", 1)])
                P.emit("v", lambda e: e.tensor_copy(reps[:, 512:1024],
                                                    ps_r[:, 512:1024]), inc=[("ev", 1)])
                P.flush()

                P = Prog(b)
                ps_mm = [ps_m0, ps_m1]
                for dc in range(ND):
                    gci = b.wf_hist
                    if gci >= 2:
                        P.wait("g", "mm", b.mm_after_col[gci - 2])
                    b.wf_hist += 1
                    buf = b.wf_hist % 2
                    wsem = "w0" if buf == 0 else "w1"
                    P.emit("g", lambda e, dc=dc, buf=buf: e.dma_start(
                        out=wbuf[:, buf * NKI * 128:(buf + 1) * NKI * 128].rearrange(
                            "p (c n) -> p c n", n=128),
                        in_=b.outprojT_bf.ap()[:, dc * 128:(dc + 1) * 128].rearrange(
                            "(c p) n -> p c n", p=128)), inc=[(wsem, 16)])
                    xp = dc % 2
                    xsem = "z0" if xp == 0 else "z1"
                    if x1par[xp] > 0:
                        P.wait("s", "ev", x1par[xp])
                    P.emit("s", lambda e, dc=dc, tau=tau, xp=xp: e.dma_start(
                        out=x1t[:, xp * 512:(xp + 1) * 512],
                        in_=src[dc * 128:(dc + 1) * 128, tau * 512:(tau + 1) * 512]),
                        inc=[(xsem, 16)])
                    P.wait("t", wsem)
                    ti = tau * ND + dc
                    pp = ps_mm[ti % 2]
                    if ti >= 2:
                        P.wait("t", "ev", b.ev_free_o[ti - 2])
                    for k in range(NKI):
                        P.emit("t", lambda e, pp=pp, k=k, buf=buf: e.matmul(
                            pp[:, 0:512],
                            wbuf[:, buf * NKI * 128 + k * 128:buf * NKI * 128 + (k + 1) * 128],
                            ygr[:, k * 512:(k + 1) * 512],
                            start=(k == 0), stop=(k == NKI - 1)),
                            inc=([("mm", 1)] if k == NKI - 1 else None))
                    b.mm_after_col.append(b.cnt["mm"])
                    yp = ti % 2
                    ysem = "y0" if yp == 0 else "y1"
                    x2s = slice(yp * 512, (yp + 1) * 512)
                    P.wait("v", "mm")
                    P.wait("v", xsem)
                    if x2par[yp] > 0:
                        P.wait("v", ysem, x2par[yp])
                    P.emit("v", lambda e, xp=xp, x2s=x2s: e.tensor_mul(
                        x2t[:, x2s], x1t[:, xp * 512:(xp + 1) * 512], reps[:, 0:512]))
                    P.emit("v", lambda e, x2s=x2s: e.tensor_sub(x2t[:, x2s], x2t[:, x2s],
                                                                reps[:, 512:1024]))
                    P.emit("v", lambda e, pp=pp, x2s=x2s: e.tensor_add(
                        x2t[:, x2s], x2t[:, x2s], pp[:, 0:512]), inc=[("ev", 1)])
                    b.ev_free_o[ti] = b.cnt["ev"]
                    x1par[xp] = b.cnt["ev"]
                    P.wait("s", "ev")
                    P.emit("s", lambda e, dc=dc, tau=tau, x2s=x2s: e.dma_start(
                        out=dst[dc * 128:(dc + 1) * 128, tau * 512:(tau + 1) * 512],
                        in_=x2t[:, x2s]), inc=[(ysem, 16)])
                    x2par[yp] = b.cnt[ysem]
                P.wait("s", "st")
                P.flush()


def make_inputs(inp, core):
    t0 = core * TOK
    x = np.asarray(inp["L0_attn_x"]).reshape(NCORES * TOK, D)[t0:t0 + TOK]
    bf = lambda a: np.ascontiguousarray(a)
    eye = np.eye(128, dtype=np.float32)
    oh = np.zeros((E, E * 128), dtype=np.float32)
    for e in range(E):
        oh[e, e * 128:(e + 1) * 128] = 1.0
    chain_start = core % 4 == 0
    prev = core - 1
    hmask = np.zeros((128, E), dtype=np.float32)
    if not chain_start:
        hmask[:, prev] = 1.0
    cm128 = np.zeros((128, E), dtype=np.float32)
    for d in range(E):
        if d // 4 == core // 4 and d < core:
            cm128[:, d] = 1.0
    conv_state = np.asarray(inp["L0_attn_conv_state"])[0]
    conv_init = (conv_state if chain_start else np.zeros_like(conv_state))
    ssm = np.asarray(inp["L0_attn_ssm_state"])[0]
    m = {
        "x_fm": np.ascontiguousarray(x.T),
        "wg_s": bf(inp["L0_ffn_Wg_s"]), "wu_s": bf(inp["L0_ffn_Wu_s"]),
        "wd_s": bf(inp["L0_ffn_Wd_s"]),
        "wg_e": bf(inp["L0_ffn_Wg"]), "wu_e": bf(inp["L0_ffn_Wu"]),
        "wd_e": bf(inp["L0_ffn_Wd"]),
        "wr": bf(inp["L0_ffn_Wr"]),
        "br_rb": np.ascontiguousarray(
            np.stack([inp["L0_ffn_br"], inp["L0_ffn_rb"]], axis=1)),
        "inproj": bf(inp["L0_attn_in_proj_w"]),
        "outproj": bf(inp["L0_attn_out_proj_w"]),
        "convw": np.ascontiguousarray(
            np.asarray(inp["L0_attn_conv_w"]).reshape(NKI, 128, 4)
            .transpose(1, 0, 2).reshape(128, NKI * 4)),
        "convb": np.ascontiguousarray(
            np.asarray(inp["L0_attn_conv_b"]).reshape(NKI, 128).T),
        "dtbias": np.asarray(inp["L0_attn_dt_bias"]).reshape(NH, 1),
        "alog": np.asarray(inp["L0_attn_A_log"]).reshape(NH, 1),
        "drow": np.asarray(inp["L0_attn_D_param"]).reshape(1, NH),
        "normw": np.ascontiguousarray(
            np.asarray(inp["L0_attn_norm_w"]).reshape(ND, 128).T),
        "innw": np.asarray(inp["L0_attn_inner_norm_w"]).reshape(1, DI),
        "h0_in": np.ascontiguousarray(ssm.transpose(1, 0, 2).reshape(128, NH * PH)),
        "conv_init": np.ascontiguousarray(
            conv_init.reshape(NKI, 128, 3).transpose(1, 0, 2).reshape(128, NKI * 3)),
        "flaginv": np.full((128, 1), 0.0 if chain_start else 1.0, dtype=np.float32),
        "hmask": hmask,
        "cmask128": cm128,
        "idf_in": eye,
        "idb_in": eye.astype(ml_dtypes.bfloat16),
        "oh_in": oh,
    }
    return {k: np.ascontiguousarray(np.asarray(v, dtype=(
        ml_dtypes.bfloat16 if k == "idb_in" else np.float32))) for k, v in m.items()}




def kernel(**inputs):
    mk_ = MK(("ffn1", "mamba", "ffn2"))
    nc = mk_.build()
    in_maps = [make_inputs(inputs, c) for c in range(NCORES)]
    res = run_bass_kernel_spmd(nc, in_maps, core_ids=list(range(NCORES)))
    outs = [np.asarray(r["y_fm"], dtype=np.float32).T for r in res.results]
    return np.concatenate(outs, axis=0).reshape(2, 8192, D)


# version 04
# speedup vs baseline: 1.1377x; 1.0305x over previous
import contextlib
import numpy as np
import ml_dtypes
import concourse.bass as bass
import concourse.mybir as mybir
from concourse.bass_utils import run_bass_kernel_spmd

F32 = mybir.dt.float32
BF16 = mybir.dt.bfloat16
AF = mybir.ActivationFunctionType
ALU = mybir.AluOpType
AX = mybir.AxisListType

NCORES = 8
TOK = 2048
HALF = 1024
D = 2048
ND = 16
H = 1408
NHC = 11
E = 8
DI = 3072
NKI = 24
NH = 48
PH = 64
NCH = 16
EPS = 1e-5


class Prog:

    ENGS = ("t", "v", "a", "s", "g")

    def __init__(self, b):
        self.b = b
        self.q = {e: [] for e in self.ENGS}

    def on(self, eng, fn):
        self.q[eng].append(fn)

    def wait(self, eng, sem, val=None):
        v = self.b.cnt[sem] if val is None else val
        self.q[eng].append(lambda e, s=self.b.sems[sem], v=v: e.wait_ge(s, v))

    def emit(self, eng, fn, inc=None):
        incs = []
        if inc:
            for sem, k in inc:
                self.b.cnt[sem] += k
                incs.append((self.b.sems[sem], k))

        def run(e, fn=fn, incs=incs):
            i = fn(e)
            for s, k in incs:
                i = i.then_inc(s, k)
        self.q[eng].append(run)
        return {s: self.b.cnt[s] for s in self.b.cnt} if inc else None

    def flush(self):
        nc = self.b.nc
        with nc.Block() as blk:
            if self.q["t"]:
                blk.tensor(lambda e, fns=self.q["t"]: [f(e) for f in fns])
            if self.q["v"]:
                blk.vector(lambda e, fns=self.q["v"]: [f(e) for f in fns])
            if self.q["a"]:
                blk.scalar(lambda e, fns=self.q["a"]: [f(e) for f in fns])
            if self.q["s"]:
                blk.sync(lambda e, fns=self.q["s"]: [f(e) for f in fns])
            if self.q["g"]:
                blk.gpsimd(lambda e, fns=self.q["g"]: [f(e) for f in fns])
        self.q = {e: [] for e in self.ENGS}


class MK:
    def __init__(self, phases=("ffn1", "mamba", "ffn2"), detect_races=False):
        self.phases = phases
        nc = bass.Bass("TRN2", target_bir_lowering=False, detect_race_conditions=detect_races)
        self.nc = nc
        self.cnt = {}
        self.sems = {}
        for name in ["ld", "st", "sc", "gp", "c0", "c1", "c2", "c3", "w0", "w1", "z0", "z1", "y0", "y1", "cc", "mm", "ev", "ac"]:
            self.sems[name] = nc.alloc_semaphore(name)
            self.cnt[name] = 0

        ap = lambda n, s, d=F32: nc.dram_tensor(n, s, d, kind="ExternalInput").ap()
        self.x_in = ap("x_fm", [D, TOK])
        self.wg_s = ap("wg_s", [D, H]);  self.wu_s = ap("wu_s", [D, H])
        self.wd_s = ap("wd_s", [H, D])
        self.wg_e = ap("wg_e", [E, D, H]); self.wu_e = ap("wu_e", [E, D, H])
        self.wd_e = ap("wd_e", [E, H, D])
        self.wr_in = ap("wr", [D, E])
        self.br_rb = ap("br_rb", [E, 2])
        self.inproj = ap("inproj", [6448, D])
        self.outproj = ap("outproj", [D, DI])
        self.convw_in = ap("convw", [128, NKI * 4])
        self.convb_in = ap("convb", [128, NKI])
        self.dtbias_in = ap("dtbias", [NH, 1])
        self.alog_in = ap("alog", [NH, 1])
        self.drow_in = ap("drow", [1, NH])
        self.normw_in = ap("normw", [128, ND])
        self.innw_in = ap("innw", [1, DI])
        self.h0_in = ap("h0_in", [128, NH * PH])
        self.convinit_in = ap("conv_init", [128, NKI * 3])
        self.flaginv_in = ap("flaginv", [128, 1])
        self.hmask_in = ap("hmask", [128, E])
        self.cm128_in = ap("cmask128", [128, E])
        self.idf_in = ap("idf_in", [128, 128])
        self.idb_in = nc.dram_tensor("idb_in", [128, 128], BF16, kind="ExternalInput").ap()
        self.oh_in = ap("oh_in", [E, E * 128])

        self.y_out = nc.dram_tensor("y_fm", [D, TOK], F32, kind="ExternalOutput").ap()
        self.dbg = False

        dr = lambda n, s, d: nc.dram_tensor(n, s, d, kind="Internal")
        self.x1_d = dr("x1_d", [D, TOK], F32)
        self.wr_bf = dr("wr_bf", [D, E], BF16)
        self.inproj_bf = dr("inproj_bf", [6448, D], BF16)
        self.inprojT_bf = dr("inprojT_bf", [D, 6400], BF16)
        self.outproj_bf = dr("outproj_bf", [D, DI], BF16)
        self.outprojT_bf = dr("outprojT_bf", [DI, D], BF16)
        self.z_d = dr("z_d", [DI, TOK], BF16)
        self.xc_d = dr("xc_d", [DI, TOK], BF16)
        self.yg_d = dr("yg_d", [TOK, DI], BF16)
        self.a_d = dr("a_d", [1, NH * TOK], F32)
        self.bounce = dr("bounce", [1, 8192], F32)

        self.cc_halo_in = dr("cc_halo_in", [128, NKI * 3], BF16)
        self.cc_halo_out = dr("cc_halo_out", [NCORES * 128, NKI * 3], BF16)
        self.cc_sum_in = dr("cc_sum_in", [129, NH * PH], F32)
        self.cc_sum_out = dr("cc_sum_out", [NCORES * 129, NH * PH], F32)

        self.ctx = contextlib.ExitStack()
        self.uid = 0
        self.wf_hist = 0

    def _nm(self, n):
        self.uid += 1
        return f"{n}_{self.uid}"

    def sb(self, n, sh, d):
        return self.ctx.enter_context(self.nc.sbuf_tensor(self._nm(n), sh, d))

    def psum(self, es, n, sh, d=F32):
        return es.enter_context(self.nc.psum_tensor(self._nm(n), sh, d))

    def build(self):
        with self.ctx:
            self._alloc_globals()
            self._p0()
            src = self.x_in
            if "ffn1" in self.phases:
                self._ffn_call(src, self.x1_d.ap(), first=True)
                src = self.x1_d.ap()
                if self.chain_ops:
                    P = Prog(self)
                    self._pump_chain(P, len(self.chain_ops))
                    P.flush()
            if "mamba" in self.phases:
                self._mamba(src, self.x1_d.ap())
                src = self.x1_d.ap()
            if "ffn2" in self.phases:
                self._ffn_call(src, self.y_out, first=False)
            else:
                self._copy_out(src)
            self._final_wait()
        return self.nc

    def _alloc_globals(self):
        s = self.sb
        self.idf = s("idf", [128, 128], F32)
        self.idb = s("idb", [128, 128], BF16)
        self.ones_rf = s("ones_rf", [1, 128], F32)
        self.ones_cb = s("ones_cb", [128, 1], BF16)
        self.ones_rb = s("ones_rb", [1, 128], BF16)
        self.zero128 = s("zero128", [128, 128], F32)
        self.epsc = s("epsc", [128, 1], F32)
        self.wr_sb = s("wr_sb", [128, ND * E], BF16)
        self.biasr = s("biasr", [E, 1], F32)
        self.normw_sb = s("normw_sb", [128, ND], F32)
        self.dcols = s("dcols", [128, NH], F32)
        self.rowtmp = s("rowtmp", [1, 64], F32)
        self.convw_sb = s("convw_sb", [128, NKI * 4], F32)
        self.convb_sb = s("convb_sb", [128, NKI], F32)
        self.flaginv_sb = s("flaginv_sb", [128, 1], F32)
        self.hmask_sb = s("hmask_sb", [128, E], F32)
        self.dtb = s("dtb", [NH, 1], F32)
        self.acol = s("acol", [NH, 1], F32)
        self.cm128 = s("cm128", [128, E], F32)
        self.cm128i = s("cm128i", [128, E], F32)
        self.brrb = s("brrb", [E, 2], F32)
        self.dtlhs = s("dtlhs", [128, ND * NH], BF16)

    def _p0(self):
        b = self
        P = Prog(b)
        loads = [
            (self.idf[:, :], self.idf_in), (self.idb[:, :], self.idb_in),
            (self.normw_sb[:, :], self.normw_in),
            (self.convw_sb[:, :], self.convw_in), (self.convb_sb[:, :], self.convb_in),
            (self.flaginv_sb[:, :], self.flaginv_in), (self.hmask_sb[:, :], self.hmask_in),
            (self.dtb[:, :], self.dtbias_in), (self.acol[:, :], self.alog_in),
            (self.cm128[:, :], self.cm128_in), (self.brrb[:, :], self.br_rb),
        ]
        for dst, srcc in loads:
            P.emit("s", lambda e, d=dst, x=srcc: e.dma_start(out=d, in_=x),
                   inc=[("ld", 16)])
        P.emit("g", lambda e: e.dma_start(out=self.wr_bf.ap(), in_=self.wr_in),
               inc=[("gp", 16)])
        P.wait("g", "gp")
        P.emit("g", lambda e: e.dma_start(
            out=self.wr_sb[:, :].rearrange("p (c ee) -> p c ee", ee=E),
            in_=self.wr_bf.ap().rearrange("(c p) ee -> p c ee", p=128)),
            inc=[("gp", 16)])
        P.emit("g", lambda e: e.memset(self.ones_rf[:, :], 1.0))
        P.emit("g", lambda e: e.memset(self.ones_cb[:, :], 1.0))
        P.emit("g", lambda e: e.memset(self.ones_rb[:, :], 1.0))
        P.emit("g", lambda e: e.memset(self.zero128[:, :], 0.0))
        P.emit("g", lambda e: e.memset(self.epsc[:, :], EPS))
        casts = [("c3", self.inproj_bf.ap(), self.inproj),
                 ("c3", self.outproj_bf.ap(), self.outproj)]
        for grp, dst, srcc in casts:
            P.emit("g", lambda e, d=dst, x=srcc: e.dma_start(out=d, in_=x),
                   inc=[(grp, 16)])
        self.castgrp = {"c3": self.cnt["c3"]}
        P.wait("s", "ld")
        P.flush()

        P = Prog(b)
        P.emit("v", lambda e: e.tensor_add(self.biasr[:, :], self.brrb[:, 0:1],
                                           self.brrb[:, 1:2]))
        P.emit("s", lambda e: e.dma_start(out=self.rowtmp[0:1, 0:NH], in_=self.drow_in),
               inc=[("ld", 16)])
        P.wait("t", "ld")
        with self.nc.psum_tensor(self._nm("ps0_p0"), [128, 512], F32) as ps0:
            P.emit("t", lambda e, p=ps0: e.matmul(p[:, 0:NH], self.ones_rf[:, :],
                                                  self.rowtmp[0:1, 0:NH],
                                                  start=True, stop=True),
                   inc=[("mm", 1)])
            P.wait("v", "mm")
            P.emit("v", lambda e, p=ps0: e.tensor_copy(self.dcols[:, :], p[:, 0:NH]))
            P.flush()

        self.stg = self.sb("p0stage", [128, 3200], BF16)
        self.prep_sc_done = None
        self._make_chain_ops()
        if "ffn1" not in self.phases:
            P = Prog(self)
            self._pump_chain(P, len(self.chain_ops))
            P.flush()

    def _make_chain_ops(self):
        stg = self.stg
        first = [True]

        def mk_pair(xf, sf):
            def run(P):
                if first[0]:
                    P.wait("s", "c3", self.castgrp["c3"])
                    first[0] = False
                P.emit("s", xf, inc=[("sc", 16)])
                P.wait("s", "sc")
                P.emit("s", sf, inc=[("sc", 16)])
                P.wait("s", "sc")
            return run
        ops = []
        for dc in range(ND):
            for hf in range(2):
                ops.append(mk_pair(
                    lambda e, dc=dc, hf=hf: e.dma_start_transpose(
                        stg[:, 0:3200],
                        self.inproj_bf.ap()[hf * 3200:(hf + 1) * 3200,
                                            dc * 128:(dc + 1) * 128]),
                    lambda e, dc=dc, hf=hf: e.dma_start(
                        out=self.inprojT_bf.ap()[dc * 128:(dc + 1) * 128,
                                                 hf * 3200:(hf + 1) * 3200],
                        in_=stg[:, 0:3200])))
        for kc in range(NKI):
            ops.append(mk_pair(
                lambda e, kc=kc: e.dma_start_transpose(
                    stg[:, 0:2048], self.outproj_bf.ap()[:, kc * 128:(kc + 1) * 128]),
                lambda e, kc=kc: e.dma_start(
                    out=self.outprojT_bf.ap()[kc * 128:(kc + 1) * 128, :],
                    in_=stg[:, 0:2048])))

        def last(P):
            P.emit("s", lambda e: e.dma_start(
                out=stg[0:48, 0:2048], in_=self.inproj_bf.ap()[6400:6448, :]),
                inc=[("sc", 16)])
            self.prep_sc_done = self.cnt["sc"]
        ops.append(last)
        self.chain_ops = ops

    def _pump_chain(self, P, n):
        while n > 0 and self.chain_ops:
            self.chain_ops.pop(0)(P)
            n -= 1

    def _emit_dtlhs(self):
        P = Prog(self)
        stg = self.stg
        P.wait("t", "sc", self.prep_sc_done)
        with self.nc.psum_tensor(self._nm("ps_tp0"), [128, 1024], BF16) as pstb:
            for dc in range(ND):
                if dc > 0:
                    P.wait("t", "ev")
                P.emit("t", lambda e, dc=dc, p=pstb, stg=stg: e.transpose(
                    p[:, 0:48], stg[0:48, dc * 128:(dc + 1) * 128], self.idb[0:48, 0:48]),
                    inc=[("mm", 1)])
                P.wait("v", "mm")
                P.emit("v", lambda e, dc=dc, p=pstb: e.tensor_copy(
                    self.dtlhs[:, dc * NH:(dc + 1) * NH], p[:, 0:48]), inc=[("ev", 1)])
            P.flush()

    def _ln_tile(self, P, xsrc, sink, tglob, psb, mode="ln"):
        b = self
        ps_sx, ps_sq, ps_r1, ps_r2 = psb
        P.wait("t", "ev")
        P.wait("t", "ac")
        P.wait("a", "mm")
        P.wait("v", "mm")
        for dc in range(ND):
            P.emit("v", lambda e, dc=dc: e.tensor_copy(sink(dc), xsrc(dc)),
                   inc=[("ev", 1)])
            P.wait("a", "ev")
            P.emit("a", lambda e, dc=dc: e.activation(
                self.xsq[:, dc * 512:(dc + 1) * 512], sink(dc),
                AF.Square), inc=[("ac", 1)])
            P.wait("t", "ev")
            P.emit("t", lambda e, dc=dc: e.matmul(
                ps_sx[0:1, 0:512], self.ones_cb[:, :], sink(dc),
                start=(dc == 0), stop=(dc == ND - 1)), inc=([("mm", 1)] if dc == ND - 1 else None))
            P.wait("t", "ac")
            P.emit("t", lambda e, dc=dc: e.matmul(
                ps_sq[0:1, 0:512], self.ones_cb[:, :],
                self.xsq[:, dc * 512:(dc + 1) * 512],
                start=(dc == 0), stop=(dc == ND - 1)), inc=([("mm", 1)] if dc == ND - 1 else None))
        P.wait("v", "mm")
        P.emit("v", lambda e: e.tensor_scalar(self.rowa[0:1, :], ps_sx[0:1, 0:512],
                                              1.0 / D, None, ALU.mult), inc=[("ev", 1)])
        P.emit("v", lambda e: e.tensor_scalar(self.rowb[0:1, :], ps_sq[0:1, 0:512],
                                              1.0 / D, None, ALU.mult), inc=[("ev", 1)])
        P.wait("a", "ev")
        P.emit("a", lambda e: e.activation(self.rowc[0:1, :], self.rowa[0:1, :],
                                           AF.Square), inc=[("ac", 1)])
        P.wait("v", "ac")
        P.emit("v", lambda e: e.tensor_sub(self.rowb[0:1, :], self.rowb[0:1, :],
                                           self.rowc[0:1, :]), inc=[("ev", 1)])
        P.wait("a", "ev")
        P.emit("a", lambda e: e.activation(self.rowc[0:1, :], self.rowb[0:1, :],
                                           AF.Ln, bias=self.epsc[0:1, :]))
        P.emit("a", lambda e: e.activation(self.rowc[0:1, :], self.rowc[0:1, :],
                                           AF.Exp, scale=-0.5), inc=[("ac", 1)])
        scale_row = self.rowc
        if mode == "mamba":
            P.wait("v", "ac")
            P.emit("v", lambda e: e.tensor_copy(self.mrow[0:1, tglob:tglob + 512],
                                                self.rowa[0:1, :]))
            P.emit("v", lambda e: e.tensor_copy(self.srow[0:1, tglob:tglob + 512],
                                                self.rowc[0:1, :]))
            P.emit("v", lambda e: e.tensor_mul(self.rowd[0:1, :], self.rowc[0:1, :],
                                               self.rowc[0:1, :]))
            P.emit("v", lambda e: e.tensor_mul(self.rowd[0:1, :], self.rowd[0:1, :],
                                               self.rowb[0:1, :]), inc=[("ev", 1)])
            P.wait("a", "ev")
            P.emit("a", lambda e: e.activation(self.rowd[0:1, :], self.rowd[0:1, :],
                                               AF.Ln, bias=self.epsc[0:1, :]))
            P.emit("a", lambda e: e.activation(self.rowd[0:1, :], self.rowd[0:1, :],
                                               AF.Exp, scale=-0.5), inc=[("ac", 1)])
            P.wait("v", "ac")
            P.emit("v", lambda e: e.tensor_mul(self.rowd[0:1, :], self.rowd[0:1, :],
                                               self.rowc[0:1, :]), inc=[("ev", 1)])
            scale_row = self.rowd
        P.wait("t", "ev" if mode == "mamba" else "ac")
        P.emit("t", lambda e: e.matmul(ps_r1[:, 0:512], self.ones_rf[:, :],
                                       self.rowa[0:1, :], start=True, stop=True))
        P.emit("t", lambda e, sr=scale_row: e.matmul(ps_r2[:, 0:512], self.ones_rf[:, :],
                                                     sr[0:1, :], start=True, stop=True),
               inc=[("mm", 1)])
        P.wait("v", "mm")
        for dc in range(ND):
            P.emit("v", lambda e, dc=dc: e.tensor_sub(self.tf32[:, 0:512], xsrc(dc),
                                                      ps_r1[:, 0:512]))
            if mode == "mamba":
                P.emit("v", lambda e, dc=dc: e.tensor_mul(self.tf32[:, 0:512],
                                                          self.tf32[:, 0:512],
                                                          ps_r2[:, 0:512]))
                P.emit("v", lambda e, dc=dc: e.tensor_scalar(
                    sink(dc), self.tf32[:, 0:512], self.normw_sb[:, dc:dc + 1], None,
                    ALU.mult), inc=[("ev", 1)])
            else:
                P.emit("v", lambda e, dc=dc: e.tensor_mul(sink(dc), self.tf32[:, 0:512],
                                                          ps_r2[:, 0:512]),
                       inc=[("ev", 1)])
        self.u_done = self.cnt["ev"]

    def _ffn_call(self, src, dst, first):
        b = self
        nc = self.nc
        with contextlib.ExitStack() as fs:
            s = lambda n, sh, d: fs.enter_context(nc.sbuf_tensor(self._nm(n), sh, d))
            self.xh = s("xh", [128, ND * HALF], F32)
            self.uh = s("uh", [128, ND * HALF], BF16)
            self.gp_b = s("gp_b", [128, NHC * HALF], BF16)
            self.wbuf = s("wbuf", [128, 2 * ND * 128], BF16)
            self.wrep = s("wrep", [128, HALF], F32)
            self.xsq = s("xsq", [128, ND * 512], BF16)
            self.tf32 = s("tf32", [128, 512], F32)
            self.rowa = s("rowa", [1, 512], F32)
            self.rowb = s("rowb", [1, 512], F32)
            self.rowc = s("rowc", [1, 512], F32)
            self.rowd = s("rowd", [1, 512], F32)
            self.oh_sb = s("oh_sb", [E, E * 128], F32)
            self.wTh = s("wTh", [E, HALF], F32)
            self.probs = s("probs", [E, 512], F32)
            self.ptm = s("ptm", [128, 8], F32)
            self.maxs = s("maxs", [128, 8], F32)
            self.wtm = s("wtm", [128, 8], F32)
            self.v12 = s("v12", [128, 2], F32)

            P = Prog(b)
            P.emit("s", lambda e: e.dma_start(out=self.oh_sb[:, :], in_=self.oh_in),
                   inc=[("ld", 16)])
            P.wait("s", "ld")
            P.flush()

            for ha in range(2):
                self._ffn_half(src, dst, ha, first)

    def _ffn_half(self, src, dst, ha, first):
        b = self
        nc = self.nc
        t0 = ha * HALF
        P = Prog(b)
        for dc in range(ND):
            P.emit("s", lambda e, dc=dc: e.dma_start(
                out=self.xh[:, dc * HALF:(dc + 1) * HALF],
                in_=src[dc * 128:(dc + 1) * 128, t0:t0 + HALF]), inc=[("ld", 16)])
        P.wait("s", "ld")
        P.flush()

        with contextlib.ExitStack() as pes:
            ps_a0 = self.psum(pes, "ps_a0", [128, 512])
            ps_a1 = self.psum(pes, "ps_a1", [128, 512])
            ps_sx = self.psum(pes, "ps_sx", [128, 512])
            ps_sq = self.psum(pes, "ps_sq", [128, 512])
            ps_r1 = self.psum(pes, "ps_r1", [128, 512])
            ps_r2 = self.psum(pes, "ps_r2", [128, 512])
            ps_x = self.psum(pes, "ps_x", [128, 512])
            psb = (ps_sx, ps_sq, ps_r1, ps_r2)
            ps_mm = [ps_a0, ps_a1]

            P = Prog(b)
            u_tile_done = []
            for ti in range(2):
                xsrc = lambda dc, ti=ti: self.xh[:, dc * HALF + ti * 512:
                                                 dc * HALF + (ti + 1) * 512]
                sink = lambda dc, ti=ti: self.uh[:, dc * HALF + ti * 512:
                                                 dc * HALF + (ti + 1) * 512]
                self._ln_tile(P, xsrc, sink, t0 + ti * 512, psb, mode="ln")
                u_tile_done.append(self.u_done)
                P.wait("t", "ev")
                for dc in range(ND):
                    P.emit("t", lambda e, dc=dc, ti=ti: e.matmul(
                        ps_x[0:8, 0:512], self.wr_sb[:, dc * E:(dc + 1) * E],
                        self.uh[:, dc * HALF + ti * 512:dc * HALF + (ti + 1) * 512],
                        start=(dc == 0), stop=(dc == ND - 1)), inc=(
                        [("mm", 1)] if dc == ND - 1 else None))
                P.wait("a", "mm")
                P.emit("a", lambda e: e.activation(self.probs[:, 0:512], ps_x[0:8, 0:512],
                                                   AF.Sigmoid, bias=self.biasr[:, :]),
                       inc=[("ac", 1)])
                for j in range(4):
                    P.wait("t", "ac")
                    if j > 0 or ti > 0:
                        P.wait("t", "ev")
                    P.emit("t", lambda e, j=j: e.transpose(
                        ps_x[0:128, 0:8] if False else ps_x[:, 0:8],
                        self.probs[0:8, j * 128:(j + 1) * 128], self.idf[0:8, 0:8]),
                        inc=[("mm", 1)])
                    P.wait("v", "mm")
                    for fn in [
                        lambda e: e.tensor_copy(self.ptm[:, :], ps_x[:, 0:8]),
                        lambda e: e.max(out=self.maxs[:, :], in_=self.ptm[:, :]),
                        lambda e: e.tensor_add(self.v12[:, 0:1], self.maxs[:, 0:1],
                                               self.maxs[:, 1:2]),
                        lambda e: e.reciprocal(self.v12[:, 1:2], self.v12[:, 0:1]),
                        lambda e: e.tensor_scalar(self.wtm[:, :], self.ptm[:, :],
                                                  self.maxs[:, 1:2], None, ALU.is_ge),
                        lambda e: e.tensor_mul(self.wtm[:, :], self.wtm[:, :],
                                               self.ptm[:, :]),
                        lambda e: e.tensor_scalar(self.wtm[:, :], self.wtm[:, :],
                                                  self.v12[:, 1:2], None, ALU.mult),
                    ]:
                        P.emit("v", fn, inc=[("ev", 1)])
                        P.wait("v", "ev")
                    P.wait("t", "ev")
                    P.emit("t", lambda e, j=j: e.transpose(
                        ps_x[0:8, 8:136], self.wtm[:, :], self.idf[:, :]),
                        inc=[("mm", 1)])
                    P.wait("v", "mm")
                    P.emit("v", lambda e, j=j, ti=ti: e.tensor_copy(
                        self.wTh[0:8, ti * 512 + j * 128:ti * 512 + (j + 1) * 128],
                        ps_x[0:8, 8:136]), inc=[("ev", 1)])
            self.wth_done = self.cnt["ev"]
            P.flush()

            for e_ in range(9):
                self._ffn_expert(e_, ha, first, ps_mm, ps_x, (ps_r1, ps_r2))

        P = Prog(b)
        P.wait("s", "ev")
        for dc in range(ND):
            P.emit("s", lambda e, dc=dc: e.dma_start(
                out=dst[dc * 128:(dc + 1) * 128, t0:t0 + HALF],
                in_=self.xh[:, dc * HALF:(dc + 1) * HALF]), inc=[("st", 16)])
        P.flush()

    def _ffn_expert(self, e_, ha, first, ps_mm, ps_x, ps_wreps):
        b = self
        P = Prog(b)
        if first and ha == 1:
            self._pump_chain(P, 7)
        shared = (e_ == 0)
        if shared:
            wg, wu, wd = self.wg_s, self.wu_s, self.wd_s
        else:
            wg, wu, wd = self.wg_e[e_ - 1], self.wu_e[e_ - 1], self.wd_e[e_ - 1]

        if not shared:
            P.wait("t", "ev")
            for ti in range(2):
                pw = ps_wreps[ti % 2]
                P.emit("t", lambda e, ti=ti, pw=pw: e.matmul(
                    pw[:, 0:512], self.oh_sb[:, (e_ - 1) * 128:e_ * 128],
                    self.wTh[0:8, ti * 512:(ti + 1) * 512], start=True, stop=True),
                    inc=[("mm", 1)])
                P.wait("v", "mm")
                P.emit("v", lambda e, ti=ti, pw=pw: e.tensor_copy(
                    self.wrep[:, ti * 512:(ti + 1) * 512], pw[:, 0:512]),
                    inc=[("ev", 1)])
        self.wrep_done = self.cnt["ev"]

        nbuf = ND * 128
        cols = [("g", i) for i in range(NHC)] + [("u", i) for i in range(NHC)] + \
               [("d", i) for i in range(ND)]

        mm_of_tile = {}
        ev_free = []
        if not hasattr(self, "mm_after_col"):
            self.mm_after_col = []
        tidx = 0
        P.wait("t", "ev")
        P.wait("t", "ac")
        for ci, (kind, col) in enumerate(cols):
            gci = self.wf_hist
            if gci >= 2:
                P.wait("g", "mm", self.mm_after_col[gci - 2])
            self.wf_hist += 1
            if kind == "g":
                srcw = wg[:, col * 128:(col + 1) * 128]
            elif kind == "u":
                srcw = wu[:, col * 128:(col + 1) * 128]
            else:
                srcw = wd[:, col * 128:(col + 1) * 128]
            buf = self.wf_hist % 2
            wsem = "w0" if buf == 0 else "w1"
            nkw = ND if kind != "d" else NHC
            P.emit("g", lambda e, srcw=srcw, buf=buf, nkw=nkw: e.dma_start(
                out=self.wbuf[:, buf * nbuf:buf * nbuf + nkw * 128].rearrange(
                    "p (c n) -> p c n", n=128),
                in_=srcw.rearrange("(c p) n -> p c n", p=128)), inc=[(wsem, 16)])

            P.wait("t", wsem)
            nk = ND if kind != "d" else NHC
            for ti in range(2):
                pp = ps_mm[tidx % 2]
                if tidx >= 2:
                    P.wait("t", "ev" if ev_free[tidx - 2][0] == "ev" else "ac",
                           ev_free[tidx - 2][1])
                for k in range(nk):
                    if kind == "d":
                        rhs = self.gp_b[:, k * HALF + ti * 512:k * HALF + (ti + 1) * 512]
                    else:
                        rhs = self.uh[:, k * HALF + ti * 512:k * HALF + (ti + 1) * 512]
                    last = (k == nk - 1)
                    incs = [("mm", 1)] if last else None
                    P.emit("t", lambda e, pp=pp, k=k, rhs=rhs, buf=buf, kk=(k == 0),
                           st=last: e.matmul(
                        pp[:, 0:512],
                        self.wbuf[:, buf * nbuf + k * 128:buf * nbuf + (k + 1) * 128],
                        rhs, start=kk, stop=st), inc=incs)
                mm_of_tile[(kind, col, ti)] = self.cnt["mm"]

                gslice = self.gp_b[:, col * HALF + ti * 512:col * HALF + (ti + 1) * 512] \
                    if kind != "d" else None
                if kind == "g":
                    P.wait("a", "mm", mm_of_tile[(kind, col, ti)])
                    P.emit("a", lambda e, pp=pp, gs=gslice: e.activation(
                        gs, pp[:, 0:512], AF.Silu), inc=[("ac", 1)])
                    ev_free.append(("ac", self.cnt["ac"]))
                elif kind == "u":
                    P.wait("v", "mm", mm_of_tile[(kind, col, ti)])
                    if not shared:
                        P.emit("v", lambda e, pp=pp, gs=gslice: e.tensor_mul(
                            self.tf32[:, 0:512], pp[:, 0:512], gs))
                        P.emit("v", lambda e, gs=gslice, ti=ti: e.tensor_mul(
                            gs, self.tf32[:, 0:512],
                            self.wrep[:, ti * 512:(ti + 1) * 512]), inc=[("ev", 1)])
                    else:
                        P.emit("v", lambda e, pp=pp, gs=gslice: e.tensor_mul(
                            gs, pp[:, 0:512], gs), inc=[("ev", 1)])
                    ev_free.append(("ev", self.cnt["ev"]))
                else:
                    xsl = self.xh[:, col * HALF + ti * 512:col * HALF + (ti + 1) * 512]
                    P.wait("v", "mm", mm_of_tile[(kind, col, ti)])
                    P.emit("v", lambda e, pp=pp, xsl=xsl: e.scalar_tensor_tensor(
                        xsl, pp[:, 0:512], 0.5, xsl, ALU.mult, ALU.add),
                        inc=[("ev", 1)])
                    ev_free.append(("ev", self.cnt["ev"]))
                tidx += 1
            self.mm_after_col.append(self.cnt["mm"])
        P.flush()

    def _copy_out(self, src):
        P = Prog(self)
        for dc in range(ND):
            P.emit("s", lambda e, dc=dc: e.dma_start(
                out=self.y_out[dc * 128:(dc + 1) * 128, :],
                in_=src[dc * 128:(dc + 1) * 128, :]), inc=[("st", 16)])
        P.wait("s", "st")
        P.flush()

    def _final_wait(self):
        P = Prog(self)
        P.wait("g", "st")
        P.flush()

    def _mamba(self, src, dst):
        emit_mamba(self, src, dst)




def emit_mamba(b, src, dst):
    nc = b.nc
    b.ev_free_m = {}
    b.ev_free_o = {}
    with contextlib.ExitStack() as ms:
        s = lambda n, sh, d: ms.enter_context(nc.sbuf_tensor(b._nm(n), sh, d))
        psum = lambda es, n, sh, d=F32: b.psum(es, n, sh, d)
        b.mrow = s("mrow", [1, TOK], F32)
        b.srow = s("srow", [1, TOK], F32)
        b.innw_rep = s("innw_rep", [128, DI], BF16)
        b.b_bf = s("b_bf", [128, TOK], BF16)
        b.c_bf = s("c_bf", [128, TOK], BF16)
        b.dtraw = s("dtraw", [NH, TOK], F32)
        b.halo = s("halo", [128, NKI * 3], BF16)
        b.convinit_sb = s("convinit_sb", [128, NKI * 3], F32)
        b.tot = s("tot", [NH, NCH + 1], F32)
        b.deT = s("deT", [128, NCH * NH], F32)
        b.prow = s("prow", [1, (NCH + 1) * NH], F32)
        b.pcols = s("pcols", [128, (NCH + 1) * NH], F32)
        b.runall = s("runall", [128, NCH * NH], F32)
        b.bT = s("bT", [128, NCH * 128], BF16)


        P = Prog(b)
        P.emit("g", lambda e: e.dma_start(out=b.innw_rep[:, :],
                                          in_=b.innw_in.broadcast_to([128, DI])),
               inc=[("gp", 16)])
        P.emit("s", lambda e: e.dma_start(out=b.convinit_sb[:, :], in_=b.convinit_in),
               inc=[("ld", 16)])
        P.emit("v", lambda e: e.tensor_scalar(b.cm128i[:, :], b.cm128[:, :], -1.0, 1.0,
                                              ALU.mult, ALU.add))
        P.wait("g", "gp")
        P.wait("s", "ld")
        P.flush()

        b._emit_dtlhs()
        _m1_u_inproj(b, ms, src)
        _m2_prep(b, ms)
        with contextlib.ExitStack() as bs:
            b.big = bs.enter_context(nc.sbuf_tensor(
                b._nm("big"), [128, NKI * NCH * 128], BF16))
            b.hst = bs.enter_context(nc.sbuf_tensor(
                b._nm("hst"), [128, NH * PH], F32))
            _m3_conv(b, ms)
            _m4_pass1_exchange(b, ms)
            _m5_pass2_gate(b, ms)
        _m6_outproj(b, ms, src, dst)


def _m1_u_inproj(b, ms, src):
    nc = b.nc
    with contextlib.ExitStack() as fs:
        s = lambda n, sh, d: fs.enter_context(nc.sbuf_tensor(b._nm(n), sh, d))
        b.uh = s("uh", [128, ND * TOK], BF16)
        b.xh = None
        xt = s("xt", [128, ND * 512], F32)
        b.xsq = s("xsq", [128, ND * 512], BF16)
        b.tf32 = s("tf32", [128, 512], F32)
        b.rowa = s("rowa", [1, 512], F32)
        b.rowb = s("rowb", [1, 512], F32)
        b.rowc = s("rowc", [1, 512], F32)
        b.rowd = s("rowd", [1, 512], F32)
        wbuf = s("wbufm", [128, 2 * ND * 128], BF16)
        zst = s("zst", [128, 2 * 512], BF16)

        with contextlib.ExitStack() as pes:
            ps_a0 = b.psum(pes, "ps_a0", [128, 512])
            ps_a1 = b.psum(pes, "ps_a1", [128, 512])
            ps_sx = b.psum(pes, "ps_sx", [128, 512])
            ps_sq = b.psum(pes, "ps_sq", [128, 512])
            ps_r1 = b.psum(pes, "ps_r1", [128, 512])
            ps_r2 = b.psum(pes, "ps_r2", [128, 512])
            psb = (ps_sx, ps_sq, ps_r1, ps_r2)
            ps_mm = [ps_a0, ps_a1]

            P = Prog(b)
            for tau in range(4):
                P.wait("s", "mm")
                P.wait("s", "ev")
                for dc in range(ND):
                    P.emit("s", lambda e, dc=dc, tau=tau: e.dma_start(
                        out=xt[:, dc * 512:(dc + 1) * 512],
                        in_=src[dc * 128:(dc + 1) * 128, tau * 512:(tau + 1) * 512]),
                        inc=[("ld", 16)])
                P.wait("s", "ld")
                P.flush()
                P = Prog(b)
                xsrc = lambda dc: xt[:, dc * 512:(dc + 1) * 512]
                sink = lambda dc, tau=tau: b.uh[:, dc * TOK + tau * 512:
                                                dc * TOK + (tau + 1) * 512]
                b._ln_tile(P, xsrc, sink, tau * 512, psb, mode="mamba")
                P.flush()
                P = Prog(b)

            P.wait("t", "ev")
            P.wait("t", "ac")
            P.wait("g", "sc", b.prep_sc_done)
            zpar_cnt = [0, 0]
            zpar_ev = [0, 0]
            for oc in range(50):
                gci = b.wf_hist
                if gci >= 2:
                    P.wait("g", "mm", b.mm_after_col[gci - 2])
                b.wf_hist += 1
                buf = b.wf_hist % 2
                wsem = "w0" if buf == 0 else "w1"
                srcw = b.inprojT_bf.ap()[:, oc * 128:(oc + 1) * 128]
                P.emit("g", lambda e, srcw=srcw, buf=buf: e.dma_start(
                    out=wbuf[:, buf * ND * 128:(buf + 1) * ND * 128].rearrange(
                        "p (c n) -> p c n", n=128),
                    in_=srcw.rearrange("(c p) n -> p c n", p=128)), inc=[(wsem, 16)])
                P.wait("t", wsem)
                for tau in range(4):
                    ti = oc * 4 + tau
                    pp = ps_mm[ti % 2]
                    if ti >= 2:
                        kind_prev, cnt_prev = b.ev_free_m[ti - 2]
                        P.wait("t", kind_prev, cnt_prev)
                    for k in range(ND):
                        P.emit("t", lambda e, pp=pp, k=k, buf=buf, tau=tau: e.matmul(
                            pp[:, 0:512],
                            wbuf[:, buf * ND * 128 + k * 128:buf * ND * 128 + (k + 1) * 128],
                            b.uh[:, k * TOK + tau * 512:k * TOK + (tau + 1) * 512],
                            start=(k == 0), stop=(k == ND - 1)),
                            inc=([("mm", 1)] if k == ND - 1 else None))
                    mmv = b.cnt["mm"]
                    if oc < 48:
                        zp = ti % 2
                        zsem = "z0" if zp == 0 else "z1"
                        if zpar_cnt[zp] > 0:
                            P.wait("a", zsem, zpar_cnt[zp])
                        if zpar_ev[zp] > 0:
                            P.wait("a", "ev", zpar_ev[zp])
                        P.wait("a", "mm", mmv)
                        P.emit("a", lambda e, pp=pp, zp=zp: e.activation(
                            zst[:, zp * 512:(zp + 1) * 512], pp[:, 0:512], AF.Copy),
                            inc=[("ac", 1)])
                        b.ev_free_m[ti] = ("ac", b.cnt["ac"])
                        P.wait("s", "ac", b.cnt["ac"])
                        dstt = b.z_d.ap() if oc < 24 else b.xc_d.ap()
                        r0 = (oc % 24) * 128
                        P.emit("s", lambda e, dstt=dstt, r0=r0, tau=tau, zp=zp:
                               e.dma_start(out=dstt[r0:r0 + 128,
                                                    tau * 512:(tau + 1) * 512],
                                           in_=zst[:, zp * 512:(zp + 1) * 512]),
                               inc=[(zsem, 16)])
                        zpar_cnt[zp] = b.cnt[zsem]
                        if oc >= 24 and tau == 3:
                            P.wait("v", "ac", b.ev_free_m[ti][1])
                            P.emit("v", lambda e, oc=oc, zp=zp: e.tensor_copy(
                                b.halo[:, (oc - 24) * 3:(oc - 23) * 3],
                                zst[:, zp * 512 + 509:zp * 512 + 512]), inc=[("ev", 1)])
                            zpar_ev[zp] = b.cnt["ev"]
                            if oc == 47:
                                b.halo_ev = b.cnt["ev"]
                    elif oc == 48:
                        P.wait("v", "mm", mmv)
                        P.emit("v", lambda e, pp=pp, tau=tau: e.tensor_copy(
                            b.b_bf[:, tau * 512:(tau + 1) * 512], pp[:, 0:512]),
                            inc=[("ev", 1)])
                        b.ev_free_m[ti] = ("ev", b.cnt["ev"])
                    else:
                        P.wait("v", "mm", mmv)
                        P.emit("v", lambda e, pp=pp, tau=tau: e.tensor_copy(
                            b.c_bf[:, tau * 512:(tau + 1) * 512], pp[:, 0:512]),
                            inc=[("ev", 1)])
                        b.ev_free_m[ti] = ("ev", b.cnt["ev"])
                b.mm_after_col.append(b.cnt["mm"])
            P.wait("s", "ev", b.halo_ev)
            P.emit("s", lambda e: e.dma_start(out=b.cc_halo_in.ap(), in_=b.halo[:, :]),
                   inc=[("st", 16)])
            P.wait("g", "st", b.cnt["st"])
            P.emit("g", lambda e: e.collective_compute(
                "AllGather", ALU.bypass, replica_groups=[list(range(NCORES))],
                ins=[b.cc_halo_in.ap().opt()], outs=[b.cc_halo_out.ap().opt()]),
                inc=[("cc", 1)])
            for tau in range(4):
                ti = 200 + tau
                pp = ps_mm[tau % 2]
                kind_prev, cnt_prev = b.ev_free_m[198 + tau if tau < 2 else 200 + tau - 2]
                P.wait("t", kind_prev, cnt_prev)
                for k in range(ND):
                    P.emit("t", lambda e, pp=pp, k=k, tau=tau: e.matmul(
                        pp[0:NH, 0:512], b.dtlhs[:, k * NH:(k + 1) * NH],
                        b.uh[:, k * TOK + tau * 512:k * TOK + (tau + 1) * 512],
                        start=(k == 0), stop=(k == ND - 1)),
                        inc=([("mm", 1)] if k == ND - 1 else None))
                P.wait("v", "mm")
                P.emit("v", lambda e, pp=pp, tau=tau: e.tensor_copy(
                    b.dtraw[:, tau * 512:(tau + 1) * 512], pp[0:NH, 0:512]),
                    inc=[("ev", 1)])
                b.ev_free_m[200 + tau] = ("ev", b.cnt["ev"])
            P.flush()


def _m2_prep(b, ms):
    nc = b.nc
    with contextlib.ExitStack() as fs:
        s = lambda n, sh, d: fs.enter_context(nc.sbuf_tensor(b._nm(n), sh, d))
        la = s("la", [NH, TOK], F32)
        afm = s("afm", [NH, TOK], F32)
        cum = s("cum", [NH, TOK], F32)
        sc48 = s("sc48", [NH, TOK], F32)
        sc48b = s("sc48b", [NH, TOK], F32)
        agh = s("agh", [128, NCORES * NKI * 3], BF16)

        P = Prog(b)
        P.emit("a", lambda e: e.activation(sc48[:, :], b.dtraw[:, :], AF.Abs,
                                           bias=b.dtb[:, :], scale=1.0))
        P.emit("a", lambda e: e.activation(sc48[:, :], sc48[:, :], AF.Exp, scale=-1.0))
        P.emit("a", lambda e: e.activation(sc48[:, :], sc48[:, :], AF.Ln, bias=1.0),
               inc=[("ac", 1)])
        P.emit("a", lambda e: e.activation(b.acol[:, :], b.acol[:, :], AF.Exp),
               inc=[("ac", 1)])
        P.wait("v", "ac")
        P.emit("v", lambda e: e.tensor_scalar(sc48b[:, :], b.dtraw[:, :], b.dtb[:, :],
                                              0.0, ALU.add, ALU.max), inc=[("ev", 1)])
        P.emit("v", lambda e: e.tensor_add(sc48b[:, :], sc48b[:, :], sc48[:, :]),
               inc=[("ev", 1)])
        P.emit("v", lambda e: e.tensor_scalar(la[:, :], sc48b[:, :], b.acol[:, :],
                                              -1.0, ALU.mult, ALU.mult), inc=[("ev", 1)])
        P.wait("a", "ev")
        P.emit("a", lambda e: e.activation(afm[:, :], la[:, :], AF.Exp), inc=[("ac", 1)])
        P.wait("v", "ac")
        for c in range(NCH):
            sl = slice(c * 128, (c + 1) * 128)
            P.emit("v", lambda e, sl=sl: e.tensor_tensor_scan(
                cum[:, sl], la[:, sl], la[:, sl], 0.0, ALU.add, ALU.bypass))
            P.emit("v", lambda e, c=c, sl=sl: e.reduce_sum(
                b.tot[:, c:c + 1], la[:, sl], axis=AX.X), inc=[("ev", 1)])
            P.wait("v", "ev")
        P.emit("v", lambda e: e.reduce_sum(b.tot[:, NCH:NCH + 1], la[:, :], axis=AX.X),
               inc=[("ev", 1)])
        P.wait("v", "ev")
        P.wait("a", "ev")
        for c in range(NCH):
            sl = slice(c * 128, (c + 1) * 128)
            P.emit("a", lambda e, c=c, sl=sl: e.activation(
                sc48[:, sl], cum[:, sl], AF.Exp, bias=b.tot[:, c:c + 1], scale=-1.0))
        P.emit("a", lambda e: e.activation(sc48b[:, 0:NCH + 1], b.tot[:, 0:NCH + 1],
                                           AF.Exp), inc=[("ac", 1)])
        P.wait("s", "ac")
        P.emit("s", lambda e: e.dma_start(
            out=b.a_d.ap().rearrange("o (hh t) -> (o hh) t", t=TOK), in_=afm[:, :]),
               inc=[("st", 16)])
        P.emit("s", lambda e: e.dma_start(
            out=b.bounce.ap()[0:1, 0:NH * (NCH + 1)].rearrange(
                "o (hh c) -> (o hh) c", c=NCH + 1),
            in_=sc48b[:, 0:NCH + 1]), inc=[("st", 16)])
        P.wait("s", "st")
        for c in range(NCH + 1):
            P.emit("s", lambda e, c=c: e.dma_start(
                out=b.prow[0:1, c * NH:(c + 1) * NH],
                in_=b.bounce.ap()[0:1, 0:NH * (NCH + 1)].rearrange(
                    "o (hh c) -> o c hh", c=NCH + 1)[:, c, :],
                allow_slow_non_contiguous=True), inc=[("ld", 16)])
        P.wait("s", "ld")
        P.wait("s", "cc", 1)
        P.emit("s", lambda e: e.dma_start(
            out=agh[:, :].rearrange("p (d t) -> p d t", t=NKI * 3),
            in_=b.cc_halo_out.ap().rearrange("(d p) t -> p d t", p=128)),
            inc=[("ld", 16)])
        b.agh_ld = b.cnt["ld"]
        P.flush()

        with contextlib.ExitStack() as pes:
            ps_t = b.psum(pes, "ps_t", [128, 512])
            ps_tb = b.psum(pes, "ps_tb", [128, 1024], BF16)
            items = []
            for c in range(NCH):
                items.append((
                    (lambda t, c=c: t.transpose(ps_t[:, 0:NH],
                                                sc48[0:NH, c * 128:(c + 1) * 128],
                                                b.idf[0:NH, 0:NH])),
                    (lambda v, c=c: v.tensor_copy(b.deT[:, c * NH:(c + 1) * NH],
                                                  ps_t[:, 0:NH])),
                ))
            for c in range(NCH + 1):
                items.append((
                    (lambda t, c=c: t.matmul(ps_t[:, 0:NH], b.ones_rf[:, :],
                                             b.prow[0:1, c * NH:(c + 1) * NH],
                                             start=True, stop=True)),
                    (lambda v, c=c: v.tensor_copy(b.pcols[:, c * NH:(c + 1) * NH],
                                                  ps_t[:, 0:NH])),
                ))
            for c in range(NCH):
                items.append((
                    (lambda t, c=c: t.transpose(ps_tb[:, 0:128],
                                                b.b_bf[:, c * 128:(c + 1) * 128],
                                                b.idb[:, :])),
                    (lambda v, c=c: v.tensor_copy(b.bT[:, c * 128:(c + 1) * 128],
                                                  ps_tb[:, 0:128])),
                ))
            P = Prog(b)
            pe_b, dv_b = b.cnt["mm"], b.cnt["ev"]
            n = len(items)
            for i, (epe, _edv) in enumerate(items):
                if i > 0:
                    P.wait("t", "ev", dv_b + i)
                P.emit("t", epe, inc=[("mm", 1)])
            for i, (_epe, edv) in enumerate(items):
                P.wait("v", "mm", pe_b + i + 1)
                P.emit("v", edv, inc=[("ev", 1)])
            P.emit("g", lambda e: e.memset(b.runall[:, (NCH - 1) * NH:NCH * NH], 1.0))
            P.flush()
            P = Prog(b)
            for c in range(NCH - 2, -1, -1):
                P.emit("v", lambda e, c=c: e.tensor_mul(
                    b.runall[:, c * NH:(c + 1) * NH],
                    b.runall[:, (c + 1) * NH:(c + 2) * NH],
                    b.pcols[:, (c + 1) * NH:(c + 2) * NH]), inc=[("ev", 1)])
                P.wait("v", "ev")
            P.wait("v", "ld", b.agh_ld)
            W = NKI * 3
            P.emit("v", lambda e: e.tensor_scalar(
                b.halo[:, :], agh[:, 0:W], b.hmask_sb[:, 0:1], None, ALU.mult),
                inc=[("ev", 1)])
            for d in range(1, NCORES):
                P.wait("v", "ev")
                P.emit("v", lambda e, d=d: e.scalar_tensor_tensor(
                    b.halo[:, :], agh[:, d * W:(d + 1) * W], b.hmask_sb[:, d:d + 1],
                    b.halo[:, :], ALU.mult, ALU.add), inc=[("ev", 1)])
            P.wait("v", "ev")
            P.emit("v", lambda e: e.tensor_add(b.halo[:, :], b.halo[:, :],
                                               b.convinit_sb[:, :]), inc=[("ev", 1)])
            P.flush()


def _m3_conv(b, ms):
    nc = b.nc
    XW = 3 + TOK
    with contextlib.ExitStack() as fs:
        s = lambda n, sh, d: fs.enter_context(nc.sbuf_tensor(b._nm(n), sh, d))
        xck = s("xck", [128, 2 * XW], BF16)
        convt = s("convt", [128, 2 * TOK], BF16)
        with contextlib.ExitStack() as pes:
            ps_c = b.psum(pes, "ps_c", [128, 512])
            ps_tb = b.psum(pes, "ps_tb", [128, 1024], BF16)
            P = Prog(b)
            conv_done = {}
            tr_mm = {}
            for k in range(NKI):
                par = k % 2
                x0 = par * XW
                c0 = par * TOK
                if k >= 2:
                    P.wait("s", "ev", conv_done[k - 2])
                P.emit("s", lambda e, k=k, x0=x0: e.dma_start(
                    out=xck[:, x0 + 3:x0 + 3 + TOK],
                    in_=b.xc_d.ap()[k * 128:(k + 1) * 128, :]), inc=[("ld", 16)])
                ldv = b.cnt["ld"]
                P.wait("v", "ld", ldv)
                P.emit("v", lambda e, k=k, x0=x0: e.tensor_copy(
                    xck[:, x0:x0 + 3], b.halo[:, k * 3:(k + 1) * 3]), inc=[("ev", 1)])
                P.wait("v", "ev")
                if k >= 2:
                    P.wait("a", "mm", tr_mm[k - 2])
                for tc in range(4):
                    o = x0 + tc * 512
                    P.emit("v", lambda e, o=o, k=k: e.tensor_scalar(
                        ps_c[:, 0:512], xck[:, o:o + 512],
                        b.convw_sb[:, k * 4:k * 4 + 1], b.convb_sb[:, k:k + 1],
                        ALU.mult, ALU.add))
                    for kk in range(1, 4):
                        P.emit("v", lambda e, o=o, k=k, kk=kk: e.scalar_tensor_tensor(
                            ps_c[:, 0:512], xck[:, o + kk:o + kk + 512],
                            b.convw_sb[:, k * 4 + kk:k * 4 + kk + 1],
                            ps_c[:, 0:512], ALU.mult, ALU.add))
                    P.emit("v", lambda e: e.engine_nop(), inc=[("ev", 1)])
                    P.wait("a", "ev")
                    P.emit("a", lambda e, c0=c0, tc=tc: e.activation(
                        convt[:, c0 + tc * 512:c0 + (tc + 1) * 512], ps_c[:, 0:512],
                        AF.Silu), inc=[("ac", 1)])
                    P.wait("v", "ac")
                conv_done[k] = b.cnt["ev"]
                for c in range(NCH):
                    P.wait("t", "ac")
                    P.wait("t", "ev")
                    P.emit("t", lambda e, c=c, c0=c0: e.transpose(
                        ps_tb[:, 0:128], convt[:, c0 + c * 128:c0 + (c + 1) * 128],
                        b.idb[:, :]), inc=[("mm", 1)])
                    P.wait("v", "mm")
                    P.emit("v", lambda e, k=k, c=c: e.tensor_copy(
                        b.big[:, k * TOK + c * 128:k * TOK + (c + 1) * 128],
                        ps_tb[:, 0:128]), inc=[("ev", 1)])
                tr_mm[k] = b.cnt["mm"]
            P.flush()


def xslice(b, hh, c):
    k, half = divmod(hh, 2)
    off = k * TOK + c * 128 + half * 64
    return b.big[:, off:off + 64]


def _m4_pass1_exchange(b, ms):
    nc = b.nc
    with contextlib.ExitStack() as fs:
        s = lambda n, sh, d: fs.enter_context(nc.sbuf_tensor(b._nm(n), sh, d))
        bhat = s("bhat", [128, 2 * 128], BF16)
        sstage = s("sstage", [128, NH * PH // 2], F32)
        with contextlib.ExitStack() as pes:
            ps_sc = b.psum(pes, "ps_sc", [128, NH * PH])
            P = Prog(b)
            pe_b, dv_b = b.cnt["mm"], b.cnt["ev"]
            i = 0
            for hh in range(NH):
                for ci, c in enumerate(reversed(range(NCH))):
                    P.wait("t", "ev", dv_b + i + 1)
                    P.emit("t", lambda e, hh=hh, c=c, ci=ci, i=i: e.matmul(
                        ps_sc[:, hh * PH:(hh + 1) * PH],
                        bhat[:, (i % 2) * 128:(i % 2) * 128 + 128],
                        xslice(b, hh, c),
                        start=(ci == 0), stop=(ci == NCH - 1)), inc=[("mm", 1)])
                    i += 1
            i = 0
            for hh in range(NH):
                for ci, c in enumerate(reversed(range(NCH))):
                    if i >= 2:
                        P.wait("v", "mm", pe_b + i - 1)
                    P.emit("v", lambda e, hh=hh, c=c, i=i: e.tensor_scalar(
                        bhat[:, (i % 2) * 128:(i % 2) * 128 + 128],
                        b.bT[:, c * 128:(c + 1) * 128],
                        b.deT[:, c * NH + hh:c * NH + hh + 1],
                        b.runall[:, c * NH + hh:c * NH + hh + 1],
                        ALU.mult, ALU.mult), inc=[("ev", 1)])
                    i += 1
            P.flush()
            P = Prog(b)
            P.emit("g", lambda e: e.memset(sstage[0:1, :], 0.0))
            P.emit("g", lambda e: e.dma_start(
                out=b.cc_sum_in.ap()[128:129, 0:1536], in_=sstage[0:1, :]),
                inc=[("gp", 16)])
            P.emit("g", lambda e: e.dma_start(
                out=b.cc_sum_in.ap()[128:129, 1536:3072], in_=sstage[0:1, :]),
                inc=[("gp", 16)])
            P.wait("g", "gp")
            P.flush()
            P = Prog(b)
            P.wait("v", "mm")
            for hf in range(2):
                if hf == 1:
                    P.wait("v", "st")
                P.emit("v", lambda e, hf=hf: e.tensor_copy(
                    sstage[:, :], ps_sc[:, hf * 1536:(hf + 1) * 1536]), inc=[("ev", 1)])
                P.wait("s", "ev")
                P.emit("s", lambda e, hf=hf: e.dma_start(
                    out=b.cc_sum_in.ap()[0:128, hf * 1536:(hf + 1) * 1536],
                    in_=sstage[:, :]), inc=[("st", 16)])
                P.wait("s", "st")
            P.emit("s", lambda e: e.dma_start(
                out=b.cc_sum_in.ap()[128:129, :].rearrange("o (a hh) -> o a hh", hh=NH)
                if False else b.cc_sum_in.ap()[128:129, 0:NH],
                in_=b.prow[0:1, NCH * NH:(NCH + 1) * NH]), inc=[("st", 16)])
            P.wait("s", "st")
            P.flush()

        P = Prog(b)
        P.emit("g", lambda e: e.collective_compute(
            "AllGather", ALU.bypass, replica_groups=[list(range(NCORES))],
            ins=[b.cc_sum_in.ap().opt()], outs=[b.cc_sum_out.ap().opt()]),
            inc=[("cc", 1)])
        P.flush()

        with contextlib.ExitStack() as fs:
            s = lambda n, sh, d: fs.enter_context(nc.sbuf_tensor(b._nm(n), sh, d))
            S_sb = s("S_sb", [128, 2 * NH * PH], F32)
            pd_row = s("pd_row", [1, 2 * NH], F32)
            acols = s("acols", [128, NH], F32)
            with contextlib.ExitStack() as pes:
                ps_pc = b.psum(pes, "ps_pc", [128, 512])
                P = Prog(b)
                P.emit("s", lambda e: e.dma_start(out=b.hst[:, :], in_=b.h0_in),
                       inc=[("ld", 16)])
                ld_val = {}
                stt_done = {}
                mm_prev = {}
                for d in range(NCORES):
                    par = d % 2
                    ssl = slice(par * NH * PH, (par + 1) * NH * PH)
                    if d == 0:
                        P.wait("s", "cc", 2)
                    if d >= 2:
                        P.wait("s", "ev", stt_done[d - 2])
                    P.emit("s", lambda e, d=d, par=par: e.dma_start(
                        out=pd_row[0:1, par * NH:(par + 1) * NH],
                        in_=b.cc_sum_out.ap()[d * 129 + 128:d * 129 + 129, 0:NH]),
                        inc=[("ld", 16)])
                    P.emit("s", lambda e, d=d, ssl=ssl: e.dma_start(
                        out=S_sb[:, ssl],
                        in_=b.cc_sum_out.ap()[d * 129:d * 129 + 128, :]),
                        inc=[("ld", 16)])
                    ld_val[d] = b.cnt["ld"]
                    P.wait("t", "ld", ld_val[d])
                    if d >= 1:
                        P.wait("t", "ev", mm_prev[d - 1])
                    P.emit("t", lambda e, par=par: e.matmul(
                        ps_pc[:, 0:NH], b.ones_rf[:, :],
                        pd_row[0:1, par * NH:(par + 1) * NH],
                        start=True, stop=True), inc=[("mm", 1)])
                    P.wait("v", "mm")
                    P.emit("v", lambda e, d=d: e.tensor_scalar(
                        acols[:, :], ps_pc[:, 0:NH], b.cm128[:, d:d + 1],
                        None, ALU.mult), inc=[("ev", 1)])
                    mm_prev[d] = b.cnt["ev"]
                    P.wait("v", "ev")
                    P.emit("v", lambda e, d=d: e.tensor_scalar(
                        acols[:, :], acols[:, :], b.cm128i[:, d:d + 1],
                        None, ALU.add), inc=[("ev", 1)])
                    P.wait("v", "ev")
                    P.emit("v", lambda e, d=d, ssl=ssl: e.tensor_scalar(
                        S_sb[:, ssl], S_sb[:, ssl], b.cm128[:, d:d + 1], None,
                        ALU.mult), inc=[("ev", 1)])
                    P.wait("v", "ev")
                    for hh in range(NH):
                        P.emit("v", lambda e, hh=hh, par=par: e.scalar_tensor_tensor(
                            b.hst[:, hh * PH:(hh + 1) * PH],
                            b.hst[:, hh * PH:(hh + 1) * PH],
                            acols[:, hh:hh + 1],
                            S_sb[:, par * NH * PH + hh * PH:
                                 par * NH * PH + (hh + 1) * PH],
                            ALU.mult, ALU.add),
                            inc=[("ev", 1)])
                    stt_done[d] = b.cnt["ev"]
                P.flush()


def _m5_pass2_gate(b, ms):
    nc = b.nc
    with contextlib.ExitStack() as fs:
        s = lambda n, sh, d: fs.enter_context(nc.sbuf_tensor(b._nm(n), sh, d))
        mask_sb = s("mask", [128, 128], BF16)
        coef_sb = s("coef", [128, 128], F32)
        gm_sb = s("gm", [128, 128], BF16)
        mc_sb = s("mc", [128, 128], F32)
        bhat_sb = s("bhat2", [128, 128], BF16)
        ytm = s("ytm", [128, DI], BF16)
        yz = s("yz", [128, DI], BF16)
        ygst = s("ygst", [128, DI], BF16)
        arow = s("arow", [1, NH * 128 // 2], BF16)
        ycol = s("ycol", [128, 8], F32)
        with contextlib.ExitStack() as pes:
            ps_a = b.psum(pes, "ps_a", [128, 512])
            ps_g = b.psum(pes, "ps_g", [128, 512])
            ps_y = b.psum(pes, "ps_y", [128, 512])
            ps_s = b.psum(pes, "ps_s", [128, 512])
            ygpar = [0, 0]
            for c in range(NCH):
                P = Prog(b)
                P.emit("g", lambda e, c=c: e.dma_start(
                    out=arow[0:1, :].rearrange("o (hh j) -> o hh j", j=128),
                    in_=b.a_d.ap().rearrange("o (hh t) -> o hh t", t=TOK)[
                        :, 0:24, c * 128:(c + 1) * 128]), inc=[("gp", 16)])
                P.wait("g", "gp")
                P.flush()

                pe0, dv0 = b.cnt["mm"], b.cnt["ev"]
                P = Prog(b)
                P.emit("t", lambda e, c=c: e.matmul(
                    ps_g[:, 0:128], b.b_bf[:, c * 128:(c + 1) * 128],
                    b.c_bf[:, c * 128:(c + 1) * 128], start=True, stop=True),
                    inc=[("mm", 1)])
                for hh in range(NH):
                    if hh > 0:
                        P.wait("t", "ev", dv0 + 7 * (hh - 1) + 2)
                    if hh == 24:
                        P.wait("g", "mm", pe0 + 1 + 3 * 23 + 1)
                        P.emit("g", lambda e, c=c: e.dma_start(
                            out=arow[0:1, :].rearrange("o (hh j) -> o hh j", j=128),
                            in_=b.a_d.ap().rearrange("o (hh t) -> o hh t", t=TOK)[
                                :, 24:48, c * 128:(c + 1) * 128]), inc=[("gp", 16)])
                        P.wait("t", "gp")
                    P.emit("t", lambda e, hh=hh: e.matmul(
                        ps_a[:, 0:128], b.ones_rb[:, :],
                        arow[0:1, (hh % 24) * 128:(hh % 24 + 1) * 128],
                        start=True, stop=True), inc=[("mm", 1)])
                    P.wait("t", "ev", dv0 + 7 * hh + 4)
                    P.emit("t", lambda e, hh=hh, c=c: e.matmul(
                        ps_y[:, 0:PH], gm_sb[:, :], xslice(b, hh, c),
                        start=True, stop=False))
                    P.emit("t", lambda e, hh=hh: e.matmul(
                        ps_y[:, 0:PH], mc_sb[:, :], b.hst[:, hh * PH:(hh + 1) * PH],
                        start=False, stop=True), inc=[("mm", 1)])
                    P.wait("t", "ev", dv0 + 7 * hh + 5)
                    P.emit("t", lambda e, hh=hh, c=c: e.matmul(
                        ps_s[:, 0:PH], bhat_sb[:, :], xslice(b, hh, c),
                        start=True, stop=True), inc=[("mm", 1)])

                for hh in range(NH):
                    P.wait("v", "mm", pe0 + 1 + 3 * hh + 1)
                    P.emit("v", lambda e: e.tensor_tensor_scan(
                        mask_sb[:, :], ps_a[:, 0:128], b.idf[:, :], 0.0,
                        ALU.mult, ALU.add), inc=[("ev", 1)])
                    P.emit("v", lambda e: e.tensor_tensor_scan(
                        coef_sb[:, :], ps_a[:, 0:128], b.zero128[:, :], 1.0,
                        ALU.mult, ALU.add), inc=[("ev", 1)])
                    P.emit("v", lambda e: e.tensor_mul(gm_sb[:, :], ps_g[:, 0:128],
                                                       mask_sb[:, :]), inc=[("ev", 1)])
                    P.emit("v", lambda e, c=c: e.tensor_mul(
                        mc_sb[:, :], b.c_bf[:, c * 128:(c + 1) * 128], coef_sb[:, :]),
                        inc=[("ev", 1)])
                    P.emit("v", lambda e, c=c, hh=hh: e.tensor_scalar(
                        bhat_sb[:, :], b.bT[:, c * 128:(c + 1) * 128],
                        b.deT[:, c * NH + hh:c * NH + hh + 1], None, ALU.mult),
                        inc=[("ev", 1)])
                    P.wait("v", "mm", pe0 + 1 + 3 * hh + 2)
                    P.emit("v", lambda e, hh=hh, c=c: e.scalar_tensor_tensor(
                        ytm[:, hh * PH:(hh + 1) * PH], xslice(b, hh, c),
                        b.dcols[:, hh:hh + 1], ps_y[:, 0:PH], ALU.mult, ALU.add),
                        inc=[("ev", 1)])
                    P.wait("v", "mm", pe0 + 1 + 3 * hh + 3)
                    P.emit("v", lambda e, hh=hh: e.scalar_tensor_tensor(
                        b.hst[:, hh * PH:(hh + 1) * PH], b.hst[:, hh * PH:(hh + 1) * PH],
                        coef_sb[:, 127:128], ps_s[:, 0:PH], ALU.mult, ALU.add),
                        inc=[("ev", 1)])
                P.flush()

                P = Prog(b)
                P.wait("a", "ev")
                P.emit("a", lambda e: e.activation(yz[:, :], ytm[:, :], AF.Square),
                       inc=[("ac", 1)])
                P.wait("v", "ac")
                P.emit("v", lambda e: e.reduce_sum(ycol[:, 0:1], yz[:, :], axis=AX.X),
                       inc=[("ev", 1)])
                P.wait("v", "ev")
                P.emit("v", lambda e: e.tensor_scalar(ycol[:, 1:2], ycol[:, 0:1],
                                                      1.0 / DI, None, ALU.mult),
                       inc=[("ev", 1)])
                P.wait("a", "ev")
                P.emit("a", lambda e: e.activation(ycol[:, 2:3], ycol[:, 1:2], AF.Ln,
                                                   bias=b.epsc[:, :]), inc=[("ac", 1)])
                P.wait("a", "ac")
                P.emit("a", lambda e: e.activation(ycol[:, 3:4], ycol[:, 2:3], AF.Exp,
                                                   scale=-0.5), inc=[("ac", 1)])
                P.wait("a", "ev")
                P.emit("a", lambda e, c=c: e.dma_start_transpose(
                    yz[:, :], b.z_d.ap()[:, c * 128:(c + 1) * 128]), inc=[("sc", 16)])
                P.wait("a", "sc")
                P.emit("a", lambda e: e.activation(yz[:, :], yz[:, :], AF.Silu),
                       inc=[("ac", 1)])
                P.wait("v", "ac")
                P.emit("v", lambda e: e.tensor_mul(yz[:, :], yz[:, :], b.innw_rep[:, :]),
                       inc=[("ev", 1)])
                if ygpar[0] > 0:
                    P.wait("v", "z0", ygpar[0])
                P.emit("v", lambda e: e.scalar_tensor_tensor(
                    ygst[:, :], ytm[:, :], ycol[:, 3:4],
                    yz[:, :], ALU.mult, ALU.mult), inc=[("ev", 1)])
                P.wait("s", "ev")
                P.emit("s", lambda e, c=c: e.dma_start(
                    out=b.yg_d.ap()[c * 128:(c + 1) * 128, :],
                    in_=ygst[:, :]), inc=[("z0", 16)])
                ygpar[0] = b.cnt["z0"]
                P.flush()
        P = Prog(b)
        P.wait("s", "z0", b.cnt["z0"])
        P.wait("s", "z1", b.cnt["z1"])
        P.flush()


def _m6_outproj(b, ms, src, dst):
    nc = b.nc
    with contextlib.ExitStack() as fs:
        s = lambda n, sh, d: fs.enter_context(nc.sbuf_tensor(b._nm(n), sh, d))
        ygr = s("ygr", [128, 2 * NKI * 512], BF16)
        wbuf = s("wbufo", [128, 2 * NKI * 128], BF16)
        x1t = s("x1t", [128, 2 * 512], F32)
        x2t = s("x2t", [128, 2 * 512], F32)
        reps = s("reps", [128, 2 * 512], F32)
        rw1 = s("rw1", [1, 512], F32)
        rw2 = s("rw2", [1, 512], F32)
        with contextlib.ExitStack() as pes:
            ps_m0 = b.psum(pes, "ps_m0", [128, 512])
            ps_m1 = b.psum(pes, "ps_m1", [128, 512])
            ps_r = b.psum(pes, "ps_r", [128, 1024])
            x1par = [0, 0]
            x2par = [0, 0]
            xb_done = {}

            def pump_xbars(P, tau):
                slot = (tau % 2) * NKI * 512
                for k in range(NKI):
                    P.emit("a", lambda e, k=k, tau=tau, slot=slot:
                           e.dma_start_transpose(
                        ygr[:, slot + k * 512:slot + (k + 1) * 512],
                        b.yg_d.ap()[tau * 512:(tau + 1) * 512, k * 128:(k + 1) * 128]),
                        inc=[("sc", 16)])
                xb_done[tau] = b.cnt["sc"]

            P = Prog(b)
            pump_xbars(P, 0)
            P.flush()
            for tau in range(4):
                P = Prog(b)
                if tau < 3:
                    pump_xbars(P, tau + 1)
                P.emit("v", lambda e, tau=tau: e.tensor_scalar(
                    rw1[0:1, :], b.srow[0:1, tau * 512:(tau + 1) * 512], 1.0, None,
                    ALU.add), inc=[("ev", 1)])
                P.emit("v", lambda e, tau=tau: e.tensor_mul(
                    rw2[0:1, :], b.srow[0:1, tau * 512:(tau + 1) * 512],
                    b.mrow[0:1, tau * 512:(tau + 1) * 512]), inc=[("ev", 1)])
                P.wait("t", "ev")
                P.emit("t", lambda e: e.matmul(ps_r[:, 0:512], b.ones_rf[:, :],
                                               rw1[0:1, :], start=True, stop=True))
                P.emit("t", lambda e: e.matmul(ps_r[:, 512:1024], b.ones_rf[:, :],
                                               rw2[0:1, :], start=True, stop=True),
                       inc=[("mm", 1)])
                P.wait("v", "mm")
                P.emit("v", lambda e: e.tensor_copy(reps[:, 0:512], ps_r[:, 0:512]),
                       inc=[("ev", 1)])
                P.emit("v", lambda e: e.tensor_copy(reps[:, 512:1024],
                                                    ps_r[:, 512:1024]), inc=[("ev", 1)])
                P.wait("t", "sc", xb_done[tau])
                slot = (tau % 2) * NKI * 512
                ps_mm = [ps_m0, ps_m1]
                for dc in range(ND):
                    gci = b.wf_hist
                    if gci >= 2:
                        P.wait("g", "mm", b.mm_after_col[gci - 2])
                    b.wf_hist += 1
                    buf = b.wf_hist % 2
                    wsem = "w0" if buf == 0 else "w1"
                    P.emit("g", lambda e, dc=dc, buf=buf: e.dma_start(
                        out=wbuf[:, buf * NKI * 128:(buf + 1) * NKI * 128].rearrange(
                            "p (c n) -> p c n", n=128),
                        in_=b.outprojT_bf.ap()[:, dc * 128:(dc + 1) * 128].rearrange(
                            "(c p) n -> p c n", p=128)), inc=[(wsem, 16)])
                    xp = dc % 2
                    xsem = "z0" if xp == 0 else "z1"
                    if x1par[xp] > 0:
                        P.wait("s", "ev", x1par[xp])
                    P.emit("s", lambda e, dc=dc, tau=tau, xp=xp: e.dma_start(
                        out=x1t[:, xp * 512:(xp + 1) * 512],
                        in_=src[dc * 128:(dc + 1) * 128, tau * 512:(tau + 1) * 512]),
                        inc=[(xsem, 16)])
                    P.wait("t", wsem)
                    ti = tau * ND + dc
                    pp = ps_mm[ti % 2]
                    if ti >= 2:
                        P.wait("t", "ev", b.ev_free_o[ti - 2])
                    for k in range(NKI):
                        P.emit("t", lambda e, pp=pp, k=k, buf=buf, slot=slot: e.matmul(
                            pp[:, 0:512],
                            wbuf[:, buf * NKI * 128 + k * 128:buf * NKI * 128 + (k + 1) * 128],
                            ygr[:, slot + k * 512:slot + (k + 1) * 512],
                            start=(k == 0), stop=(k == NKI - 1)),
                            inc=([("mm", 1)] if k == NKI - 1 else None))
                    b.mm_after_col.append(b.cnt["mm"])
                    yp = ti % 2
                    ysem = "y0" if yp == 0 else "y1"
                    x2s = slice(yp * 512, (yp + 1) * 512)
                    P.wait("v", "mm")
                    P.wait("v", xsem)
                    if x2par[yp] > 0:
                        P.wait("v", ysem, x2par[yp])
                    P.emit("v", lambda e, xp=xp, x2s=x2s: e.tensor_mul(
                        x2t[:, x2s], x1t[:, xp * 512:(xp + 1) * 512], reps[:, 0:512]))
                    P.emit("v", lambda e, x2s=x2s: e.tensor_sub(x2t[:, x2s], x2t[:, x2s],
                                                                reps[:, 512:1024]))
                    P.emit("v", lambda e, pp=pp, x2s=x2s: e.tensor_add(
                        x2t[:, x2s], x2t[:, x2s], pp[:, 0:512]), inc=[("ev", 1)])
                    b.ev_free_o[ti] = b.cnt["ev"]
                    x1par[xp] = b.cnt["ev"]
                    P.wait("s", "ev")
                    P.emit("s", lambda e, dc=dc, tau=tau, x2s=x2s: e.dma_start(
                        out=dst[dc * 128:(dc + 1) * 128, tau * 512:(tau + 1) * 512],
                        in_=x2t[:, x2s]), inc=[(ysem, 16)])
                    x2par[yp] = b.cnt[ysem]
                P.flush()


def make_inputs(inp, core):
    t0 = core * TOK
    x = np.asarray(inp["L0_attn_x"]).reshape(NCORES * TOK, D)[t0:t0 + TOK]
    bf = lambda a: np.ascontiguousarray(a)
    eye = np.eye(128, dtype=np.float32)
    oh = np.zeros((E, E * 128), dtype=np.float32)
    for e in range(E):
        oh[e, e * 128:(e + 1) * 128] = 1.0
    chain_start = core % 4 == 0
    prev = core - 1
    hmask = np.zeros((128, E), dtype=np.float32)
    if not chain_start:
        hmask[:, prev] = 1.0
    cm128 = np.zeros((128, E), dtype=np.float32)
    for d in range(E):
        if d // 4 == core // 4 and d < core:
            cm128[:, d] = 1.0
    conv_state = np.asarray(inp["L0_attn_conv_state"])[0]
    conv_init = (conv_state if chain_start else np.zeros_like(conv_state))
    ssm = np.asarray(inp["L0_attn_ssm_state"])[0]
    m = {
        "x_fm": np.ascontiguousarray(x.T),
        "wg_s": bf(inp["L0_ffn_Wg_s"]), "wu_s": bf(inp["L0_ffn_Wu_s"]),
        "wd_s": bf(inp["L0_ffn_Wd_s"]),
        "wg_e": bf(inp["L0_ffn_Wg"]), "wu_e": bf(inp["L0_ffn_Wu"]),
        "wd_e": bf(inp["L0_ffn_Wd"]),
        "wr": bf(inp["L0_ffn_Wr"]),
        "br_rb": np.ascontiguousarray(
            np.stack([inp["L0_ffn_br"], inp["L0_ffn_rb"]], axis=1)),
        "inproj": bf(inp["L0_attn_in_proj_w"]),
        "outproj": bf(inp["L0_attn_out_proj_w"]),
        "convw": np.ascontiguousarray(
            np.asarray(inp["L0_attn_conv_w"]).reshape(NKI, 128, 4)
            .transpose(1, 0, 2).reshape(128, NKI * 4)),
        "convb": np.ascontiguousarray(
            np.asarray(inp["L0_attn_conv_b"]).reshape(NKI, 128).T),
        "dtbias": np.asarray(inp["L0_attn_dt_bias"]).reshape(NH, 1),
        "alog": np.asarray(inp["L0_attn_A_log"]).reshape(NH, 1),
        "drow": np.asarray(inp["L0_attn_D_param"]).reshape(1, NH),
        "normw": np.ascontiguousarray(
            np.asarray(inp["L0_attn_norm_w"]).reshape(ND, 128).T),
        "innw": np.asarray(inp["L0_attn_inner_norm_w"]).reshape(1, DI),
        "h0_in": np.ascontiguousarray(ssm.transpose(1, 0, 2).reshape(128, NH * PH)),
        "conv_init": np.ascontiguousarray(
            conv_init.reshape(NKI, 128, 3).transpose(1, 0, 2).reshape(128, NKI * 3)),
        "flaginv": np.full((128, 1), 0.0 if chain_start else 1.0, dtype=np.float32),
        "hmask": hmask,
        "cmask128": cm128,
        "idf_in": eye,
        "idb_in": eye.astype(ml_dtypes.bfloat16),
        "oh_in": oh,
    }
    return {k: np.ascontiguousarray(np.asarray(v, dtype=(
        ml_dtypes.bfloat16 if k == "idb_in" else np.float32))) for k, v in m.items()}




def kernel(**inputs):
    mk_ = MK(("ffn1", "mamba", "ffn2"))
    nc = mk_.build()
    in_maps = [make_inputs(inputs, c) for c in range(NCORES)]
    res = run_bass_kernel_spmd(nc, in_maps, core_ids=list(range(NCORES)))
    outs = [np.asarray(r["y_fm"], dtype=np.float32).T for r in res.results]
    return np.concatenate(outs, axis=0).reshape(2, 8192, D)


# version 05 (final)
# speedup vs baseline: 1.1495x; 1.0103x over previous
import contextlib
import numpy as np
import ml_dtypes
import concourse.bass as bass
import concourse.mybir as mybir
from concourse.bass_utils import run_bass_kernel_spmd

F32 = mybir.dt.float32
BF16 = mybir.dt.bfloat16
AF = mybir.ActivationFunctionType
ALU = mybir.AluOpType
AX = mybir.AxisListType

NCORES = 8
TOK = 2048
HALF = 1024
D = 2048
ND = 16
H = 1408
NHC = 11
E = 8
DI = 3072
NKI = 24
NH = 48
PH = 64
NCH = 16
EPS = 1e-5


class Prog:

    ENGS = ("t", "v", "a", "s", "g")

    def __init__(self, b):
        self.b = b
        self.q = {e: [] for e in self.ENGS}

    def on(self, eng, fn):
        self.q[eng].append(fn)

    def wait(self, eng, sem, val=None):
        v = self.b.cnt[sem] if val is None else val
        self.q[eng].append(lambda e, s=self.b.sems[sem], v=v: e.wait_ge(s, v))

    def emit(self, eng, fn, inc=None):
        incs = []
        if inc:
            for sem, k in inc:
                self.b.cnt[sem] += k
                incs.append((self.b.sems[sem], k))

        def run(e, fn=fn, incs=incs):
            i = fn(e)
            for s, k in incs:
                i = i.then_inc(s, k)
        self.q[eng].append(run)
        return {s: self.b.cnt[s] for s in self.b.cnt} if inc else None

    def flush(self):
        nc = self.b.nc
        with nc.Block() as blk:
            if self.q["t"]:
                blk.tensor(lambda e, fns=self.q["t"]: [f(e) for f in fns])
            if self.q["v"]:
                blk.vector(lambda e, fns=self.q["v"]: [f(e) for f in fns])
            if self.q["a"]:
                blk.scalar(lambda e, fns=self.q["a"]: [f(e) for f in fns])
            if self.q["s"]:
                blk.sync(lambda e, fns=self.q["s"]: [f(e) for f in fns])
            if self.q["g"]:
                blk.gpsimd(lambda e, fns=self.q["g"]: [f(e) for f in fns])
        self.q = {e: [] for e in self.ENGS}


class MK:
    def __init__(self, phases=("ffn1", "mamba", "ffn2"), detect_races=False):
        self.phases = phases
        nc = bass.Bass("TRN2", target_bir_lowering=False, detect_race_conditions=detect_races)
        self.nc = nc
        self.cnt = {}
        self.sems = {}
        for name in ["ld", "st", "sc", "gp", "c0", "c1", "c2", "c3", "w0", "w1", "z0", "z1", "y0", "y1", "cc", "mm", "ev", "ac"]:
            self.sems[name] = nc.alloc_semaphore(name)
            self.cnt[name] = 0

        ap = lambda n, s, d=F32: nc.dram_tensor(n, s, d, kind="ExternalInput").ap()
        self.x_in = ap("x_fm", [D, TOK])
        self.wg_s = ap("wg_s", [D, H]);  self.wu_s = ap("wu_s", [D, H])
        self.wd_s = ap("wd_s", [H, D])
        self.wg_e = ap("wg_e", [E, D, H]); self.wu_e = ap("wu_e", [E, D, H])
        self.wd_e = ap("wd_e", [E, H, D])
        self.wr_in = ap("wr", [D, E])
        self.br_rb = ap("br_rb", [E, 2])
        self.inproj = ap("inproj", [6448, D])
        self.outproj = ap("outproj", [D, DI])
        self.convw_in = ap("convw", [128, NKI * 4])
        self.convb_in = ap("convb", [128, NKI])
        self.dtbias_in = ap("dtbias", [NH, 1])
        self.alog_in = ap("alog", [NH, 1])
        self.drow_in = ap("drow", [1, NH])
        self.normw_in = ap("normw", [128, ND])
        self.innw_in = ap("innw", [1, DI])
        self.h0_in = ap("h0_in", [128, NH * PH])
        self.convinit_in = ap("conv_init", [128, NKI * 3])
        self.flaginv_in = ap("flaginv", [128, 1])
        self.hmask_in = ap("hmask", [128, E])
        self.cm128_in = ap("cmask128", [128, E])
        self.idf_in = ap("idf_in", [128, 128])
        self.idb_in = nc.dram_tensor("idb_in", [128, 128], BF16, kind="ExternalInput").ap()
        self.oh_in = ap("oh_in", [E, E * 128])

        self.y_out = nc.dram_tensor("y_fm", [D, TOK], F32, kind="ExternalOutput").ap()
        self.dbg = False

        dr = lambda n, s, d: nc.dram_tensor(n, s, d, kind="Internal")
        self.x1_d = dr("x1_d", [D, TOK], F32)
        self.wr_bf = dr("wr_bf", [D, E], BF16)
        self.inproj_bf = dr("inproj_bf", [6448, D], BF16)
        self.inprojT_bf = dr("inprojT_bf", [D, 6400], BF16)
        self.outproj_bf = dr("outproj_bf", [D, DI], BF16)
        self.outprojT_bf = dr("outprojT_bf", [DI, D], BF16)
        self.z_d = dr("z_d", [DI, TOK], BF16)
        self.xc_d = dr("xc_d", [DI, TOK], BF16)
        self.yg_d = dr("yg_d", [TOK, DI], BF16)
        self.a_d = dr("a_d", [1, NH * TOK], F32)
        self.bounce = dr("bounce", [1, 8192], F32)

        self.cc_halo_in = dr("cc_halo_in", [128, NKI * 3], BF16)
        self.cc_halo_out = dr("cc_halo_out", [NCORES * 128, NKI * 3], BF16)
        self.cc_sum_in = dr("cc_sum_in", [129, NH * PH], F32)
        self.cc_sum_out = dr("cc_sum_out", [NCORES * 129, NH * PH], F32)

        self.ctx = contextlib.ExitStack()
        self.uid = 0
        self.wf_hist = 0

    def _nm(self, n):
        self.uid += 1
        return f"{n}_{self.uid}"

    def sb(self, n, sh, d):
        return self.ctx.enter_context(self.nc.sbuf_tensor(self._nm(n), sh, d))

    def psum(self, es, n, sh, d=F32):
        return es.enter_context(self.nc.psum_tensor(self._nm(n), sh, d))

    def build(self):
        with self.ctx:
            self._alloc_globals()
            self._p0()
            src = self.x_in
            if "ffn1" in self.phases:
                self._ffn_call(src, self.x1_d.ap(), first=True)
                src = self.x1_d.ap()
                if self.chain_ops:
                    P = Prog(self)
                    self._pump_chain(P, len(self.chain_ops))
                    P.flush()
            if "mamba" in self.phases:
                self._mamba(src, self.x1_d.ap())
                src = self.x1_d.ap()
            if "ffn2" in self.phases:
                self._ffn_call(src, self.y_out, first=False)
            else:
                self._copy_out(src)
            self._final_wait()
        return self.nc

    def _alloc_globals(self):
        s = self.sb
        self.idf = s("idf", [128, 128], F32)
        self.idb = s("idb", [128, 128], BF16)
        self.ones_rf = s("ones_rf", [1, 128], F32)
        self.ones_cb = s("ones_cb", [128, 1], BF16)
        self.ones_rb = s("ones_rb", [1, 128], BF16)
        self.zero128 = s("zero128", [128, 128], F32)
        self.epsc = s("epsc", [128, 1], F32)
        self.wr_sb = s("wr_sb", [128, ND * E], BF16)
        self.biasr = s("biasr", [E, 1], F32)
        self.normw_sb = s("normw_sb", [128, ND], F32)
        self.dcols = s("dcols", [128, NH], F32)
        self.rowtmp = s("rowtmp", [1, 64], F32)
        self.convw_sb = s("convw_sb", [128, NKI * 4], F32)
        self.convb_sb = s("convb_sb", [128, NKI], F32)
        self.flaginv_sb = s("flaginv_sb", [128, 1], F32)
        self.hmask_sb = s("hmask_sb", [128, E], F32)
        self.dtb = s("dtb", [NH, 1], F32)
        self.acol = s("acol", [NH, 1], F32)
        self.cm128 = s("cm128", [128, E], F32)
        self.cm128i = s("cm128i", [128, E], F32)
        self.brrb = s("brrb", [E, 2], F32)
        self.dtlhs = s("dtlhs", [128, ND * NH], BF16)

    def _p0(self):
        b = self
        P = Prog(b)
        loads = [
            (self.idf[:, :], self.idf_in), (self.idb[:, :], self.idb_in),
            (self.normw_sb[:, :], self.normw_in),
            (self.convw_sb[:, :], self.convw_in), (self.convb_sb[:, :], self.convb_in),
            (self.flaginv_sb[:, :], self.flaginv_in), (self.hmask_sb[:, :], self.hmask_in),
            (self.dtb[:, :], self.dtbias_in), (self.acol[:, :], self.alog_in),
            (self.cm128[:, :], self.cm128_in), (self.brrb[:, :], self.br_rb),
        ]
        for dst, srcc in loads:
            P.emit("s", lambda e, d=dst, x=srcc: e.dma_start(out=d, in_=x),
                   inc=[("ld", 16)])
        P.emit("g", lambda e: e.dma_start(out=self.wr_bf.ap(), in_=self.wr_in),
               inc=[("gp", 16)])
        P.wait("g", "gp")
        P.emit("g", lambda e: e.dma_start(
            out=self.wr_sb[:, :].rearrange("p (c ee) -> p c ee", ee=E),
            in_=self.wr_bf.ap().rearrange("(c p) ee -> p c ee", p=128)),
            inc=[("gp", 16)])
        P.emit("g", lambda e: e.memset(self.ones_rf[:, :], 1.0))
        P.emit("g", lambda e: e.memset(self.ones_cb[:, :], 1.0))
        P.emit("g", lambda e: e.memset(self.ones_rb[:, :], 1.0))
        P.emit("g", lambda e: e.memset(self.zero128[:, :], 0.0))
        P.emit("g", lambda e: e.memset(self.epsc[:, :], EPS))
        self.cast_chunks = []
        for i in range(24):
            r0, r1 = i * 269, min(6448, (i + 1) * 269)
            self.cast_chunks.append(
                lambda e, r0=r0, r1=r1: e.dma_start(
                    out=self.inproj_bf.ap()[r0:r1, :], in_=self.inproj[r0:r1, :]))
        for i in range(12):
            r0, r1 = i * 171, min(2048, (i + 1) * 171)
            self.cast_chunks.append(
                lambda e, r0=r0, r1=r1: e.dma_start(
                    out=self.outproj_bf.ap()[r0:r1, :], in_=self.outproj[r0:r1, :]))
        self.castgrp = {"c3": 16 * len(self.cast_chunks)}
        if "ffn1" not in self.phases:
            for fn in self.cast_chunks:
                P.emit("g", fn, inc=[("c3", 16)])
            self.cast_chunks = []
        P.wait("s", "ld")
        P.flush()

        P = Prog(b)
        P.emit("v", lambda e: e.tensor_add(self.biasr[:, :], self.brrb[:, 0:1],
                                           self.brrb[:, 1:2]))
        P.emit("s", lambda e: e.dma_start(out=self.rowtmp[0:1, 0:NH], in_=self.drow_in),
               inc=[("ld", 16)])
        P.wait("t", "ld")
        with self.nc.psum_tensor(self._nm("ps0_p0"), [128, 512], F32) as ps0:
            P.emit("t", lambda e, p=ps0: e.matmul(p[:, 0:NH], self.ones_rf[:, :],
                                                  self.rowtmp[0:1, 0:NH],
                                                  start=True, stop=True),
                   inc=[("mm", 1)])
            P.wait("v", "mm")
            P.emit("v", lambda e, p=ps0: e.tensor_copy(self.dcols[:, :], p[:, 0:NH]))
            P.flush()

        self.stg = self.sb("p0stage", [128, 3200], BF16)
        self.prep_sc_done = None
        self._make_chain_ops()
        if "ffn1" not in self.phases:
            P = Prog(self)
            self._pump_chain(P, len(self.chain_ops))
            P.flush()

    def _make_chain_ops(self):
        stg = self.stg
        first = [True]

        def mk_pair(xf, sf):
            def run(P):
                if first[0]:
                    P.wait("s", "c3", self.castgrp["c3"])
                    first[0] = False
                P.emit("s", xf, inc=[("sc", 16)])
                P.wait("s", "sc")
                P.emit("s", sf, inc=[("sc", 16)])
                P.wait("s", "sc")
            return run
        ops = []
        for dc in range(ND):
            for hf in range(2):
                ops.append(mk_pair(
                    lambda e, dc=dc, hf=hf: e.dma_start_transpose(
                        stg[:, 0:3200],
                        self.inproj_bf.ap()[hf * 3200:(hf + 1) * 3200,
                                            dc * 128:(dc + 1) * 128]),
                    lambda e, dc=dc, hf=hf: e.dma_start(
                        out=self.inprojT_bf.ap()[dc * 128:(dc + 1) * 128,
                                                 hf * 3200:(hf + 1) * 3200],
                        in_=stg[:, 0:3200])))
        for kc in range(NKI):
            ops.append(mk_pair(
                lambda e, kc=kc: e.dma_start_transpose(
                    stg[:, 0:2048], self.outproj_bf.ap()[:, kc * 128:(kc + 1) * 128]),
                lambda e, kc=kc: e.dma_start(
                    out=self.outprojT_bf.ap()[kc * 128:(kc + 1) * 128, :],
                    in_=stg[:, 0:2048])))

        def last(P):
            P.emit("s", lambda e: e.dma_start(
                out=stg[0:48, 0:2048], in_=self.inproj_bf.ap()[6400:6448, :]),
                inc=[("sc", 16)])
            self.prep_sc_done = self.cnt["sc"]
        ops.append(last)
        self.chain_ops = ops

    def _pump_chain(self, P, n):
        while n > 0 and self.chain_ops:
            self.chain_ops.pop(0)(P)
            n -= 1

    def _emit_dtlhs(self):
        P = Prog(self)
        stg = self.stg
        P.wait("t", "sc", self.prep_sc_done)
        with self.nc.psum_tensor(self._nm("ps_tp0"), [128, 1024], BF16) as pstb:
            for dc in range(ND):
                if dc > 0:
                    P.wait("t", "ev")
                P.emit("t", lambda e, dc=dc, p=pstb, stg=stg: e.transpose(
                    p[:, 0:48], stg[0:48, dc * 128:(dc + 1) * 128], self.idb[0:48, 0:48]),
                    inc=[("mm", 1)])
                P.wait("v", "mm")
                P.emit("v", lambda e, dc=dc, p=pstb: e.tensor_copy(
                    self.dtlhs[:, dc * NH:(dc + 1) * NH], p[:, 0:48]), inc=[("ev", 1)])
            P.flush()

    def _ln_tile(self, P, xsrc, sink, tglob, psb, mode="ln"):
        b = self
        ps_sx, ps_sq, ps_r1, ps_r2 = psb
        P.wait("t", "ev")
        P.wait("t", "ac")
        P.wait("a", "mm")
        P.wait("v", "mm")
        for dc in range(ND):
            P.emit("v", lambda e, dc=dc: e.tensor_copy(sink(dc), xsrc(dc)),
                   inc=[("ev", 1)])
            P.wait("a", "ev")
            P.emit("a", lambda e, dc=dc: e.activation(
                self.xsq[:, dc * 512:(dc + 1) * 512], sink(dc),
                AF.Square), inc=[("ac", 1)])
            P.wait("t", "ev")
            P.emit("t", lambda e, dc=dc: e.matmul(
                ps_sx[0:1, 0:512], self.ones_cb[:, :], sink(dc),
                start=(dc == 0), stop=(dc == ND - 1)), inc=([("mm", 1)] if dc == ND - 1 else None))
            P.wait("t", "ac")
            P.emit("t", lambda e, dc=dc: e.matmul(
                ps_sq[0:1, 0:512], self.ones_cb[:, :],
                self.xsq[:, dc * 512:(dc + 1) * 512],
                start=(dc == 0), stop=(dc == ND - 1)), inc=([("mm", 1)] if dc == ND - 1 else None))
        P.wait("v", "mm")
        P.emit("v", lambda e: e.tensor_scalar(self.rowa[0:1, :], ps_sx[0:1, 0:512],
                                              1.0 / D, None, ALU.mult), inc=[("ev", 1)])
        P.emit("v", lambda e: e.tensor_scalar(self.rowb[0:1, :], ps_sq[0:1, 0:512],
                                              1.0 / D, None, ALU.mult), inc=[("ev", 1)])
        P.wait("a", "ev")
        P.emit("a", lambda e: e.activation(self.rowc[0:1, :], self.rowa[0:1, :],
                                           AF.Square), inc=[("ac", 1)])
        P.wait("v", "ac")
        P.emit("v", lambda e: e.tensor_sub(self.rowb[0:1, :], self.rowb[0:1, :],
                                           self.rowc[0:1, :]), inc=[("ev", 1)])
        P.wait("a", "ev")
        P.emit("a", lambda e: e.activation(self.rowc[0:1, :], self.rowb[0:1, :],
                                           AF.Ln, bias=self.epsc[0:1, :]))
        P.emit("a", lambda e: e.activation(self.rowc[0:1, :], self.rowc[0:1, :],
                                           AF.Exp, scale=-0.5), inc=[("ac", 1)])
        scale_row = self.rowc
        if mode == "mamba":
            P.wait("v", "ac")
            P.emit("v", lambda e: e.tensor_copy(self.mrow[0:1, tglob:tglob + 512],
                                                self.rowa[0:1, :]))
            P.emit("v", lambda e: e.tensor_copy(self.srow[0:1, tglob:tglob + 512],
                                                self.rowc[0:1, :]))
            P.emit("v", lambda e: e.tensor_mul(self.rowd[0:1, :], self.rowc[0:1, :],
                                               self.rowc[0:1, :]))
            P.emit("v", lambda e: e.tensor_mul(self.rowd[0:1, :], self.rowd[0:1, :],
                                               self.rowb[0:1, :]), inc=[("ev", 1)])
            P.wait("a", "ev")
            P.emit("a", lambda e: e.activation(self.rowd[0:1, :], self.rowd[0:1, :],
                                               AF.Ln, bias=self.epsc[0:1, :]))
            P.emit("a", lambda e: e.activation(self.rowd[0:1, :], self.rowd[0:1, :],
                                               AF.Exp, scale=-0.5), inc=[("ac", 1)])
            P.wait("v", "ac")
            P.emit("v", lambda e: e.tensor_mul(self.rowd[0:1, :], self.rowd[0:1, :],
                                               self.rowc[0:1, :]), inc=[("ev", 1)])
            scale_row = self.rowd
        P.wait("t", "ev" if mode == "mamba" else "ac")
        P.emit("t", lambda e: e.matmul(ps_r1[:, 0:512], self.ones_rf[:, :],
                                       self.rowa[0:1, :], start=True, stop=True))
        P.emit("t", lambda e, sr=scale_row: e.matmul(ps_r2[:, 0:512], self.ones_rf[:, :],
                                                     sr[0:1, :], start=True, stop=True),
               inc=[("mm", 1)])
        P.wait("v", "mm")
        for dc in range(ND):
            P.emit("v", lambda e, dc=dc: e.tensor_sub(self.tf32[:, 0:512], xsrc(dc),
                                                      ps_r1[:, 0:512]))
            if mode == "mamba":
                P.emit("v", lambda e, dc=dc: e.tensor_mul(self.tf32[:, 0:512],
                                                          self.tf32[:, 0:512],
                                                          ps_r2[:, 0:512]))
                P.emit("v", lambda e, dc=dc: e.tensor_scalar(
                    sink(dc), self.tf32[:, 0:512], self.normw_sb[:, dc:dc + 1], None,
                    ALU.mult), inc=[("ev", 1)])
            else:
                P.emit("v", lambda e, dc=dc: e.tensor_mul(sink(dc), self.tf32[:, 0:512],
                                                          ps_r2[:, 0:512]),
                       inc=[("ev", 1)])
        self.u_done = self.cnt["ev"]

    def _ffn_call(self, src, dst, first):
        b = self
        nc = self.nc
        with contextlib.ExitStack() as fs:
            s = lambda n, sh, d: fs.enter_context(nc.sbuf_tensor(self._nm(n), sh, d))
            self.xh = s("xh", [128, ND * HALF], F32)
            self.uh = s("uh", [128, ND * HALF], BF16)
            self.gp_b = s("gp_b", [128, NHC * HALF], BF16)
            self.wbuf = s("wbuf", [128, 2 * ND * 128], BF16)
            self.wrep = s("wrep", [128, HALF], F32)
            self.xsq = s("xsq", [128, ND * 512], BF16)
            self.tf32 = s("tf32", [128, 512], F32)
            self.rowa = s("rowa", [1, 512], F32)
            self.rowb = s("rowb", [1, 512], F32)
            self.rowc = s("rowc", [1, 512], F32)
            self.rowd = s("rowd", [1, 512], F32)
            self.oh_sb = s("oh_sb", [E, E * 128], F32)
            self.wTh = s("wTh", [E, HALF], F32)
            self.probs = s("probs", [E, 512], F32)
            self.ptm = s("ptm", [128, 8], F32)
            self.maxs = s("maxs", [128, 8], F32)
            self.wtm = s("wtm", [128, 8], F32)
            self.v12 = s("v12", [128, 2], F32)

            P = Prog(b)
            P.emit("s", lambda e: e.dma_start(out=self.oh_sb[:, :], in_=self.oh_in),
                   inc=[("ld", 16)])
            P.wait("s", "ld")
            P.flush()

            for ha in range(2):
                self._ffn_half(src, dst, ha, first)

    def _ffn_half(self, src, dst, ha, first):
        b = self
        nc = self.nc
        t0 = ha * HALF
        P = Prog(b)
        for dc in range(ND):
            P.emit("s", lambda e, dc=dc: e.dma_start(
                out=self.xh[:, dc * HALF:(dc + 1) * HALF],
                in_=src[dc * 128:(dc + 1) * 128, t0:t0 + HALF]), inc=[("ld", 16)])
        P.wait("s", "ld")
        P.flush()

        with contextlib.ExitStack() as pes:
            ps_a0 = self.psum(pes, "ps_a0", [128, 512])
            ps_a1 = self.psum(pes, "ps_a1", [128, 512])
            ps_sx = self.psum(pes, "ps_sx", [128, 512])
            ps_sq = self.psum(pes, "ps_sq", [128, 512])
            ps_r1 = self.psum(pes, "ps_r1", [128, 512])
            ps_r2 = self.psum(pes, "ps_r2", [128, 512])
            ps_x = self.psum(pes, "ps_x", [128, 512])
            psb = (ps_sx, ps_sq, ps_r1, ps_r2)
            ps_mm = [ps_a0, ps_a1]

            P = Prog(b)
            u_tile_done = []
            for ti in range(2):
                xsrc = lambda dc, ti=ti: self.xh[:, dc * HALF + ti * 512:
                                                 dc * HALF + (ti + 1) * 512]
                sink = lambda dc, ti=ti: self.uh[:, dc * HALF + ti * 512:
                                                 dc * HALF + (ti + 1) * 512]
                self._ln_tile(P, xsrc, sink, t0 + ti * 512, psb, mode="ln")
                u_tile_done.append(self.u_done)
                P.wait("t", "ev")
                for dc in range(ND):
                    P.emit("t", lambda e, dc=dc, ti=ti: e.matmul(
                        ps_x[0:8, 0:512], self.wr_sb[:, dc * E:(dc + 1) * E],
                        self.uh[:, dc * HALF + ti * 512:dc * HALF + (ti + 1) * 512],
                        start=(dc == 0), stop=(dc == ND - 1)), inc=(
                        [("mm", 1)] if dc == ND - 1 else None))
                P.wait("a", "mm")
                P.emit("a", lambda e: e.activation(self.probs[:, 0:512], ps_x[0:8, 0:512],
                                                   AF.Sigmoid, bias=self.biasr[:, :]),
                       inc=[("ac", 1)])
                for j in range(4):
                    P.wait("t", "ac")
                    if j > 0 or ti > 0:
                        P.wait("t", "ev")
                    P.emit("t", lambda e, j=j: e.transpose(
                        ps_x[0:128, 0:8] if False else ps_x[:, 0:8],
                        self.probs[0:8, j * 128:(j + 1) * 128], self.idf[0:8, 0:8]),
                        inc=[("mm", 1)])
                    P.wait("v", "mm")
                    for fn in [
                        lambda e: e.tensor_copy(self.ptm[:, :], ps_x[:, 0:8]),
                        lambda e: e.max(out=self.maxs[:, :], in_=self.ptm[:, :]),
                        lambda e: e.tensor_add(self.v12[:, 0:1], self.maxs[:, 0:1],
                                               self.maxs[:, 1:2]),
                        lambda e: e.reciprocal(self.v12[:, 1:2], self.v12[:, 0:1]),
                        lambda e: e.tensor_scalar(self.wtm[:, :], self.ptm[:, :],
                                                  self.maxs[:, 1:2], None, ALU.is_ge),
                        lambda e: e.tensor_mul(self.wtm[:, :], self.wtm[:, :],
                                               self.ptm[:, :]),
                        lambda e: e.tensor_scalar(self.wtm[:, :], self.wtm[:, :],
                                                  self.v12[:, 1:2], None, ALU.mult),
                    ]:
                        P.emit("v", fn, inc=[("ev", 1)])
                        P.wait("v", "ev")
                    P.wait("t", "ev")
                    P.emit("t", lambda e, j=j: e.transpose(
                        ps_x[0:8, 8:136], self.wtm[:, :], self.idf[:, :]),
                        inc=[("mm", 1)])
                    P.wait("v", "mm")
                    P.emit("v", lambda e, j=j, ti=ti: e.tensor_copy(
                        self.wTh[0:8, ti * 512 + j * 128:ti * 512 + (j + 1) * 128],
                        ps_x[0:8, 8:136]), inc=[("ev", 1)])
            self.wth_done = self.cnt["ev"]
            P.flush()

            for e_ in range(9):
                self._ffn_expert(e_, ha, first, ps_mm, ps_x, (ps_r1, ps_r2))

        P = Prog(b)
        P.wait("s", "ev")
        for dc in range(ND):
            P.emit("s", lambda e, dc=dc: e.dma_start(
                out=dst[dc * 128:(dc + 1) * 128, t0:t0 + HALF],
                in_=self.xh[:, dc * HALF:(dc + 1) * HALF]), inc=[("st", 16)])
        P.flush()

    def _ffn_expert(self, e_, ha, first, ps_mm, ps_x, ps_wreps):
        b = self
        P = Prog(b)
        if first and ha == 1:
            self._pump_chain(P, 7)
        shared = (e_ == 0)
        if shared:
            wg, wu, wd = self.wg_s, self.wu_s, self.wd_s
        else:
            wg, wu, wd = self.wg_e[e_ - 1], self.wu_e[e_ - 1], self.wd_e[e_ - 1]

        if not shared:
            P.wait("t", "ev")
            for ti in range(2):
                pw = ps_wreps[ti % 2]
                P.emit("t", lambda e, ti=ti, pw=pw: e.matmul(
                    pw[:, 0:512], self.oh_sb[:, (e_ - 1) * 128:e_ * 128],
                    self.wTh[0:8, ti * 512:(ti + 1) * 512], start=True, stop=True),
                    inc=[("mm", 1)])
                P.wait("v", "mm")
                P.emit("v", lambda e, ti=ti, pw=pw: e.tensor_copy(
                    self.wrep[:, ti * 512:(ti + 1) * 512], pw[:, 0:512]),
                    inc=[("ev", 1)])
        self.wrep_done = self.cnt["ev"]

        nbuf = ND * 128
        cols = [("g", i) for i in range(NHC)] + [("u", i) for i in range(NHC)] + \
               [("d", i) for i in range(ND)]

        mm_of_tile = {}
        ev_free = []
        if not hasattr(self, "mm_after_col"):
            self.mm_after_col = []
        tidx = 0
        P.wait("t", "ev")
        P.wait("t", "ac")
        for ci, (kind, col) in enumerate(cols):
            gci = self.wf_hist
            if gci >= 2:
                P.wait("g", "mm", self.mm_after_col[gci - 2])
            self.wf_hist += 1
            if kind == "g":
                srcw = wg[:, col * 128:(col + 1) * 128]
            elif kind == "u":
                srcw = wu[:, col * 128:(col + 1) * 128]
            else:
                srcw = wd[:, col * 128:(col + 1) * 128]
            buf = self.wf_hist % 2
            wsem = "w0" if buf == 0 else "w1"
            nkw = ND if kind != "d" else NHC
            P.emit("g", lambda e, srcw=srcw, buf=buf, nkw=nkw: e.dma_start(
                out=self.wbuf[:, buf * nbuf:buf * nbuf + nkw * 128].rearrange(
                    "p (c n) -> p c n", n=128),
                in_=srcw.rearrange("(c p) n -> p c n", p=128)), inc=[(wsem, 16)])
            if first and ha == 0 and ci >= 2 and self.cast_chunks:
                P.emit("g", self.cast_chunks.pop(0), inc=[("c3", 16)])

            P.wait("t", wsem)
            nk = ND if kind != "d" else NHC
            for ti in range(2):
                pp = ps_mm[tidx % 2]
                if tidx >= 2:
                    P.wait("t", "ev" if ev_free[tidx - 2][0] == "ev" else "ac",
                           ev_free[tidx - 2][1])
                for k in range(nk):
                    if kind == "d":
                        rhs = self.gp_b[:, k * HALF + ti * 512:k * HALF + (ti + 1) * 512]
                    else:
                        rhs = self.uh[:, k * HALF + ti * 512:k * HALF + (ti + 1) * 512]
                    last = (k == nk - 1)
                    incs = [("mm", 1)] if last else None
                    P.emit("t", lambda e, pp=pp, k=k, rhs=rhs, buf=buf, kk=(k == 0),
                           st=last: e.matmul(
                        pp[:, 0:512],
                        self.wbuf[:, buf * nbuf + k * 128:buf * nbuf + (k + 1) * 128],
                        rhs, start=kk, stop=st), inc=incs)
                mm_of_tile[(kind, col, ti)] = self.cnt["mm"]

                gslice = self.gp_b[:, col * HALF + ti * 512:col * HALF + (ti + 1) * 512] \
                    if kind != "d" else None
                if kind == "g":
                    P.wait("a", "mm", mm_of_tile[(kind, col, ti)])
                    P.emit("a", lambda e, pp=pp, gs=gslice: e.activation(
                        gs, pp[:, 0:512], AF.Silu), inc=[("ac", 1)])
                    ev_free.append(("ac", self.cnt["ac"]))
                elif kind == "u":
                    P.wait("v", "mm", mm_of_tile[(kind, col, ti)])
                    if not shared:
                        P.emit("v", lambda e, pp=pp, gs=gslice: e.tensor_mul(
                            self.tf32[:, 0:512], pp[:, 0:512], gs))
                        P.emit("v", lambda e, gs=gslice, ti=ti: e.tensor_mul(
                            gs, self.tf32[:, 0:512],
                            self.wrep[:, ti * 512:(ti + 1) * 512]), inc=[("ev", 1)])
                    else:
                        P.emit("v", lambda e, pp=pp, gs=gslice: e.tensor_mul(
                            gs, pp[:, 0:512], gs), inc=[("ev", 1)])
                    ev_free.append(("ev", self.cnt["ev"]))
                else:
                    xsl = self.xh[:, col * HALF + ti * 512:col * HALF + (ti + 1) * 512]
                    P.wait("v", "mm", mm_of_tile[(kind, col, ti)])
                    P.emit("v", lambda e, pp=pp, xsl=xsl: e.scalar_tensor_tensor(
                        xsl, pp[:, 0:512], 0.5, xsl, ALU.mult, ALU.add),
                        inc=[("ev", 1)])
                    ev_free.append(("ev", self.cnt["ev"]))
                tidx += 1
            self.mm_after_col.append(self.cnt["mm"])
        P.flush()

    def _copy_out(self, src):
        P = Prog(self)
        for dc in range(ND):
            P.emit("s", lambda e, dc=dc: e.dma_start(
                out=self.y_out[dc * 128:(dc + 1) * 128, :],
                in_=src[dc * 128:(dc + 1) * 128, :]), inc=[("st", 16)])
        P.wait("s", "st")
        P.flush()

    def _final_wait(self):
        P = Prog(self)
        P.wait("g", "st")
        P.flush()

    def _mamba(self, src, dst):
        emit_mamba(self, src, dst)




def emit_mamba(b, src, dst):
    nc = b.nc
    b.ev_free_m = {}
    b.ev_free_o = {}
    with contextlib.ExitStack() as ms:
        s = lambda n, sh, d: ms.enter_context(nc.sbuf_tensor(b._nm(n), sh, d))
        psum = lambda es, n, sh, d=F32: b.psum(es, n, sh, d)
        b.mrow = s("mrow", [1, TOK], F32)
        b.srow = s("srow", [1, TOK], F32)
        b.innw_rep = s("innw_rep", [128, DI], BF16)
        b.b_bf = s("b_bf", [128, TOK], BF16)
        b.c_bf = s("c_bf", [128, TOK], BF16)
        b.dtraw = s("dtraw", [NH, TOK], F32)
        b.halo = s("halo", [128, NKI * 3], BF16)
        b.convinit_sb = s("convinit_sb", [128, NKI * 3], F32)
        b.tot = s("tot", [NH, NCH + 1], F32)
        b.deT = s("deT", [128, NCH * NH], F32)
        b.prow = s("prow", [1, (NCH + 1) * NH], F32)
        b.pcols = s("pcols", [128, (NCH + 1) * NH], F32)
        b.runall = s("runall", [128, NCH * NH], F32)
        b.bT = s("bT", [128, NCH * 128], BF16)


        P = Prog(b)
        P.emit("g", lambda e: e.dma_start(out=b.innw_rep[:, :],
                                          in_=b.innw_in.broadcast_to([128, DI])),
               inc=[("gp", 16)])
        P.emit("s", lambda e: e.dma_start(out=b.convinit_sb[:, :], in_=b.convinit_in),
               inc=[("ld", 16)])
        P.emit("v", lambda e: e.tensor_scalar(b.cm128i[:, :], b.cm128[:, :], -1.0, 1.0,
                                              ALU.mult, ALU.add))
        P.wait("g", "gp")
        P.wait("s", "ld")
        P.flush()

        b._emit_dtlhs()
        _m1_u_inproj(b, ms, src)
        _m2_prep(b, ms)
        with contextlib.ExitStack() as bs:
            b.big = bs.enter_context(nc.sbuf_tensor(
                b._nm("big"), [128, NKI * NCH * 128], BF16))
            b.hst = bs.enter_context(nc.sbuf_tensor(
                b._nm("hst"), [128, NH * PH], F32))
            _m3_conv(b, ms)
            _m4_pass1_exchange(b, ms)
            _m5_pass2_gate(b, ms)
        _m6_outproj(b, ms, src, dst)


def _m1_u_inproj(b, ms, src):
    nc = b.nc
    with contextlib.ExitStack() as fs:
        s = lambda n, sh, d: fs.enter_context(nc.sbuf_tensor(b._nm(n), sh, d))
        b.uh = s("uh", [128, ND * TOK], BF16)
        b.xh = None
        xt = s("xt", [128, ND * 512], F32)
        b.xsq = s("xsq", [128, ND * 512], BF16)
        b.tf32 = s("tf32", [128, 512], F32)
        b.rowa = s("rowa", [1, 512], F32)
        b.rowb = s("rowb", [1, 512], F32)
        b.rowc = s("rowc", [1, 512], F32)
        b.rowd = s("rowd", [1, 512], F32)
        wbuf = s("wbufm", [128, 2 * ND * 128], BF16)
        zst = s("zst", [128, 2 * 512], BF16)

        with contextlib.ExitStack() as pes:
            ps_a0 = b.psum(pes, "ps_a0", [128, 512])
            ps_a1 = b.psum(pes, "ps_a1", [128, 512])
            ps_sx = b.psum(pes, "ps_sx", [128, 512])
            ps_sq = b.psum(pes, "ps_sq", [128, 512])
            ps_r1 = b.psum(pes, "ps_r1", [128, 512])
            ps_r2 = b.psum(pes, "ps_r2", [128, 512])
            psb = (ps_sx, ps_sq, ps_r1, ps_r2)
            ps_mm = [ps_a0, ps_a1]

            P = Prog(b)
            for tau in range(4):
                P.wait("s", "mm")
                P.wait("s", "ev")
                for dc in range(ND):
                    P.emit("s", lambda e, dc=dc, tau=tau: e.dma_start(
                        out=xt[:, dc * 512:(dc + 1) * 512],
                        in_=src[dc * 128:(dc + 1) * 128, tau * 512:(tau + 1) * 512]),
                        inc=[("ld", 16)])
                P.wait("s", "ld")
                P.flush()
                P = Prog(b)
                xsrc = lambda dc: xt[:, dc * 512:(dc + 1) * 512]
                sink = lambda dc, tau=tau: b.uh[:, dc * TOK + tau * 512:
                                                dc * TOK + (tau + 1) * 512]
                b._ln_tile(P, xsrc, sink, tau * 512, psb, mode="mamba")
                P.flush()
                P = Prog(b)

            P.wait("t", "ev")
            P.wait("t", "ac")
            P.wait("g", "sc", b.prep_sc_done)
            zpar_cnt = [0, 0]
            zpar_ev = [0, 0]
            for oc in range(50):
                gci = b.wf_hist
                if gci >= 2:
                    P.wait("g", "mm", b.mm_after_col[gci - 2])
                b.wf_hist += 1
                buf = b.wf_hist % 2
                wsem = "w0" if buf == 0 else "w1"
                srcw = b.inprojT_bf.ap()[:, oc * 128:(oc + 1) * 128]
                P.emit("g", lambda e, srcw=srcw, buf=buf: e.dma_start(
                    out=wbuf[:, buf * ND * 128:(buf + 1) * ND * 128].rearrange(
                        "p (c n) -> p c n", n=128),
                    in_=srcw.rearrange("(c p) n -> p c n", p=128)), inc=[(wsem, 16)])
                P.wait("t", wsem)
                for tau in range(4):
                    ti = oc * 4 + tau
                    pp = ps_mm[ti % 2]
                    if ti >= 2:
                        kind_prev, cnt_prev = b.ev_free_m[ti - 2]
                        P.wait("t", kind_prev, cnt_prev)
                    for k in range(ND):
                        P.emit("t", lambda e, pp=pp, k=k, buf=buf, tau=tau: e.matmul(
                            pp[:, 0:512],
                            wbuf[:, buf * ND * 128 + k * 128:buf * ND * 128 + (k + 1) * 128],
                            b.uh[:, k * TOK + tau * 512:k * TOK + (tau + 1) * 512],
                            start=(k == 0), stop=(k == ND - 1)),
                            inc=([("mm", 1)] if k == ND - 1 else None))
                    mmv = b.cnt["mm"]
                    if oc < 48:
                        zp = ti % 2
                        zsem = "z0" if zp == 0 else "z1"
                        if zpar_cnt[zp] > 0:
                            P.wait("a", zsem, zpar_cnt[zp])
                        if zpar_ev[zp] > 0:
                            P.wait("a", "ev", zpar_ev[zp])
                        P.wait("a", "mm", mmv)
                        P.emit("a", lambda e, pp=pp, zp=zp: e.activation(
                            zst[:, zp * 512:(zp + 1) * 512], pp[:, 0:512], AF.Copy),
                            inc=[("ac", 1)])
                        b.ev_free_m[ti] = ("ac", b.cnt["ac"])
                        P.wait("s", "ac", b.cnt["ac"])
                        dstt = b.z_d.ap() if oc < 24 else b.xc_d.ap()
                        r0 = (oc % 24) * 128
                        P.emit("s", lambda e, dstt=dstt, r0=r0, tau=tau, zp=zp:
                               e.dma_start(out=dstt[r0:r0 + 128,
                                                    tau * 512:(tau + 1) * 512],
                                           in_=zst[:, zp * 512:(zp + 1) * 512]),
                               inc=[(zsem, 16)])
                        zpar_cnt[zp] = b.cnt[zsem]
                        if oc >= 24 and tau == 3:
                            P.wait("v", "ac", b.ev_free_m[ti][1])
                            P.emit("v", lambda e, oc=oc, zp=zp: e.tensor_copy(
                                b.halo[:, (oc - 24) * 3:(oc - 23) * 3],
                                zst[:, zp * 512 + 509:zp * 512 + 512]), inc=[("ev", 1)])
                            zpar_ev[zp] = b.cnt["ev"]
                            if oc == 47:
                                b.halo_ev = b.cnt["ev"]
                    elif oc == 48:
                        P.wait("v", "mm", mmv)
                        P.emit("v", lambda e, pp=pp, tau=tau: e.tensor_copy(
                            b.b_bf[:, tau * 512:(tau + 1) * 512], pp[:, 0:512]),
                            inc=[("ev", 1)])
                        b.ev_free_m[ti] = ("ev", b.cnt["ev"])
                    else:
                        P.wait("v", "mm", mmv)
                        P.emit("v", lambda e, pp=pp, tau=tau: e.tensor_copy(
                            b.c_bf[:, tau * 512:(tau + 1) * 512], pp[:, 0:512]),
                            inc=[("ev", 1)])
                        b.ev_free_m[ti] = ("ev", b.cnt["ev"])
                b.mm_after_col.append(b.cnt["mm"])
            P.wait("s", "ev", b.halo_ev)
            P.emit("s", lambda e: e.dma_start(out=b.cc_halo_in.ap(), in_=b.halo[:, :]),
                   inc=[("st", 16)])
            P.wait("g", "st", b.cnt["st"])
            P.emit("g", lambda e: e.collective_compute(
                "AllGather", ALU.bypass, replica_groups=[list(range(NCORES))],
                ins=[b.cc_halo_in.ap().opt()], outs=[b.cc_halo_out.ap().opt()]),
                inc=[("cc", 1)])
            for tau in range(4):
                ti = 200 + tau
                pp = ps_mm[tau % 2]
                kind_prev, cnt_prev = b.ev_free_m[198 + tau if tau < 2 else 200 + tau - 2]
                P.wait("t", kind_prev, cnt_prev)
                for k in range(ND):
                    P.emit("t", lambda e, pp=pp, k=k, tau=tau: e.matmul(
                        pp[0:NH, 0:512], b.dtlhs[:, k * NH:(k + 1) * NH],
                        b.uh[:, k * TOK + tau * 512:k * TOK + (tau + 1) * 512],
                        start=(k == 0), stop=(k == ND - 1)),
                        inc=([("mm", 1)] if k == ND - 1 else None))
                P.wait("v", "mm")
                P.emit("v", lambda e, pp=pp, tau=tau: e.tensor_copy(
                    b.dtraw[:, tau * 512:(tau + 1) * 512], pp[0:NH, 0:512]),
                    inc=[("ev", 1)])
                b.ev_free_m[200 + tau] = ("ev", b.cnt["ev"])
            P.flush()


def _m2_prep(b, ms):
    nc = b.nc
    with contextlib.ExitStack() as fs:
        s = lambda n, sh, d: fs.enter_context(nc.sbuf_tensor(b._nm(n), sh, d))
        la = s("la", [NH, TOK], F32)
        afm = s("afm", [NH, TOK], F32)
        cum = s("cum", [NH, TOK], F32)
        sc48 = s("sc48", [NH, TOK], F32)
        sc48b = s("sc48b", [NH, TOK], F32)
        agh = s("agh", [128, NCORES * NKI * 3], BF16)

        P = Prog(b)
        P.emit("a", lambda e: e.activation(sc48[:, :], b.dtraw[:, :], AF.Abs,
                                           bias=b.dtb[:, :], scale=1.0))
        P.emit("a", lambda e: e.activation(sc48[:, :], sc48[:, :], AF.Exp, scale=-1.0))
        P.emit("a", lambda e: e.activation(sc48[:, :], sc48[:, :], AF.Ln, bias=1.0),
               inc=[("ac", 1)])
        P.emit("a", lambda e: e.activation(b.acol[:, :], b.acol[:, :], AF.Exp),
               inc=[("ac", 1)])
        P.wait("v", "ac")
        P.emit("v", lambda e: e.tensor_scalar(sc48b[:, :], b.dtraw[:, :], b.dtb[:, :],
                                              0.0, ALU.add, ALU.max), inc=[("ev", 1)])
        P.emit("v", lambda e: e.tensor_add(sc48b[:, :], sc48b[:, :], sc48[:, :]),
               inc=[("ev", 1)])
        P.emit("v", lambda e: e.tensor_scalar(la[:, :], sc48b[:, :], b.acol[:, :],
                                              -1.0, ALU.mult, ALU.mult), inc=[("ev", 1)])
        P.wait("a", "ev")
        P.emit("a", lambda e: e.activation(afm[:, :], la[:, :], AF.Exp), inc=[("ac", 1)])
        P.wait("v", "ac")
        for c in range(NCH):
            sl = slice(c * 128, (c + 1) * 128)
            P.emit("v", lambda e, sl=sl: e.tensor_tensor_scan(
                cum[:, sl], la[:, sl], la[:, sl], 0.0, ALU.add, ALU.bypass))
            P.emit("v", lambda e, c=c, sl=sl: e.reduce_sum(
                b.tot[:, c:c + 1], la[:, sl], axis=AX.X), inc=[("ev", 1)])
            P.wait("v", "ev")
        P.emit("v", lambda e: e.reduce_sum(b.tot[:, NCH:NCH + 1], la[:, :], axis=AX.X),
               inc=[("ev", 1)])
        P.wait("v", "ev")
        P.wait("a", "ev")
        for c in range(NCH):
            sl = slice(c * 128, (c + 1) * 128)
            P.emit("a", lambda e, c=c, sl=sl: e.activation(
                sc48[:, sl], cum[:, sl], AF.Exp, bias=b.tot[:, c:c + 1], scale=-1.0))
        P.emit("a", lambda e: e.activation(sc48b[:, 0:NCH + 1], b.tot[:, 0:NCH + 1],
                                           AF.Exp), inc=[("ac", 1)])
        P.wait("s", "ac")
        P.emit("s", lambda e: e.dma_start(
            out=b.a_d.ap().rearrange("o (hh t) -> (o hh) t", t=TOK), in_=afm[:, :]),
               inc=[("st", 16)])
        P.emit("s", lambda e: e.dma_start(
            out=b.bounce.ap()[0:1, 0:NH * (NCH + 1)].rearrange(
                "o (hh c) -> (o hh) c", c=NCH + 1),
            in_=sc48b[:, 0:NCH + 1]), inc=[("st", 16)])
        P.wait("s", "st")
        for c in range(NCH + 1):
            P.emit("s", lambda e, c=c: e.dma_start(
                out=b.prow[0:1, c * NH:(c + 1) * NH],
                in_=b.bounce.ap()[0:1, 0:NH * (NCH + 1)].rearrange(
                    "o (hh c) -> o c hh", c=NCH + 1)[:, c, :],
                allow_slow_non_contiguous=True), inc=[("ld", 16)])
        P.wait("s", "ld")
        P.wait("s", "cc", 1)
        P.emit("s", lambda e: e.dma_start(
            out=agh[:, :].rearrange("p (d t) -> p d t", t=NKI * 3),
            in_=b.cc_halo_out.ap().rearrange("(d p) t -> p d t", p=128)),
            inc=[("ld", 16)])
        b.agh_ld = b.cnt["ld"]
        P.flush()

        with contextlib.ExitStack() as pes:
            ps_t = b.psum(pes, "ps_t", [128, 512])
            ps_tb = b.psum(pes, "ps_tb", [128, 1024], BF16)
            items = []
            for c in range(NCH):
                items.append((
                    (lambda t, c=c: t.transpose(ps_t[:, 0:NH],
                                                sc48[0:NH, c * 128:(c + 1) * 128],
                                                b.idf[0:NH, 0:NH])),
                    (lambda v, c=c: v.tensor_copy(b.deT[:, c * NH:(c + 1) * NH],
                                                  ps_t[:, 0:NH])),
                ))
            for c in range(NCH + 1):
                items.append((
                    (lambda t, c=c: t.matmul(ps_t[:, 0:NH], b.ones_rf[:, :],
                                             b.prow[0:1, c * NH:(c + 1) * NH],
                                             start=True, stop=True)),
                    (lambda v, c=c: v.tensor_copy(b.pcols[:, c * NH:(c + 1) * NH],
                                                  ps_t[:, 0:NH])),
                ))
            for c in range(NCH):
                items.append((
                    (lambda t, c=c: t.transpose(ps_tb[:, 0:128],
                                                b.b_bf[:, c * 128:(c + 1) * 128],
                                                b.idb[:, :])),
                    (lambda v, c=c: v.tensor_copy(b.bT[:, c * 128:(c + 1) * 128],
                                                  ps_tb[:, 0:128])),
                ))
            P = Prog(b)
            pe_b, dv_b = b.cnt["mm"], b.cnt["ev"]
            n = len(items)
            for i, (epe, _edv) in enumerate(items):
                if i > 0:
                    P.wait("t", "ev", dv_b + i)
                P.emit("t", epe, inc=[("mm", 1)])
            for i, (_epe, edv) in enumerate(items):
                P.wait("v", "mm", pe_b + i + 1)
                P.emit("v", edv, inc=[("ev", 1)])
            P.emit("g", lambda e: e.memset(b.runall[:, (NCH - 1) * NH:NCH * NH], 1.0))
            P.flush()
            P = Prog(b)
            for c in range(NCH - 2, -1, -1):
                P.emit("v", lambda e, c=c: e.tensor_mul(
                    b.runall[:, c * NH:(c + 1) * NH],
                    b.runall[:, (c + 1) * NH:(c + 2) * NH],
                    b.pcols[:, (c + 1) * NH:(c + 2) * NH]), inc=[("ev", 1)])
                P.wait("v", "ev")
            P.wait("v", "ld", b.agh_ld)
            W = NKI * 3
            P.emit("v", lambda e: e.tensor_scalar(
                b.halo[:, :], agh[:, 0:W], b.hmask_sb[:, 0:1], None, ALU.mult),
                inc=[("ev", 1)])
            for d in range(1, NCORES):
                P.wait("v", "ev")
                P.emit("v", lambda e, d=d: e.scalar_tensor_tensor(
                    b.halo[:, :], agh[:, d * W:(d + 1) * W], b.hmask_sb[:, d:d + 1],
                    b.halo[:, :], ALU.mult, ALU.add), inc=[("ev", 1)])
            P.wait("v", "ev")
            P.emit("v", lambda e: e.tensor_add(b.halo[:, :], b.halo[:, :],
                                               b.convinit_sb[:, :]), inc=[("ev", 1)])
            P.flush()


def _m3_conv(b, ms):
    nc = b.nc
    XW = 3 + TOK
    with contextlib.ExitStack() as fs:
        s = lambda n, sh, d: fs.enter_context(nc.sbuf_tensor(b._nm(n), sh, d))
        xck = s("xck", [128, 2 * XW], BF16)
        convt = s("convt", [128, 2 * TOK], BF16)
        with contextlib.ExitStack() as pes:
            ps_c = b.psum(pes, "ps_c", [128, 512])
            ps_tb = b.psum(pes, "ps_tb", [128, 1024], BF16)
            P = Prog(b)
            conv_done = {}
            tr_mm = {}
            for k in range(NKI):
                par = k % 2
                x0 = par * XW
                c0 = par * TOK
                if k >= 2:
                    P.wait("s", "ev", conv_done[k - 2])
                P.emit("s", lambda e, k=k, x0=x0: e.dma_start(
                    out=xck[:, x0 + 3:x0 + 3 + TOK],
                    in_=b.xc_d.ap()[k * 128:(k + 1) * 128, :]), inc=[("ld", 16)])
                ldv = b.cnt["ld"]
                P.wait("v", "ld", ldv)
                P.emit("v", lambda e, k=k, x0=x0: e.tensor_copy(
                    xck[:, x0:x0 + 3], b.halo[:, k * 3:(k + 1) * 3]), inc=[("ev", 1)])
                P.wait("v", "ev")
                if k >= 2:
                    P.wait("a", "mm", tr_mm[k - 2])
                for tc in range(4):
                    o = x0 + tc * 512
                    P.emit("v", lambda e, o=o, k=k: e.tensor_scalar(
                        ps_c[:, 0:512], xck[:, o:o + 512],
                        b.convw_sb[:, k * 4:k * 4 + 1], b.convb_sb[:, k:k + 1],
                        ALU.mult, ALU.add))
                    for kk in range(1, 4):
                        P.emit("v", lambda e, o=o, k=k, kk=kk: e.scalar_tensor_tensor(
                            ps_c[:, 0:512], xck[:, o + kk:o + kk + 512],
                            b.convw_sb[:, k * 4 + kk:k * 4 + kk + 1],
                            ps_c[:, 0:512], ALU.mult, ALU.add))
                    P.emit("v", lambda e: e.engine_nop(), inc=[("ev", 1)])
                    P.wait("a", "ev")
                    P.emit("a", lambda e, c0=c0, tc=tc: e.activation(
                        convt[:, c0 + tc * 512:c0 + (tc + 1) * 512], ps_c[:, 0:512],
                        AF.Silu), inc=[("ac", 1)])
                    P.wait("v", "ac")
                conv_done[k] = b.cnt["ev"]
                for c in range(NCH):
                    P.wait("t", "ac")
                    P.wait("t", "ev")
                    P.emit("t", lambda e, c=c, c0=c0: e.transpose(
                        ps_tb[:, 0:128], convt[:, c0 + c * 128:c0 + (c + 1) * 128],
                        b.idb[:, :]), inc=[("mm", 1)])
                    P.wait("v", "mm")
                    P.emit("v", lambda e, k=k, c=c: e.tensor_copy(
                        b.big[:, k * TOK + c * 128:k * TOK + (c + 1) * 128],
                        ps_tb[:, 0:128]), inc=[("ev", 1)])
                tr_mm[k] = b.cnt["mm"]
            P.flush()


def xslice(b, hh, c):
    k, half = divmod(hh, 2)
    off = k * TOK + c * 128 + half * 64
    return b.big[:, off:off + 64]


def _m4_pass1_exchange(b, ms):
    nc = b.nc
    with contextlib.ExitStack() as fs:
        s = lambda n, sh, d: fs.enter_context(nc.sbuf_tensor(b._nm(n), sh, d))
        bhat = s("bhat", [128, 2 * 128], BF16)
        sstage = s("sstage", [128, NH * PH // 2], F32)
        with contextlib.ExitStack() as pes:
            ps_sc = b.psum(pes, "ps_sc", [128, NH * PH])
            P = Prog(b)
            pe_b, dv_b = b.cnt["mm"], b.cnt["ev"]
            i = 0
            for hh in range(NH):
                for ci, c in enumerate(reversed(range(NCH))):
                    P.wait("t", "ev", dv_b + i + 1)
                    P.emit("t", lambda e, hh=hh, c=c, ci=ci, i=i: e.matmul(
                        ps_sc[:, hh * PH:(hh + 1) * PH],
                        bhat[:, (i % 2) * 128:(i % 2) * 128 + 128],
                        xslice(b, hh, c),
                        start=(ci == 0), stop=(ci == NCH - 1)), inc=[("mm", 1)])
                    i += 1
            i = 0
            for hh in range(NH):
                for ci, c in enumerate(reversed(range(NCH))):
                    if i >= 2:
                        P.wait("v", "mm", pe_b + i - 1)
                    P.emit("v", lambda e, hh=hh, c=c, i=i: e.tensor_scalar(
                        bhat[:, (i % 2) * 128:(i % 2) * 128 + 128],
                        b.bT[:, c * 128:(c + 1) * 128],
                        b.deT[:, c * NH + hh:c * NH + hh + 1],
                        b.runall[:, c * NH + hh:c * NH + hh + 1],
                        ALU.mult, ALU.mult), inc=[("ev", 1)])
                    i += 1
            P.flush()
            P = Prog(b)
            P.emit("g", lambda e: e.memset(sstage[0:1, :], 0.0))
            P.emit("g", lambda e: e.dma_start(
                out=b.cc_sum_in.ap()[128:129, 0:1536], in_=sstage[0:1, :]),
                inc=[("gp", 16)])
            P.emit("g", lambda e: e.dma_start(
                out=b.cc_sum_in.ap()[128:129, 1536:3072], in_=sstage[0:1, :]),
                inc=[("gp", 16)])
            P.wait("g", "gp")
            P.flush()
            P = Prog(b)
            P.wait("v", "mm")
            for hf in range(2):
                if hf == 1:
                    P.wait("v", "st")
                P.emit("v", lambda e, hf=hf: e.tensor_copy(
                    sstage[:, :], ps_sc[:, hf * 1536:(hf + 1) * 1536]), inc=[("ev", 1)])
                P.wait("s", "ev")
                P.emit("s", lambda e, hf=hf: e.dma_start(
                    out=b.cc_sum_in.ap()[0:128, hf * 1536:(hf + 1) * 1536],
                    in_=sstage[:, :]), inc=[("st", 16)])
                P.wait("s", "st")
            P.emit("s", lambda e: e.dma_start(
                out=b.cc_sum_in.ap()[128:129, :].rearrange("o (a hh) -> o a hh", hh=NH)
                if False else b.cc_sum_in.ap()[128:129, 0:NH],
                in_=b.prow[0:1, NCH * NH:(NCH + 1) * NH]), inc=[("st", 16)])
            P.wait("s", "st")
            P.flush()

        P = Prog(b)
        P.emit("g", lambda e: e.collective_compute(
            "AllGather", ALU.bypass, replica_groups=[list(range(NCORES))],
            ins=[b.cc_sum_in.ap().opt()], outs=[b.cc_sum_out.ap().opt()]),
            inc=[("cc", 1)])
        P.flush()

        with contextlib.ExitStack() as fs:
            s = lambda n, sh, d: fs.enter_context(nc.sbuf_tensor(b._nm(n), sh, d))
            S_sb = s("S_sb", [128, 2 * NH * PH], F32)
            pd_row = s("pd_row", [1, 2 * NH], F32)
            acols = s("acols", [128, NH], F32)
            with contextlib.ExitStack() as pes:
                ps_pc = b.psum(pes, "ps_pc", [128, 512])
                P = Prog(b)
                P.emit("s", lambda e: e.dma_start(out=b.hst[:, :], in_=b.h0_in),
                       inc=[("ld", 16)])
                ld_val = {}
                stt_done = {}
                mm_prev = {}
                for d in range(NCORES):
                    par = d % 2
                    ssl = slice(par * NH * PH, (par + 1) * NH * PH)
                    if d == 0:
                        P.wait("s", "cc", 2)
                    if d >= 2:
                        P.wait("s", "ev", stt_done[d - 2])
                    P.emit("s", lambda e, d=d, par=par: e.dma_start(
                        out=pd_row[0:1, par * NH:(par + 1) * NH],
                        in_=b.cc_sum_out.ap()[d * 129 + 128:d * 129 + 129, 0:NH]),
                        inc=[("ld", 16)])
                    P.emit("s", lambda e, d=d, ssl=ssl: e.dma_start(
                        out=S_sb[:, ssl],
                        in_=b.cc_sum_out.ap()[d * 129:d * 129 + 128, :]),
                        inc=[("ld", 16)])
                    ld_val[d] = b.cnt["ld"]
                    P.wait("t", "ld", ld_val[d])
                    if d >= 1:
                        P.wait("t", "ev", mm_prev[d - 1])
                    P.emit("t", lambda e, par=par: e.matmul(
                        ps_pc[:, 0:NH], b.ones_rf[:, :],
                        pd_row[0:1, par * NH:(par + 1) * NH],
                        start=True, stop=True), inc=[("mm", 1)])
                    P.wait("v", "mm")
                    P.emit("v", lambda e, d=d: e.tensor_scalar(
                        acols[:, :], ps_pc[:, 0:NH], b.cm128[:, d:d + 1],
                        None, ALU.mult), inc=[("ev", 1)])
                    mm_prev[d] = b.cnt["ev"]
                    P.wait("v", "ev")
                    P.emit("v", lambda e, d=d: e.tensor_scalar(
                        acols[:, :], acols[:, :], b.cm128i[:, d:d + 1],
                        None, ALU.add), inc=[("ev", 1)])
                    P.wait("v", "ev")
                    P.emit("v", lambda e, d=d, ssl=ssl: e.tensor_scalar(
                        S_sb[:, ssl], S_sb[:, ssl], b.cm128[:, d:d + 1], None,
                        ALU.mult), inc=[("ev", 1)])
                    P.wait("v", "ev")
                    for hh in range(NH):
                        P.emit("v", lambda e, hh=hh, par=par: e.scalar_tensor_tensor(
                            b.hst[:, hh * PH:(hh + 1) * PH],
                            b.hst[:, hh * PH:(hh + 1) * PH],
                            acols[:, hh:hh + 1],
                            S_sb[:, par * NH * PH + hh * PH:
                                 par * NH * PH + (hh + 1) * PH],
                            ALU.mult, ALU.add),
                            inc=[("ev", 1)])
                    stt_done[d] = b.cnt["ev"]
                P.flush()


def _m5_pass2_gate(b, ms):
    nc = b.nc
    with contextlib.ExitStack() as fs:
        s = lambda n, sh, d: fs.enter_context(nc.sbuf_tensor(b._nm(n), sh, d))
        mask_sb = s("mask", [128, 128], BF16)
        coef_sb = s("coef", [128, 128], F32)
        gm_sb = s("gm", [128, 128], BF16)
        mc_sb = s("mc", [128, 128], F32)
        bhat_sb = s("bhat2", [128, 128], BF16)
        ytm = s("ytm", [128, DI], BF16)
        yz = s("yz", [128, DI], BF16)
        ygst = s("ygst", [128, DI], BF16)
        arow = s("arow", [1, NH * 128 // 2], BF16)
        ycol = s("ycol", [128, 8], F32)
        with contextlib.ExitStack() as pes:
            ps_a = b.psum(pes, "ps_a", [128, 512])
            ps_g = b.psum(pes, "ps_g", [128, 512])
            ps_y = b.psum(pes, "ps_y", [128, 512])
            ps_s = b.psum(pes, "ps_s", [128, 512])
            ygpar = [0, 0]
            for c in range(NCH):
                P = Prog(b)
                P.emit("g", lambda e, c=c: e.dma_start(
                    out=arow[0:1, :].rearrange("o (hh j) -> o hh j", j=128),
                    in_=b.a_d.ap().rearrange("o (hh t) -> o hh t", t=TOK)[
                        :, 0:24, c * 128:(c + 1) * 128]), inc=[("gp", 16)])
                P.wait("g", "gp")
                P.flush()

                pe0, dv0, ac0 = b.cnt["mm"], b.cnt["ev"], b.cnt["ac"]
                P = Prog(b)
                P.emit("t", lambda e, c=c: e.matmul(
                    ps_g[:, 0:128], b.b_bf[:, c * 128:(c + 1) * 128],
                    b.c_bf[:, c * 128:(c + 1) * 128], start=True, stop=True),
                    inc=[("mm", 1)])
                for hh in range(NH):
                    if hh > 0:
                        P.wait("t", "ev", dv0 + 6 * (hh - 1) + 2)
                    if hh == 24:
                        P.wait("g", "mm", pe0 + 1 + 3 * 23 + 1)
                        P.emit("g", lambda e, c=c: e.dma_start(
                            out=arow[0:1, :].rearrange("o (hh j) -> o hh j", j=128),
                            in_=b.a_d.ap().rearrange("o (hh t) -> o hh t", t=TOK)[
                                :, 24:48, c * 128:(c + 1) * 128]), inc=[("gp", 16)])
                        P.wait("t", "gp")
                    P.emit("t", lambda e, hh=hh: e.matmul(
                        ps_a[:, 0:128], b.ones_rb[:, :],
                        arow[0:1, (hh % 24) * 128:(hh % 24 + 1) * 128],
                        start=True, stop=True), inc=[("mm", 1)])
                    P.wait("t", "ev", dv0 + 6 * hh + 4)
                    P.emit("t", lambda e, hh=hh, c=c: e.matmul(
                        ps_y[:, 0:PH], gm_sb[:, :], xslice(b, hh, c),
                        start=True, stop=False))
                    P.emit("t", lambda e, hh=hh: e.matmul(
                        ps_y[:, 0:PH], mc_sb[:, :], b.hst[:, hh * PH:(hh + 1) * PH],
                        start=False, stop=True), inc=[("mm", 1)])
                    P.wait("t", "ac", ac0 + hh + 1)
                    P.emit("t", lambda e, hh=hh, c=c: e.matmul(
                        ps_s[:, 0:PH], bhat_sb[:, :], xslice(b, hh, c),
                        start=True, stop=True), inc=[("mm", 1)])

                for hh in range(NH):
                    P.wait("v", "mm", pe0 + 1 + 3 * hh + 1)
                    P.emit("v", lambda e: e.tensor_tensor_scan(
                        mask_sb[:, :], ps_a[:, 0:128], b.idf[:, :], 0.0,
                        ALU.mult, ALU.add), inc=[("ev", 1)])
                    P.emit("v", lambda e: e.tensor_tensor_scan(
                        coef_sb[:, :], ps_a[:, 0:128], b.zero128[:, :], 1.0,
                        ALU.mult, ALU.add), inc=[("ev", 1)])
                    P.emit("v", lambda e: e.tensor_mul(gm_sb[:, :], ps_g[:, 0:128],
                                                       mask_sb[:, :]), inc=[("ev", 1)])
                    P.emit("v", lambda e, c=c: e.tensor_mul(
                        mc_sb[:, :], b.c_bf[:, c * 128:(c + 1) * 128], coef_sb[:, :]),
                        inc=[("ev", 1)])
                    P.wait("a", "mm", pe0 + 1 + 3 * hh + 1)
                    P.emit("a", lambda e, c=c, hh=hh: e.activation(
                        bhat_sb[:, :], b.bT[:, c * 128:(c + 1) * 128], AF.Copy,
                        scale=b.deT[:, c * NH + hh:c * NH + hh + 1]),
                        inc=[("ac", 1)])
                    P.wait("v", "mm", pe0 + 1 + 3 * hh + 2)
                    P.emit("v", lambda e, hh=hh, c=c: e.scalar_tensor_tensor(
                        ytm[:, hh * PH:(hh + 1) * PH], xslice(b, hh, c),
                        b.dcols[:, hh:hh + 1], ps_y[:, 0:PH], ALU.mult, ALU.add),
                        inc=[("ev", 1)])
                    P.wait("v", "mm", pe0 + 1 + 3 * hh + 3)
                    P.emit("v", lambda e, hh=hh: e.scalar_tensor_tensor(
                        b.hst[:, hh * PH:(hh + 1) * PH], b.hst[:, hh * PH:(hh + 1) * PH],
                        coef_sb[:, 127:128], ps_s[:, 0:PH], ALU.mult, ALU.add),
                        inc=[("ev", 1)])
                P.flush()

                P = Prog(b)
                P.wait("a", "ev")
                P.emit("a", lambda e: e.activation(yz[:, :], ytm[:, :], AF.Square),
                       inc=[("ac", 1)])
                P.wait("v", "ac")
                P.emit("v", lambda e: e.reduce_sum(ycol[:, 0:1], yz[:, :], axis=AX.X),
                       inc=[("ev", 1)])
                P.wait("v", "ev")
                P.emit("v", lambda e: e.tensor_scalar(ycol[:, 1:2], ycol[:, 0:1],
                                                      1.0 / DI, None, ALU.mult),
                       inc=[("ev", 1)])
                P.wait("a", "ev")
                P.emit("a", lambda e: e.activation(ycol[:, 2:3], ycol[:, 1:2], AF.Ln,
                                                   bias=b.epsc[:, :]), inc=[("ac", 1)])
                P.wait("a", "ac")
                P.emit("a", lambda e: e.activation(ycol[:, 3:4], ycol[:, 2:3], AF.Exp,
                                                   scale=-0.5), inc=[("ac", 1)])
                P.wait("a", "ev")
                P.emit("a", lambda e, c=c: e.dma_start_transpose(
                    yz[:, :], b.z_d.ap()[:, c * 128:(c + 1) * 128]), inc=[("sc", 16)])
                P.wait("a", "sc")
                P.emit("a", lambda e: e.activation(yz[:, :], yz[:, :], AF.Silu),
                       inc=[("ac", 1)])
                P.wait("v", "ac")
                P.emit("v", lambda e: e.tensor_mul(yz[:, :], yz[:, :], b.innw_rep[:, :]),
                       inc=[("ev", 1)])
                if ygpar[0] > 0:
                    P.wait("v", "z0", ygpar[0])
                P.emit("v", lambda e: e.scalar_tensor_tensor(
                    ygst[:, :], ytm[:, :], ycol[:, 3:4],
                    yz[:, :], ALU.mult, ALU.mult), inc=[("ev", 1)])
                P.wait("s", "ev")
                P.emit("s", lambda e, c=c: e.dma_start(
                    out=b.yg_d.ap()[c * 128:(c + 1) * 128, :],
                    in_=ygst[:, :]), inc=[("z0", 16)])
                ygpar[0] = b.cnt["z0"]
                P.flush()
        P = Prog(b)
        P.wait("s", "z0", b.cnt["z0"])
        P.wait("s", "z1", b.cnt["z1"])
        P.flush()


def _m6_outproj(b, ms, src, dst):
    nc = b.nc
    with contextlib.ExitStack() as fs:
        s = lambda n, sh, d: fs.enter_context(nc.sbuf_tensor(b._nm(n), sh, d))
        ygr = s("ygr", [128, 2 * NKI * 512], BF16)
        wbuf = s("wbufo", [128, 2 * NKI * 128], BF16)
        x1t = s("x1t", [128, 2 * 512], F32)
        x2t = s("x2t", [128, 2 * 512], F32)
        reps = s("reps", [128, 2 * 512], F32)
        rw1 = s("rw1", [1, 512], F32)
        rw2 = s("rw2", [1, 512], F32)
        with contextlib.ExitStack() as pes:
            ps_m0 = b.psum(pes, "ps_m0", [128, 512])
            ps_m1 = b.psum(pes, "ps_m1", [128, 512])
            ps_r = b.psum(pes, "ps_r", [128, 1024])
            x1par = [0, 0]
            x2par = [0, 0]
            xb_done = {}

            def pump_xbars(P, tau):
                slot = (tau % 2) * NKI * 512
                for k in range(NKI):
                    P.emit("a", lambda e, k=k, tau=tau, slot=slot:
                           e.dma_start_transpose(
                        ygr[:, slot + k * 512:slot + (k + 1) * 512],
                        b.yg_d.ap()[tau * 512:(tau + 1) * 512, k * 128:(k + 1) * 128]),
                        inc=[("sc", 16)])
                xb_done[tau] = b.cnt["sc"]

            P = Prog(b)
            pump_xbars(P, 0)
            P.flush()
            for tau in range(4):
                P = Prog(b)
                if tau < 3:
                    pump_xbars(P, tau + 1)
                P.emit("v", lambda e, tau=tau: e.tensor_scalar(
                    rw1[0:1, :], b.srow[0:1, tau * 512:(tau + 1) * 512], 1.0, None,
                    ALU.add), inc=[("ev", 1)])
                P.emit("v", lambda e, tau=tau: e.tensor_mul(
                    rw2[0:1, :], b.srow[0:1, tau * 512:(tau + 1) * 512],
                    b.mrow[0:1, tau * 512:(tau + 1) * 512]), inc=[("ev", 1)])
                P.wait("t", "ev")
                P.emit("t", lambda e: e.matmul(ps_r[:, 0:512], b.ones_rf[:, :],
                                               rw1[0:1, :], start=True, stop=True))
                P.emit("t", lambda e: e.matmul(ps_r[:, 512:1024], b.ones_rf[:, :],
                                               rw2[0:1, :], start=True, stop=True),
                       inc=[("mm", 1)])
                P.wait("v", "mm")
                P.emit("v", lambda e: e.tensor_copy(reps[:, 0:512], ps_r[:, 0:512]),
                       inc=[("ev", 1)])
                P.emit("v", lambda e: e.tensor_copy(reps[:, 512:1024],
                                                    ps_r[:, 512:1024]), inc=[("ev", 1)])
                P.wait("t", "sc", xb_done[tau])
                slot = (tau % 2) * NKI * 512
                ps_mm = [ps_m0, ps_m1]
                for dc in range(ND):
                    gci = b.wf_hist
                    if gci >= 2:
                        P.wait("g", "mm", b.mm_after_col[gci - 2])
                    b.wf_hist += 1
                    buf = b.wf_hist % 2
                    wsem = "w0" if buf == 0 else "w1"
                    P.emit("g", lambda e, dc=dc, buf=buf: e.dma_start(
                        out=wbuf[:, buf * NKI * 128:(buf + 1) * NKI * 128].rearrange(
                            "p (c n) -> p c n", n=128),
                        in_=b.outprojT_bf.ap()[:, dc * 128:(dc + 1) * 128].rearrange(
                            "(c p) n -> p c n", p=128)), inc=[(wsem, 16)])
                    xp = dc % 2
                    xsem = "z0" if xp == 0 else "z1"
                    if x1par[xp] > 0:
                        P.wait("s", "ev", x1par[xp])
                    P.emit("s", lambda e, dc=dc, tau=tau, xp=xp: e.dma_start(
                        out=x1t[:, xp * 512:(xp + 1) * 512],
                        in_=src[dc * 128:(dc + 1) * 128, tau * 512:(tau + 1) * 512]),
                        inc=[(xsem, 16)])
                    P.wait("t", wsem)
                    ti = tau * ND + dc
                    pp = ps_mm[ti % 2]
                    if ti >= 2:
                        P.wait("t", "ev", b.ev_free_o[ti - 2])
                    for k in range(NKI):
                        P.emit("t", lambda e, pp=pp, k=k, buf=buf, slot=slot: e.matmul(
                            pp[:, 0:512],
                            wbuf[:, buf * NKI * 128 + k * 128:buf * NKI * 128 + (k + 1) * 128],
                            ygr[:, slot + k * 512:slot + (k + 1) * 512],
                            start=(k == 0), stop=(k == NKI - 1)),
                            inc=([("mm", 1)] if k == NKI - 1 else None))
                    b.mm_after_col.append(b.cnt["mm"])
                    yp = ti % 2
                    ysem = "y0" if yp == 0 else "y1"
                    x2s = slice(yp * 512, (yp + 1) * 512)
                    P.wait("v", "mm")
                    P.wait("v", xsem)
                    if x2par[yp] > 0:
                        P.wait("v", ysem, x2par[yp])
                    P.emit("v", lambda e, xp=xp, x2s=x2s: e.tensor_mul(
                        x2t[:, x2s], x1t[:, xp * 512:(xp + 1) * 512], reps[:, 0:512]))
                    P.emit("v", lambda e, x2s=x2s: e.tensor_sub(x2t[:, x2s], x2t[:, x2s],
                                                                reps[:, 512:1024]))
                    P.emit("v", lambda e, pp=pp, x2s=x2s: e.tensor_add(
                        x2t[:, x2s], x2t[:, x2s], pp[:, 0:512]), inc=[("ev", 1)])
                    b.ev_free_o[ti] = b.cnt["ev"]
                    x1par[xp] = b.cnt["ev"]
                    P.wait("s", "ev")
                    P.emit("s", lambda e, dc=dc, tau=tau, x2s=x2s: e.dma_start(
                        out=dst[dc * 128:(dc + 1) * 128, tau * 512:(tau + 1) * 512],
                        in_=x2t[:, x2s]), inc=[(ysem, 16)])
                    x2par[yp] = b.cnt[ysem]
                P.flush()


def make_inputs(inp, core):
    t0 = core * TOK
    x = np.asarray(inp["L0_attn_x"]).reshape(NCORES * TOK, D)[t0:t0 + TOK]
    bf = lambda a: np.ascontiguousarray(a)
    eye = np.eye(128, dtype=np.float32)
    oh = np.zeros((E, E * 128), dtype=np.float32)
    for e in range(E):
        oh[e, e * 128:(e + 1) * 128] = 1.0
    chain_start = core % 4 == 0
    prev = core - 1
    hmask = np.zeros((128, E), dtype=np.float32)
    if not chain_start:
        hmask[:, prev] = 1.0
    cm128 = np.zeros((128, E), dtype=np.float32)
    for d in range(E):
        if d // 4 == core // 4 and d < core:
            cm128[:, d] = 1.0
    conv_state = np.asarray(inp["L0_attn_conv_state"])[0]
    conv_init = (conv_state if chain_start else np.zeros_like(conv_state))
    ssm = np.asarray(inp["L0_attn_ssm_state"])[0]
    m = {
        "x_fm": np.ascontiguousarray(x.T),
        "wg_s": bf(inp["L0_ffn_Wg_s"]), "wu_s": bf(inp["L0_ffn_Wu_s"]),
        "wd_s": bf(inp["L0_ffn_Wd_s"]),
        "wg_e": bf(inp["L0_ffn_Wg"]), "wu_e": bf(inp["L0_ffn_Wu"]),
        "wd_e": bf(inp["L0_ffn_Wd"]),
        "wr": bf(inp["L0_ffn_Wr"]),
        "br_rb": np.ascontiguousarray(
            np.stack([inp["L0_ffn_br"], inp["L0_ffn_rb"]], axis=1)),
        "inproj": bf(inp["L0_attn_in_proj_w"]),
        "outproj": bf(inp["L0_attn_out_proj_w"]),
        "convw": np.ascontiguousarray(
            np.asarray(inp["L0_attn_conv_w"]).reshape(NKI, 128, 4)
            .transpose(1, 0, 2).reshape(128, NKI * 4)),
        "convb": np.ascontiguousarray(
            np.asarray(inp["L0_attn_conv_b"]).reshape(NKI, 128).T),
        "dtbias": np.asarray(inp["L0_attn_dt_bias"]).reshape(NH, 1),
        "alog": np.asarray(inp["L0_attn_A_log"]).reshape(NH, 1),
        "drow": np.asarray(inp["L0_attn_D_param"]).reshape(1, NH),
        "normw": np.ascontiguousarray(
            np.asarray(inp["L0_attn_norm_w"]).reshape(ND, 128).T),
        "innw": np.asarray(inp["L0_attn_inner_norm_w"]).reshape(1, DI),
        "h0_in": np.ascontiguousarray(ssm.transpose(1, 0, 2).reshape(128, NH * PH)),
        "conv_init": np.ascontiguousarray(
            conv_init.reshape(NKI, 128, 3).transpose(1, 0, 2).reshape(128, NKI * 3)),
        "flaginv": np.full((128, 1), 0.0 if chain_start else 1.0, dtype=np.float32),
        "hmask": hmask,
        "cmask128": cm128,
        "idf_in": eye,
        "idb_in": eye.astype(ml_dtypes.bfloat16),
        "oh_in": oh,
    }
    return {k: np.ascontiguousarray(np.asarray(v, dtype=(
        ml_dtypes.bfloat16 if k == "idb_in" else np.float32))) for k, v in m.items()}




def kernel(**inputs):
    mk_ = MK(("ffn1", "mamba", "ffn2"))
    nc = mk_.build()
    in_maps = [make_inputs(inputs, c) for c in range(NCORES)]
    res = run_bass_kernel_spmd(nc, in_maps, core_ids=list(range(NCORES)))
    outs = [np.asarray(r["y_fm"], dtype=np.float32).T for r in res.results]
    return np.concatenate(outs, axis=0).reshape(2, 8192, D)
